# version 15
# baseline (speedup 1.0000x reference)
import numpy as np
import concourse.bass as bass
import concourse.mybir as mybir
from concourse.bass_utils import run_bass_kernel_spmd

F32 = mybir.dt.float32
BF16 = mybir.dt.bfloat16
I32 = mybir.dt.int32
AF = mybir.ActivationFunctionType
ALU = mybir.AluOpType

S = 8192
DM = 1024
NCORE = 8
TPC = 2048
EPS1 = 1e-5
EPS2 = EPS1 / 4.0
DEBUG = False

A_QSTG = 0
A_QLD = 0
A_KT = (1024, 9216)
A_V = tuple(17408 + h * 4160 for h in range(4))
A_XTB = (34048, 38144)
A_SC = (34048, 35072, 36096)
A_DEN = 34048
A_STG = (38144, 38656)
A_SELSCR = 38144
A_RHI = 39168
A_DSTG = 39168
A_SEL = 41216
A_DENH = 42240
A_QWB, A_KWB, A_VWB = 42240, 44288, 46336
A_OW = 48384
A_NRM = 0
A_X1N = (16384, 17408)
A_W1 = 18432
A_W2 = 51200
A_YT = 0
A_NTB = 8192
A_X1NR = (10240, 11264)
ARENA_BF = 83968

Z_SCALE = (0, 1024)
Z_XRES = (2048, 3072)
Z_UPRE = (4096, 5120)
Z_GAM = 6144
Z_BET = 7168
Z_VBBC = 8192
Z_W1BT = 8448
Z_MV2A = 8480
Z_VEPS16 = 8512
Z_RSTD16 = 8528
Z_BN6P = (8544, 8556)
Z_MVP = (8568, 8570)
Z_SDP = (8572, 8574)
Z_RSTP = (8576, 8578)
Z_NMP = (8580, 8582)
Z_QBS = 8584
Z_KBC = 8586
Z_DSTG = (8592, 9104)
ARENA_F32 = 9616

SEM_NAMES = (
    "pe", "act", "dve", "gp", "cc",
    "ls",
    "lg",
    "lq",
    "sq",
    "lx0", "lx1",
    "lw1", "lw2",
    "pql", "pst",
    "lgb",
    "lc0", "lc1",
    "la",
    "lr0", "lr1",
    "lp0", "lp1",
    "ln0", "ln1",
    "sg",
    "sd",
    "sx0", "sx1",
    "sn0", "sn1",
    "su0", "su1",
    "so0", "so1",
    "dp0", "dp1",
)


class Plan:

    DRAIN_ENGINES = ("vector", "scalar", "gpsimd")

    def __init__(self):
        self.ops = []
        self.cnt = {}

    def op(self, eng, build, waits=(), inc=None, amt=1, dma=False):
        tok = None
        if inc is not None:
            self.cnt[inc] = self.cnt.get(inc, 0) + amt
            tok = (inc, self.cnt[inc])
        ws = tuple((s, v) for w in waits if w is not None for (s, v) in (w,))
        self.ops.append((eng, build, ws, inc, amt, dma))
        return tok

    def emit(self, section, engobj, sems):
        hw = {}
        for eng, build, waits, inc, amt, dma in self.ops:
            if eng != section:
                continue
            for s, v in waits:
                if v > hw.get(s, 0):
                    engobj.wait_ge(sems[s], v)
                    hw[s] = v
            ins = build(engobj)
            if inc is not None:
                if not dma and eng in self.DRAIN_ENGINES:
                    engobj.drain().then_inc(sems[inc], amt)
                else:
                    ins.then_inc(sems[inc], amt)


def build_nc():
    nc = bass.Bass("TRN2", target_bir_lowering=False)

    xT = nc.dram_tensor("xT", [DM, S], F32, kind="ExternalInput")
    x_res = nc.dram_tensor("x_res", [TPC, DM], F32, kind="ExternalInput")
    qw = nc.dram_tensor("qw", [DM, 256], F32, kind="ExternalInput")
    kw = nc.dram_tensor("kw", [DM, 256], F32, kind="ExternalInput")
    vw = nc.dram_tensor("vw", [DM, 256], F32, kind="ExternalInput")
    qbT = nc.dram_tensor("qbT", [128, 2], F32, kind="ExternalInput")
    kbT = nc.dram_tensor("kbT", [128, 2], F32, kind="ExternalInput")
    vb = nc.dram_tensor("vb", [2, 128], F32, kind="ExternalInput")
    ow = nc.dram_tensor("ow", [DM, DM], F32, kind="ExternalInput")
    ob = nc.dram_tensor("ob", [DM], F32, kind="ExternalInput")
    gam = nc.dram_tensor("gam", [DM], F32, kind="ExternalInput")
    bet = nc.dram_tensor("bet", [DM], F32, kind="ExternalInput")
    w1q = nc.dram_tensor("w1q", [DM, 4096], I32, kind="ExternalInput")
    w1s = nc.dram_tensor("w1s", [8, 4096], F32, kind="ExternalInput")
    w1bT = nc.dram_tensor("w1bT", [128, 32], F32, kind="ExternalInput")
    w2q = nc.dram_tensor("w2q", [4096, DM], I32, kind="ExternalInput")
    w2s = nc.dram_tensor("w2s", [32, DM], F32, kind="ExternalInput")
    w2b = nc.dram_tensor("w2b", [DM], F32, kind="ExternalInput")
    rk = nc.dram_tensor("rk", [1, 16], I32, kind="ExternalInput")
    out = nc.dram_tensor("out", [TPC, DM], F32, kind="ExternalOutput")

    qdram = nc.dram_tensor("qdram", [256, S], BF16, kind="Internal")
    w1d = nc.dram_tensor("w1d", [128, 32768], BF16, kind="Internal")
    w2d = nc.dram_tensor("w2d", [128, 32768], BF16, kind="Internal")
    ag_in = nc.dram_tensor("ag_in", [264, S], BF16, kind="Internal")
    ag_out = nc.dram_tensor("ag_out", [1056, S], BF16, kind="Internal",
                            addr_space="Local")
    x1_sp = nc.dram_tensor("x1_sp", [TPC, DM], F32, kind="Internal")
    x1n_sp = nc.dram_tensor("x1n_sp", [TPC, DM], BF16, kind="Internal")
    dbg_pre = (nc.dram_tensor("d_pre", [TPC, DM], F32, kind="ExternalOutput")
               if DEBUG else None)

    from contextlib import ExitStack
    with ExitStack() as ctx:
        e = ctx.enter_context
        AB = e(nc.sbuf_tensor("AB", [128, ARENA_BF], BF16))
        AZ = e(nc.sbuf_tensor("AZ", [128, ARENA_F32], F32))
        cmask = e(nc.sbuf_tensor("cmask", [128, 128], BF16))
        ident = e(nc.sbuf_tensor("ident", [128, 128], BF16))
        onesb = e(nc.sbuf_tensor("onesb", [1, 128], BF16))
        ob_sb = e(nc.sbuf_tensor("ob_sb", [1, 1024], BF16))
        w2b_sb = e(nc.sbuf_tensor("w2b_sb", [1, 1024], BF16))
        iotf = e(nc.sbuf_tensor("iotf", [128, 128], F32))
        rksb = e(nc.sbuf_tensor("rksb", [1, 16], I32))
        pL0 = e(nc.psum_tensor("pL0", [128, 1024], F32))
        pL1 = e(nc.psum_tensor("pL1", [128, 1024], F32))
        pL2 = e(nc.psum_tensor("pL2", [128, 1024], F32))
        c01 = e(nc.psum_tensor("c01", [128, 512], F32))
        c23 = e(nc.psum_tensor("c23", [128, 512], F32))
        pL = (pL0, pL1, pL2)
        sems = {}
        for s in SEM_NAMES:
            sems[s] = e(nc.semaphore(s))
        block = e(nc.Block())

        P = Plan()

        def bf(off, width, rows=128, row0=0):
            return AB[row0:row0 + rows, off:off + width]

        def ff(off, width, rows=128, row0=0):
            return AZ[row0:row0 + rows, off:off + width]

        def sdma(sem, dst, src, waits=()):
            return P.op("sync", lambda g, d=dst, s=src: g.dma_start(out=d, in_=s),
                        waits=waits, inc=sem, amt=16, dma=True)

        def gdma(sem, dst, src, waits=()):
            return P.op("gpsimd", lambda g, d=dst, s=src: g.dma_start(out=d, in_=s),
                        waits=waits, inc=sem, amt=16, dma=True)

        def cdma(sem, dst, src, waits=()):
            return P.op("scalar", lambda g, d=dst, s=src: g.dma_start(out=d, in_=s),
                        waits=waits, inc=sem, amt=16, dma=True)

        P.op("gpsimd", lambda g: g.iota(iotf[:, :], [[1, 128]], base=0,
                                        channel_multiplier=-1,
                                        allow_small_or_imprecise_dtypes=True), inc="gp")
        t_iota = ("gp", P.cnt["gp"])
        P.op("vector", lambda v: v.tensor_single_scalar(cmask[:, :], iotf[:, :], 0.0, ALU.is_ge),
             waits=[t_iota], inc="dve")
        P.op("vector", lambda v: v.tensor_single_scalar(ident[:, :], iotf[:, :], 0.0, ALU.is_equal),
             waits=[t_iota], inc="dve")
        t_masks = ("dve", P.cnt["dve"])
        P.op("vector", lambda v: v.memset(onesb[:, :], 1.0), inc="dve")
        for h in range(4):
            ap = AB[:, A_V[h]:A_V[h] + 4160].rearrange("p (k c) -> p k c", c=65)[:, :, 64:65]
            P.op("vector", lambda v, ap=ap: v.memset(ap, 1.0), inc="dve")
        t_setup_v = ("dve", P.cnt["dve"])

        def r_tp(t):
            return t.ap().rearrange("(t p) c -> p t c", p=128)

        def bf3(off, width, inner):
            return bf(off, width).rearrange("p (t c) -> p t c", c=inner)

        gdma("lg", bf3(A_QWB, 2048, 256), r_tp(qw))
        gdma("lg", bf3(A_KWB, 2048, 256), r_tp(kw))
        gdma("lg", bf3(A_VWB, 2048, 256), r_tp(vw))
        gdma("lg", bf3(A_OW, 8192, 1024), r_tp(ow))
        gdma("lg", ob_sb[:, :], ob.ap().rearrange("(a c) -> a c", a=1))
        gdma("lg", w2b_sb[:, :], w2b.ap().rearrange("(a c) -> a c", a=1))
        t_wsetup = ("lg", P.cnt["lg"])
        sdma("ls", ff(Z_QBS, 2), qbT[:, :])
        sdma("ls", ff(Z_KBC, 2), kbT[:, :])
        for p in range(2):
            sdma("ls", ff(Z_VBBC + p * 128, 128), bass.AP(vb, p * 128, [[0, 128], [1, 128]]))
        sdma("ls", ff(Z_W1BT, 32), w1bT[:, :])
        sdma("ls", rksb[:, :], rk[:, :])
        t_bsetup = ("ls", P.cnt["ls"])
        P.op("vector", lambda v: v.tensor_scalar_mul(ff(Z_QBS, 2), ff(Z_QBS, 2), 0.125),
             waits=[t_bsetup], inc="dve")
        t_qbs = ("dve", P.cnt["dve"])

        NTB = 16
        qk_rot = 0
        v_rot = 0
        t_qkevac = {}
        t_vevac = {}
        t_xtb = {}
        t_p1mm_lastx = {}
        t_qspill = {}
        t_qstg_free = {}
        qk_slots = (pL0.ap()[:, 0:512], pL0.ap()[:, 512:1024],
                    pL1.ap()[:, 0:512], pL1.ap()[:, 512:1024])
        for tb in range(NTB):
            par = tb % 2
            src = bass.AP(xT, tb * 512, [[S, 128], [128 * S, 8], [1, 512]])
            t_xtb[par] = gdma("lx%d" % par, bf(A_XTB[par], 4096), src,
                              waits=[t_p1mm_lastx.get(par)])
            last_mm = None
            for which in range(2):
                woff = (A_QWB, A_KWB)[which]
                bz = (Z_QBS, Z_KBC)[which]
                sc = (0.125, 1.0)[which]
                for p in range(2):
                    slot = qk_rot % 4
                    ps = qk_slots[slot]
                    for mt in range(8):
                        w = ([t_xtb[par], t_wsetup, t_qkevac.get(slot)]
                             if mt == 0 else [])
                        tok = P.op("tensor", lambda t, ps=ps,
                                   l=bf(woff + mt * 256 + p * 128, 128),
                                   r=bf(A_XTB[par] + mt * 512, 512),
                                   st=(mt == 0), sp=(mt == 7):
                                   t.matmul(ps, l, r, start=st, stop=sp),
                                   waits=w, inc="pe" if mt == 7 else None)
                    last_mm = tok
                    if which == 0:
                        stg = (tb * 2 + p) % 2
                        d = bf(A_QSTG + stg * 512, 512)
                        ev = P.op(
                            "scalar", lambda s, d=d, ps=ps, pp=p:
                            s.activation(d, ps, AF.Identity, bias=ff(Z_QBS + pp, 1),
                                         scale=0.125),
                            waits=[tok, t_qbs, t_qstg_free.get(stg)], inc="act")
                        t_qspill[(tb, p)] = cdma(
                            "sq", qdram.ap()[p * 128:(p + 1) * 128, tb * 512:(tb + 1) * 512],
                            d, waits=[ev])
                        t_qstg_free[stg] = t_qspill[(tb, p)]
                        t_qkevac[slot] = ev
                    else:
                        t_qkevac[slot] = P.op(
                            "scalar", lambda s, d=bf(A_KT[p] + tb * 512, 512), ps=ps, pp=p:
                            s.activation(d, ps, AF.Identity, bias=ff(Z_KBC + pp, 1),
                                         scale=1.0),
                            waits=[tok, t_qbs], inc="act")
                    qk_rot += 1
            for p in range(2):
                for tt in range(4):
                    ps = (c01, c23)[v_rot % 2]
                    for mt in range(8):
                        w = ([t_xtb[par], t_wsetup, t_vevac.get(v_rot % 2)]
                             if mt == 0 else [])
                        tok = P.op("tensor", lambda t, ps=ps,
                                   l=bf(A_XTB[par] + mt * 512 + tt * 128, 128),
                                   r=bf(A_VWB + mt * 256 + p * 128, 128),
                                   st=(mt == 0), sp=(mt == 7):
                                   t.matmul(ps[:, 0:128], l, r, start=st, stop=sp),
                                   waits=w, inc="pe" if mt == 7 else None)
                    last_mm = tok
                    ktg = tb * 4 + tt
                    ev = None
                    for hl in range(2):
                        ev = P.op(
                            "vector", lambda v, d=bf(A_V[2 * p + hl] + ktg * 65, 64),
                            ps=ps, hl=hl, p=p:
                            v.tensor_add(d, ps[:, hl * 64:hl * 64 + 64],
                                         ff(Z_VBBC + p * 128 + hl * 64, 64)),
                            waits=[tok, t_bsetup, t_setup_v], inc="dve")
                    t_vevac[v_rot % 2] = ev
                    v_rot += 1
            t_p1mm_lastx[par] = last_mm
        t_p1_act = ("act", P.cnt["act"])
        t_p1_dve = ("dve", P.cnt["dve"])

        steps = []
        block_last_step = {}
        for j in range(16):
            for pair in range(2):
                nkt = 4 * j + 4
                for kt in range(nkt):
                    steps.append((j, pair, kt, nkt))
                block_last_step[2 * j + pair] = len(steps) - 1
        NSTEP = len(steps)

        PP_IN = tuple(56576 + i * 1024 for i in range(4))
        PP_OUT = tuple(60672 + i * 1024 for i in range(4))
        PP_SC = tuple(AB.ap().bitcast(F32)[0:128, 32384 + i * 1024:33408 + i * 1024]
                      for i in range(4))
        t_pp_in = {}
        t_pp_sc = {}
        t_pp_stt = {}
        t_pp_st = {}

        def pp_loads(g):
            if g < 32:
                kt, qt = g // 4, g % 4
                src = bass.AP(w1q, kt * 128 * 4096 + qt * 1024, [[4096, 128], [1, 1024]])
                ssrc = bass.AP(w1s, kt * 4096 + qt * 1024, [[0, 128], [1, 1024]])
            else:
                ft = g - 32
                src = bass.AP(w2q, ft * 128 * DM, [[DM, 128], [1, DM]])
                ssrc = bass.AP(w2s, ft * DM, [[0, 128], [1, 1024]])
            t_pp_in[g] = gdma("pql", bf(PP_IN[g % 4], 1024), src,
                              waits=[t_pp_stt.get(g - 4)])
            t_pp_sc[g] = cdma("lc%d" % (g % 2), PP_SC[g % 4], ssrc,
                              waits=[t_pp_stt.get(g - 4)])

        def pp_work(g):
            t = P.op("vector", lambda v, g=g:
                     v.scalar_tensor_tensor(bf(PP_OUT[g % 4], 1024), bf(PP_IN[g % 4], 1024),
                                            -128.0, PP_SC[g % 4],
                                            ALU.add, ALU.mult),
                     waits=[t_pp_in[g], t_pp_sc[g], t_pp_st.get(g - 4)], inc="dve")
            t_pp_stt[g] = t
            dst = (w1d.ap()[0:128, g * 1024:(g + 1) * 1024] if g < 32 else
                   w2d.ap()[0:128, (g - 32) * 1024:(g - 31) * 1024])
            t_pp_st[g] = cdma("pst", dst, bf(PP_OUT[g % 4], 1024), waits=[t])

        t_qld = {}
        t_qk = {}
        t_exp = {}
        t_pv = {}
        t_mask = {}
        t_epi_cp = {}
        t_den = {}
        t_ag_slot = {}
        t_dstg_free = {}

        def emit_qload(b):
            j, pair = b // 2, b % 2
            w = [t_qspill[(j, pair)], t_qspill[(15, pair)]]
            if b >= 2:
                w.append(t_qk[block_last_step[b - 2]])
            t_qld[b] = sdma("lq", bf(A_QLD + (b % 2) * 512, 512),
                            qdram.ap()[pair * 128:(pair + 1) * 128,
                                       j * 512:(j + 1) * 512], waits=w)

        for b in range(2):
            emit_qload(b)

        def emit_qk(s):
            j, pair, kt, nkt = steps[s]
            b = 2 * j + pair
            ch = s % 3
            for hl in range(2):
                w = []
                if hl == 0:
                    w = [t_exp.get(s - 3), t_qld[b]]
                    if s < 3:
                        w.append(t_p1_act)
                tok = P.op("tensor", lambda t, ps=pL[ch], hl=hl,
                           l=bf(A_KT[pair] + kt * 128, 128, rows=64, row0=hl * 64),
                           r=bf(A_QLD + (b % 2) * 512, 512, rows=64, row0=hl * 64),
                           tp=(hl * 64, 0):
                           t.matmul(ps.ap()[:, hl * 512:hl * 512 + 512], l, r,
                                    start=True, stop=True, tile_position=tp),
                           waits=w, inc="pe" if hl == 1 else None)
            t_qk[s] = tok
            odd = (s % 2 == 1)
            tok = P.op(
                "scalar", lambda sc_, d=bf(A_SC[ch], 1024), ps=pL[ch]:
                sc_.activation(d, ps.ap()[:, :], AF.Exp),
                waits=[t_qk[s], t_pv.get(s - 3)],
                inc="act" if odd else None, amt=2)
            if odd:
                t_exp[s] = tok
                t_exp[s - 1] = tok

        def emit_pv(s):
            j, pair, kt, nkt = steps[s]
            b = 2 * j + pair
            ch = s % 3
            off = (kt - 4 * j) * 128
            offc = max(off, 0)
            if off >= 0 and s not in t_mask:
                tm = None
                for hl in range(2):
                    tm = P.op(
                        "vector", lambda v, d=bf(A_SC[ch] + hl * 512 + off, 128):
                        v.tensor_mul(d, d, cmask[:, :]),
                        waits=[t_exp[s], t_masks], inc="dve" if hl == 1 else None)
                t_mask[s] = tm
            gate = t_mask.get(s, t_exp[s])
            for hl in range(2):
                c = (c01, c23)[hl]
                w = [gate]
                if kt == 0:
                    if b == 0:
                        w.append(t_p1_dve)
                    else:
                        w += [t_epi_cp.get((b - 1, hl)), t_den.get((b - 1, hl))]
                tok = P.op("tensor", lambda t, c=c,
                           l=AB[:, A_V[2 * pair + hl] + kt * 65:A_V[2 * pair + hl] + kt * 65 + 65],
                           r=bf(A_SC[ch] + hl * 512 + offc, 512 - offc),
                           o0=offc, st=(kt == 0), sp=(kt == nkt - 1):
                           t.matmul(c.ap()[0:65, o0:512], l, r, start=st, stop=sp),
                           waits=w, inc="pe" if hl == 1 else None)
            t_pv[s] = tok
            if kt == nkt - 1:
                for hl in range(2):
                    c = (c01, c23)[hl]
                    sp_ = hl
                    t_cp = P.op(
                        "vector", lambda v, d=bf(A_STG[sp_], 512, rows=64), c=c:
                        v.tensor_copy(d, c.ap()[0:64, 0:512]),
                        waits=[t_pv[s], t_ag_slot.get(sp_)], inc="dve")
                    t_epi_cp[(b, hl)] = t_cp
                    row0 = (pair * 2 + hl) * 64
                    t_ag_slot[sp_] = gdma(
                        "sg", ag_in.ap()[row0:row0 + 64, j * 512:(j + 1) * 512],
                        bf(A_STG[sp_], 512, rows=64), waits=[t_cp])
                    h_ = 2 * pair + hl
                    dsh = bf(A_DSTG + hl * 1024, 512, rows=1)
                    dsl = bf(A_DSTG + hl * 1024 + 512, 512, rows=1)
                    P.op("vector", lambda v, d=dsh, c=c:
                         v.tensor_copy(d, c.ap()[64:65, 0:512]),
                         waits=[t_pv[s], t_dstg_free.get(hl)], inc="dve")
                    t_cpd = P.op("vector", lambda v, d=dsl, c=c, dh=dsh:
                                 v.tensor_tensor(d, c.ap()[64:65, 0:512], dh,
                                                 ALU.subtract), inc="dve")
                    t_den[(b, hl)] = t_cpd
                    cdma("sd", ag_in.ap()[256 + h_:257 + h_, j * 512:(j + 1) * 512],
                         dsh, waits=[t_cpd])
                    t_dstg_free[hl] = cdma(
                        "sd", ag_in.ap()[260 + h_:261 + h_, j * 512:(j + 1) * 512],
                        dsl, waits=[t_cpd])

        pp_l = 0
        pp_w = 0
        for s in range(NSTEP + 2):
            if s < NSTEP:
                emit_qk(s)
                if s % 17 == 0 and pp_l < 64:
                    pp_loads(pp_l)
                    pp_l += 1
                if s % 17 == 8 and pp_w < 64 and pp_l >= pp_w + 4:
                    pp_work(pp_w)
                    pp_w += 1
                j, pair, kt, nkt = steps[s]
                b = 2 * j + pair
                if kt == nkt - 1 and (b + 2) < 32 and (b + 2) not in t_qld:
                    emit_qload(b + 2)
            if s >= 2:
                emit_pv(s - 2)
        while pp_w < 64:
            pp_work(pp_w)
            pp_w += 1
        t_p2_act = ("act", P.cnt["act"])
        t_p2_pe = ("pe", P.cnt["pe"])
        t_ag_w = [("sg", P.cnt["sg"]), ("sd", P.cnt["sd"])]

        t_cc1 = P.op("gpsimd", lambda g: g.collective_compute(
            "AllGather", ALU.bypass,
            replica_groups=[[0, 1, 2, 3], [4, 5, 6, 7]],
            ins=[ag_in.ap().opt()], outs=[ag_out.ap().opt()]),
            waits=t_ag_w, inc="cc", amt=1, dma=True)

        t_w1ld = {}
        t_w2ld = {}

        def w1_data(kt, waits=()):
            t_w1ld[kt] = sdma("lw1", bf(A_W1 + kt * 4096, 4096),
                              w1d.ap()[0:128, kt * 4096:(kt + 1) * 4096],
                              waits=list(waits) + [t_pp_st[63]])

        def w2_data(c, waits=()):
            t_w2ld[c] = sdma("lw2", bf(A_W2 + c * 4096, 4096),
                             w2d.ap()[0:128, c * 4096:(c + 1) * 4096],
                             waits=list(waits) + [t_pp_st[63]])

        for kt in range(3):
            w1_data(kt, waits=[t_p2_pe])
        for c in range(2, 8):
            w2_data(c)

        t_la = {}

        def _ld_chunk(k):
            def go(g):
                rv = nc.values_load(rksb[0:1, 0:1], engines=[mybir.EngineType.SP],
                                    min_val=0, max_val=S - TPC,
                                    skip_runtime_bounds_check=True)
                r0 = (k // 2) * 264 + (k % 2) * 128
                return g.dma_start(out=bf(A_NRM + k * 2048, 2048),
                                   in_=ag_out[r0:r0 + 128,
                                              bass.DynSlice(rv, TPC)])
            return go

        for k in range(8):
            t_la[k] = P.op("sync", _ld_chunk(k), waits=[("cc", 1)] if k == 0 else [],
                           inc="la", amt=16, dma=True)

        den_view = AB.ap().bitcast(F32)[0:16, A_DEN // 2:A_DEN // 2 + 2048]

        def _ld_den(rr, part):
            def go(g):
                rv = nc.values_load(rksb[0:1, 0:1], engines=[mybir.EngineType.SP],
                                    min_val=0, max_val=S - TPC,
                                    skip_runtime_bounds_check=True)
                r0 = rr * 264 + 256 + part * 4
                return g.dma_start(out=bf(A_DENH + part * 2048, 2048,
                                          rows=4, row0=rr * 4),
                                   in_=ag_out[r0:r0 + 4, bass.DynSlice(rv, TPC)])
            return go

        t_ldd = None
        for part in range(2):
            for rr in range(4):
                t_ldd = P.op("sync", _ld_den(rr, part),
                             waits=[("cc", 1)] if t_ldd is None else [],
                             inc="la", amt=16, dma=True)
        t_laden = P.op("vector", lambda v: v.tensor_tensor(
            den_view, bf(A_DENH, 2048, rows=16), bf(A_DENH + 2048, 2048, rows=16),
            ALU.add), waits=[t_ldd], inc="dve")

        P.op("gpsimd", lambda g: g.iota(bf(A_SELSCR, 1024, rows=16), [[1, 16], [0, 64]],
                                        base=0, channel_multiplier=-1,
                                        allow_small_or_imprecise_dtypes=True), inc="gp")
        t_iosel = ("gp", P.cnt["gp"])
        t_sel = P.op("vector", lambda v: v.tensor_single_scalar(
            bf(A_SEL, 1024, rows=16), bf(A_SELSCR, 1024, rows=16), 0.0, ALU.is_equal),
            waits=[t_iosel], inc="dve")

        cdma("lgb", ff(Z_GAM, 1024), bass.AP(gam, 0, [[0, 128], [1, 1024]]))
        t_gab = cdma("lgb", ff(Z_BET, 1024), bass.AP(bet, 0, [[0, 128], [1, 1024]]))

        FT_ORDER = list(range(8, 32)) + list(range(8))

        def _recip(v):
            with nc.allow_low_precision(reason="bf16 divisor: becomes a bf16 "
                                        "matmul operand like every other matmul "
                                        "input; numerators are bf16 already"):
                return v.reciprocal(bf(A_RHI, 2048, rows=16), den_view)

        t_recip = P.op("vector", _recip, waits=[t_laden], inc="dve")
        t_div = {}
        t_nrm_mult = {}
        for k in range(8):
            for h in range(2):
                par = (k * 2 + h) % 2
                w = [t_recip, t_sel, t_la[k], t_nrm_mult.get(par)]
                if k == 0 and h == 0:
                    w.append(t_p2_act)
                for q in range(2):
                    tok = P.op(
                        "tensor", lambda t, ps=pL[par], k=k, h=h, q=q:
                        t.matmul(ps.ap()[:, q * 512:(q + 1) * 512],
                                 bf(A_SEL + k * 128, 128, rows=16),
                                 bf(A_RHI + h * 1024 + q * 512, 512, rows=16),
                                 start=True, stop=True),
                        waits=w if q == 0 else [], inc="pe" if q == 1 else None)
                t_div[(k, h)] = tok
                t_nrm_mult[par] = P.op(
                    "vector", lambda v, d=bf(A_NRM + k * 2048 + h * 1024, 1024),
                    ps=pL[par]:
                    v.tensor_mul(d, d, ps.ap()[:, :]),
                    waits=[t_div[(k, h)], t_la[k]], inc="dve")
        t_nrm_done = ("dve", P.cnt["dve"])
        for kt in range(3, 7):
            w1_data(kt, waits=[t_nrm_done])

        t_xr = {}
        t_o_mm = {}
        t_x1pre = {}
        t_x1sp_cnt = {0: 0, 1: 0}
        t_x1n_cnt = {0: 0, 1: 0}
        t_sx_tok = {}
        t_sn_tok = {}
        t_apply = {}
        t_oproj_done = None
        for tt in range(16):
            par = tt % 2
            ps = pL[par]
            t_xr[par] = gdma("lr%d" % par, ff(Z_XRES[par], 1024),
                             x_res.ap()[tt * 128:(tt + 1) * 128, :],
                             waits=[("sx%d" % par, t_x1sp_cnt[par] * 16)])
            for ht in range(8):
                for mc in range(2):
                    w = ([t_nrm_done, t_x1pre.get(par)]
                         if (ht == 0 and mc == 0) else [])
                    P.op("tensor", lambda t, ps=ps, l=bf(A_NRM + ht * 2048 + tt * 128, 128),
                         r=bf(A_OW + ht * 1024 + mc * 512, 512), mc=mc, st=(ht == 0):
                         t.matmul(ps.ap()[:, mc * 512:mc * 512 + 512], l, r,
                                  start=st, stop=False),
                         waits=w, inc=None)
            for mc in range(2):
                tok = P.op("tensor", lambda t, ps=ps, mc=mc:
                           t.matmul(ps.ap()[:, mc * 512:mc * 512 + 512], onesb[0:1, :],
                                    ob_sb[0:1, mc * 512:mc * 512 + 512],
                                    start=False, stop=True),
                           waits=[], inc="pe" if mc == 1 else None)
            t_o_mm[par] = tok
            if tt == 15:
                t_oproj_done = tok
            t_pre = P.op("vector", lambda v, ps=ps, par=par:
                         v.tensor_add(ff(Z_UPRE[par], 1024), ps.ap()[:, :],
                                      ff(Z_XRES[par], 1024)),
                         waits=[t_o_mm[par], t_xr[par], t_apply.get(par)]
                         + ([("dp%d" % par, P.cnt.get("dp%d" % par, 0))] if DEBUG else []),
                         inc="dve")
            t_x1pre[par] = t_pre
            if DEBUG:
                cdma("dp%d" % par, dbg_pre.ap()[tt * 128:(tt + 1) * 128, :],
                     ff(Z_UPRE[par], 1024), waits=[t_pre])
            P.op("vector", lambda v, par=par: v.bn_stats(ff(Z_BN6P[par], 6),
                                                         ff(Z_UPRE[par], 512)),
                 inc="dve")
            P.op("vector", lambda v, par=par:
                 v.bn_stats(ff(Z_BN6P[par] + 6, 6), ff(Z_UPRE[par] + 512, 512)),
                 inc="dve")
            P.op("vector", lambda v, par=par: v.bn_aggr(ff(Z_MVP[par], 2),
                                                        ff(Z_BN6P[par], 12)), inc="dve")
            P.op("vector", lambda v, par=par: v.tensor_scalar_add(
                ff(Z_SDP[par], 1), ff(Z_MVP[par] + 1, 1), EPS1), inc="dve")
            P.op("vector", lambda v, par=par: v.tensor_scalar(
                ff(Z_SDP[par] + 1, 1), ff(Z_MVP[par] + 1, 1),
                1.0 + EPS1, EPS1 * EPS1, ALU.mult, ALU.add), inc="dve")
            t_v = ("dve", P.cnt["dve"])
            t_sq = P.op("scalar", lambda s, par=par: s.activation(
                ff(Z_SDP[par], 2), ff(Z_SDP[par], 2), AF.Sqrt),
                waits=[t_v], inc="act")
            P.op("vector", lambda v, par=par: v.reciprocal(ff(Z_RSTP[par], 2),
                                                           ff(Z_SDP[par], 2)),
                 waits=[t_sq], inc="dve")
            t_nm = P.op("vector", lambda v, par=par: v.tensor_scalar(
                ff(Z_NMP[par], 2), ff(Z_RSTP[par], 2),
                ff(Z_MVP[par], 1), -1.0, ALU.mult, ALU.mult), inc="dve")
            t_x1 = P.op("scalar", lambda s, par=par: s.activation(
                ff(Z_XRES[par], 1024), ff(Z_UPRE[par], 1024), AF.Identity,
                bias=ff(Z_NMP[par], 1), scale=ff(Z_RSTP[par], 1)),
                waits=[t_nm], inc="act")
            t_sx_tok[tt] = cdma("sx%d" % par, x1_sp.ap()[tt * 128:(tt + 1) * 128, :],
                                ff(Z_XRES[par], 1024), waits=[t_x1])
            t_x1sp_cnt[par] += 1
            t_xn = P.op("scalar", lambda s, par=par: s.activation(
                bf(A_X1N[par], 1024), ff(Z_UPRE[par], 1024), AF.Identity,
                bias=ff(Z_NMP[par] + 1, 1), scale=ff(Z_RSTP[par] + 1, 1)),
                waits=[t_nm, ("sn%d" % par, t_x1n_cnt[par] * 16)], inc="act")
            t_apply[par] = t_xn
            P.op("vector", lambda v, par=par:
                 v.tensor_mul(bf(A_X1N[par], 1024), bf(A_X1N[par], 1024),
                              ff(Z_GAM, 1024)), waits=[t_xn, t_gab], inc="dve")
            t_n = P.op("vector", lambda v, par=par:
                       v.tensor_add(bf(A_X1N[par], 1024), bf(A_X1N[par], 1024),
                                    ff(Z_BET, 1024)), inc="dve")
            t_sn_tok[tt] = cdma("sn%d" % par, x1n_sp.ap()[tt * 128:(tt + 1) * 128, :],
                                bf(A_X1N[par], 1024), waits=[t_n])
            t_x1n_cnt[par] += 1
        t_sn_all = [("sn0", t_x1n_cnt[0] * 16), ("sn1", t_x1n_cnt[1] * 16)]
        t_sx_all = [("sx0", t_x1sp_cnt[0] * 16), ("sx1", t_x1sp_cnt[1] * 16)]

        w1_data(7, waits=[t_oproj_done])
        for c in range(2):
            w2_data(c, waits=[t_oproj_done])
        t_zfree = {0: t_x1pre[0], 1: t_x1pre[1]}

        ctr = pL2.ap().bitcast(BF16)
        t_x1nld = {}
        t_tr_cp = None
        t_fc1 = {}
        t_x1ld = {}
        t_outts = {}
        t_so_cnt = {0: 0, 1: 0}
        for blk in range(8):
            for tsl in range(2):
                t_x1nld[tsl] = sdma(
                    "ln%d" % tsl, bf(A_X1NR[tsl], 1024),
                    x1n_sp.ap()[blk * 256 + tsl * 128: blk * 256 + (tsl + 1) * 128, :],
                    waits=[t_sn_tok[blk * 2 + tsl], t_tr_cp])
            for tsl in range(2):
                t_tr = None
                for mt in range(8):
                    w = ([t_x1nld[tsl], t_masks, t_tr_cp] if mt == 0 else [])
                    t_tr = P.op("tensor", lambda t, tsl=tsl, mt=mt:
                                t.transpose(ctr[:, mt * 128:(mt + 1) * 128],
                                            bf(A_X1NR[tsl] + mt * 128, 128), ident[:, :]),
                                waits=w, inc="pe" if mt == 7 else None)
                dst = AB[:, A_NTB:A_NTB + 2048].rearrange(
                    "p (m c) -> p m c", m=8)[:, :, tsl * 128:(tsl + 1) * 128]
                srcp = ctr[:, 0:1024].rearrange("p (m c) -> p m c", m=8)
                t_tr_cp = P.op("vector", lambda v, d=dst, s=srcp:
                               v.tensor_copy(d, s), waits=[t_tr], inc="dve")
            for ft in range(32):
                cbank = (c01, c23)[ft % 2]
                for kt in range(8):
                    w = [t_w1ld[kt]] if blk == 0 else []
                    if kt == 0:
                        w += [t_tr_cp, t_fc1.get(ft % 2)]
                    tok = P.op("tensor", lambda t, ps=cbank,
                               l=bf(A_W1 + kt * 4096 + ft * 128, 128),
                               r=bf(A_NTB + kt * 256, 256), st=(kt == 0):
                               t.matmul(ps.ap()[:, 0:256], l, r, start=st, stop=(kt == 7)),
                               waits=w, inc="pe" if kt == 7 else None)
                t_fc1[ft % 2] = P.op(
                    "scalar", lambda s, ps=cbank, d=bf(A_YT + ft * 256, 256), ft=ft:
                    s.activation(d, ps.ap()[:, 0:256], AF.Silu, bias=ff(Z_W1BT + ft, 1)),
                    waits=[tok, t_bsetup], inc="act")
            t_y = ("act", P.cnt["act"])
            for tsub in range(2):
                g = blk * 2 + tsub
                par = g % 2
                ps = pL[par]
                for hc in range(2):
                    for i, ft in enumerate(FT_ORDER):
                        w = []
                        if blk == 0:
                            w.append(t_w2ld[ft // 4])
                        if hc == 0 and i == 0:
                            w += [t_y, t_zfree.get(par)]
                        P.op("tensor", lambda t, ps=ps,
                             l=bf(A_YT + ft * 256 + tsub * 128, 128),
                             r=bf(A_W2 + ft * 1024 + hc * 512, 512),
                             hc=hc, st=(i == 0):
                             t.matmul(ps.ap()[:, hc * 512:hc * 512 + 512], l, r,
                                      start=st, stop=False),
                             waits=w, inc=None)
                    tok = P.op("tensor", lambda t, ps=ps, hc=hc:
                               t.matmul(ps.ap()[:, hc * 512:hc * 512 + 512], onesb[0:1, :],
                                        w2b_sb[0:1, hc * 512:hc * 512 + 512],
                                        start=False, stop=True),
                               waits=[], inc="pe" if hc == 1 else None)
                t_zmm = tok
                t_x1ld[par] = gdma("lr%d" % par, ff(Z_XRES[par], 1024),
                                   x1_sp.ap()[g * 128:(g + 1) * 128, :],
                                   waits=[t_sx_tok[g], t_zfree.get(par),
                                          ("so%d" % par, t_so_cnt[par] * 16)])
                t_uu = P.op("vector", lambda v, ps=ps, par=par:
                            v.scalar_tensor_tensor(ff(Z_UPRE[par], 1024), ps.ap()[:, :], 0.25,
                                                   ff(Z_XRES[par], 1024), ALU.mult, ALU.add),
                            waits=[t_zmm, t_x1ld[par], t_outts.get(par)]
                            + ([t_apply.get(par)] if blk == 0 else []), inc="dve")
                t_zfree[par] = t_uu
                P.op("vector", lambda v, par=par:
                     v.bn_stats(ff(Z_BN6P[par], 6), ff(Z_UPRE[par], 512)), inc="dve")
                P.op("vector", lambda v, par=par:
                     v.bn_stats(ff(Z_BN6P[par] + 6, 6), ff(Z_UPRE[par] + 512, 512)), inc="dve")
                P.op("vector", lambda v, g=g:
                     v.bn_aggr(ff(Z_MV2A + 2 * g, 2), ff(Z_BN6P[g % 2], 12)), inc="dve")
                P.op("vector", lambda v, par=par, g=g: v.tensor_scalar_add(
                    ff(Z_SDP[par], 1), ff(Z_MV2A + 2 * g + 1, 1), EPS2), inc="dve")
                t_v = ("dve", P.cnt["dve"])
                t_sq = P.op("scalar", lambda s, par=par: s.activation(
                    ff(Z_SDP[par], 1), ff(Z_SDP[par], 1), AF.Sqrt),
                    waits=[t_v], inc="act")
                t_rs = P.op("vector", lambda v, par=par: v.reciprocal(
                    ff(Z_RSTP[par], 1), ff(Z_SDP[par], 1)), waits=[t_sq], inc="dve")
                t_outts[par] = P.op("vector", lambda v, par=par, g=g:
                                    v.tensor_scalar(ff(Z_XRES[par], 1024),
                                                    ff(Z_UPRE[par], 1024),
                                                    ff(Z_MV2A + 2 * g, 1),
                                                    ff(Z_RSTP[par], 1),
                                                    ALU.subtract, ALU.mult),
                                    waits=[t_rs], inc="dve")
                cdma("so%d" % par, out.ap()[g * 128:(g + 1) * 128, :],
                     ff(Z_XRES[par], 1024), waits=[t_outts[par]])
                t_so_cnt[par] += 1
        if DEBUG:
            dmap = {
                "d_agin": (ag_in, t_ag_w),
                "d_agout": (ag_out, [("cc", 1)]),
                "d_w1d": (w1d, [("pst", 64 * 16)]),
                "d_w2d": (w2d, [("pst", 64 * 16)]),
                "d_x1": (x1_sp, t_sx_all),
                "d_x1n": (x1n_sp, t_sn_all),
            }
            for nm, (src_, w) in dmap.items():
                dt_ = nc.dram_tensor(nm, list(src_.shape), src_.dtype, kind="ExternalOutput")
                P.op("gpsimd", lambda g, d=dt_, s=src_: g.dma_start(out=d.ap(), in_=s.ap()),
                     waits=w, inc="lg", amt=16, dma=True)
            P.op("gpsimd", lambda g: g.nop(), waits=[("lg", P.cnt["lg"])])

        P.op("sync", lambda s: s.nop(),
             waits=[("so0", t_so_cnt[0] * 16), ("so1", t_so_cnt[1] * 16)])

        @block.sync
        def _(engobj):
            P.emit("sync", engobj, sems)

        @block.gpsimd
        def _(engobj):
            P.emit("gpsimd", engobj, sems)

        @block.tensor
        def _(engobj):
            P.emit("tensor", engobj, sems)

        @block.vector
        def _(engobj):
            P.emit("vector", engobj, sems)

        @block.scalar
        def _(engobj):
            P.emit("scalar", engobj, sems)

    return nc


def kernel(**inputs):
    x = np.asarray(inputs["L0_attn_x"])
    Qw = np.asarray(inputs["L0_attn_Q_w"])
    Qb = np.asarray(inputs["L0_attn_Q_b"])
    Kw = np.asarray(inputs["L0_attn_K_w"])
    Kb = np.asarray(inputs["L0_attn_K_b"])
    Vw = np.asarray(inputs["L0_attn_V_w"])
    Vb = np.asarray(inputs["L0_attn_V_b"])
    Ow = np.asarray(inputs["L0_attn_O_w"])
    Ob = np.asarray(inputs["L0_attn_O_b"])
    gam = np.asarray(inputs["L0_ffn_ln_gamma"])
    bet = np.asarray(inputs["L0_ffn_ln_beta"])
    w1q = np.ascontiguousarray(inputs["L0_ffn_fc1_q"])
    w1s = np.asarray(inputs["L0_ffn_fc1_scales"])
    w1b = np.asarray(inputs["L0_ffn_fc1_bias"])
    w2q = np.ascontiguousarray(inputs["L0_ffn_fc2_q"])
    w2s = np.asarray(inputs["L0_ffn_fc2_scales"])
    w2b = np.asarray(inputs["L0_ffn_fc2_bias"])

    nc = build_nc()
    owf = np.ascontiguousarray(Ow.reshape(1024, 1024))
    w1bT = np.ascontiguousarray(w1b.reshape(32, 128).T)
    in_maps = []
    for c in range(NCORE):
        b, r = c // 4, c % 4
        hs = slice(4 * r, 4 * r + 4)
        in_maps.append({
            "xT": np.ascontiguousarray(x[b].T),
            "x_res": np.ascontiguousarray(x[b, r * TPC:(r + 1) * TPC]),
            "qw": np.ascontiguousarray(Qw[hs].transpose(1, 0, 2).reshape(DM, 256)),
            "kw": np.ascontiguousarray(Kw[hs].transpose(1, 0, 2).reshape(DM, 256)),
            "vw": np.ascontiguousarray(Vw[hs].transpose(1, 0, 2).reshape(DM, 256)),
            "qbT": np.ascontiguousarray(Qb[hs].reshape(2, 128).T),
            "kbT": np.ascontiguousarray(Kb[hs].reshape(2, 128).T),
            "vb": np.ascontiguousarray(Vb[hs].reshape(2, 128)),
            "ow": owf, "ob": Ob, "gam": gam, "bet": bet,
            "w1q": w1q, "w1s": w1s, "w1bT": w1bT,
            "w2q": w2q, "w2s": w2s, "w2b": w2b,
            "rk": (r * TPC + 128 * np.arange(16, dtype=np.int32)).reshape(1, 16),
        })
    res = run_bass_kernel_spmd(nc, in_maps, core_ids=list(range(NCORE)))
    globals()["_last_res"] = res
    outarr = np.empty((2, S, DM), dtype=np.float32)
    for c in range(NCORE):
        b, r = c // 4, c % 4
        outarr[b, r * TPC:(r + 1) * TPC] = res.results[c]["out"]
    return outarr


# version 16
# speedup vs baseline: 1.1112x; 1.1112x over previous
import numpy as np
import concourse.bass as bass
import concourse.mybir as mybir
from concourse.bass_utils import run_bass_kernel_spmd

F32 = mybir.dt.float32
BF16 = mybir.dt.bfloat16
I32 = mybir.dt.int32
AF = mybir.ActivationFunctionType
ALU = mybir.AluOpType

S = 8192
DM = 1024
NCORE = 8
TPC = 2048
EPS1 = 1e-5
EPS2 = EPS1 / 4.0
DEBUG = False

A_QSTG = 0
A_QLD = 0
A_KT = (1024, 9216)
A_V = tuple(17408 + h * 4160 for h in range(4))
A_XTB = (34048, 38144)
A_SC = (34048, 35072, 36096)
A_DEN = 34048
A_STG = (38144, 38656, 39168, 39680)
A_SELSCR = 38144
A_RHI = 39168
A_SEL = 41216
A_QWB, A_KWB, A_VWB = 42240, 44288, 46336
A_OW = 48384
A_NRM = 0
A_X1N = (16384, 17408)
A_W1 = 18432
A_W2 = 51200
A_YT = 0
A_NTB = 8192
A_X1NR = (10240, 11264)
ARENA_BF = 83968

Z_SCALE = (0, 1024)
Z_XRES = (2048, 3072)
Z_UPRE = (4096, 5120)
Z_GAM = 6144
Z_BET = 7168
Z_VBBC = 8192
Z_W1BT = 8448
Z_MV2A = 8480
Z_VEPS16 = 8512
Z_RSTD16 = 8528
Z_BN6P = (8544, 8556)
Z_MVP = (8568, 8570)
Z_SDP = (8572, 8574)
Z_RSTP = (8576, 8578)
Z_NMP = (8580, 8582)
Z_QBS = 8584
Z_KBC = 8586
Z_DSTG = (8592, 9104)
ARENA_F32 = 9616

SEM_NAMES = (
    "pe", "act", "dve", "gp", "cc",
    "ls",
    "lg",
    "lq",
    "sq",
    "lx0", "lx1",
    "lw1", "lw2",
    "pql", "pst",
    "lgb",
    "lc0", "lc1",
    "la",
    "lr0", "lr1",
    "lp0", "lp1",
    "ln0", "ln1",
    "sg",
    "sd",
    "sx0", "sx1",
    "sn0", "sn1",
    "su0", "su1",
    "so0", "so1",
    "dp0", "dp1",
)


class Plan:

    DRAIN_ENGINES = ("vector", "scalar", "gpsimd")

    def __init__(self):
        self.ops = []
        self.cnt = {}

    def op(self, eng, build, waits=(), inc=None, amt=1, dma=False):
        tok = None
        if inc is not None:
            self.cnt[inc] = self.cnt.get(inc, 0) + amt
            tok = (inc, self.cnt[inc])
        ws = tuple((s, v) for w in waits if w is not None for (s, v) in (w,))
        self.ops.append((eng, build, ws, inc, amt, dma))
        return tok

    def emit(self, section, engobj, sems):
        hw = {}
        for eng, build, waits, inc, amt, dma in self.ops:
            if eng != section:
                continue
            for s, v in waits:
                if v > hw.get(s, 0):
                    engobj.wait_ge(sems[s], v)
                    hw[s] = v
            ins = build(engobj)
            if inc is not None:
                if not dma and eng in self.DRAIN_ENGINES:
                    engobj.drain().then_inc(sems[inc], amt)
                else:
                    ins.then_inc(sems[inc], amt)


def build_nc():
    nc = bass.Bass("TRN2", target_bir_lowering=False)

    xT = nc.dram_tensor("xT", [DM, S], F32, kind="ExternalInput")
    x_res = nc.dram_tensor("x_res", [TPC, DM], F32, kind="ExternalInput")
    qw = nc.dram_tensor("qw", [DM, 256], F32, kind="ExternalInput")
    kw = nc.dram_tensor("kw", [DM, 256], F32, kind="ExternalInput")
    vw = nc.dram_tensor("vw", [DM, 256], F32, kind="ExternalInput")
    qbT = nc.dram_tensor("qbT", [128, 2], F32, kind="ExternalInput")
    kbT = nc.dram_tensor("kbT", [128, 2], F32, kind="ExternalInput")
    vb = nc.dram_tensor("vb", [2, 128], F32, kind="ExternalInput")
    ow = nc.dram_tensor("ow", [DM, DM], F32, kind="ExternalInput")
    ob = nc.dram_tensor("ob", [DM], F32, kind="ExternalInput")
    gam = nc.dram_tensor("gam", [DM], F32, kind="ExternalInput")
    bet = nc.dram_tensor("bet", [DM], F32, kind="ExternalInput")
    w1q = nc.dram_tensor("w1q", [DM, 4096], I32, kind="ExternalInput")
    w1s = nc.dram_tensor("w1s", [8, 4096], F32, kind="ExternalInput")
    w1bT = nc.dram_tensor("w1bT", [128, 32], F32, kind="ExternalInput")
    w2q = nc.dram_tensor("w2q", [4096, DM], I32, kind="ExternalInput")
    w2s = nc.dram_tensor("w2s", [32, DM], F32, kind="ExternalInput")
    w2b = nc.dram_tensor("w2b", [DM], F32, kind="ExternalInput")
    rk = nc.dram_tensor("rk", [1, 16], I32, kind="ExternalInput")
    out = nc.dram_tensor("out", [TPC, DM], F32, kind="ExternalOutput")

    qdram = nc.dram_tensor("qdram", [256, S], BF16, kind="Internal")
    w1d = nc.dram_tensor("w1d", [128, 32768], BF16, kind="Internal")
    w2d = nc.dram_tensor("w2d", [128, 32768], BF16, kind="Internal")
    ag_in = nc.dram_tensor("ag_in", [256, S], BF16, kind="Internal")
    ag_den_in = nc.dram_tensor("ag_den_in", [4, S], F32, kind="Internal")
    ag_out = nc.dram_tensor("ag_out", [DM, S], BF16, kind="Internal", addr_space="Local")
    ag_den_out = nc.dram_tensor("ag_den_out", [16, S], F32, kind="Internal",
                                addr_space="Local")
    x1_sp = nc.dram_tensor("x1_sp", [TPC, DM], F32, kind="Internal")
    x1n_sp = nc.dram_tensor("x1n_sp", [TPC, DM], BF16, kind="Internal")
    dbg_pre = (nc.dram_tensor("d_pre", [TPC, DM], F32, kind="ExternalOutput")
               if DEBUG else None)

    from contextlib import ExitStack
    with ExitStack() as ctx:
        e = ctx.enter_context
        AB = e(nc.sbuf_tensor("AB", [128, ARENA_BF], BF16))
        AZ = e(nc.sbuf_tensor("AZ", [128, ARENA_F32], F32))
        cmask = e(nc.sbuf_tensor("cmask", [128, 128], BF16))
        ident = e(nc.sbuf_tensor("ident", [128, 128], BF16))
        onesb = e(nc.sbuf_tensor("onesb", [1, 128], BF16))
        ob_sb = e(nc.sbuf_tensor("ob_sb", [1, 1024], BF16))
        w2b_sb = e(nc.sbuf_tensor("w2b_sb", [1, 1024], BF16))
        iotf = e(nc.sbuf_tensor("iotf", [128, 128], F32))
        rksb = e(nc.sbuf_tensor("rksb", [1, 16], I32))
        pL0 = e(nc.psum_tensor("pL0", [128, 1024], F32))
        pL1 = e(nc.psum_tensor("pL1", [128, 1024], F32))
        pL2 = e(nc.psum_tensor("pL2", [128, 1024], F32))
        c01 = e(nc.psum_tensor("c01", [128, 512], F32))
        c23 = e(nc.psum_tensor("c23", [128, 512], F32))
        pL = (pL0, pL1, pL2)
        sems = {}
        for s in SEM_NAMES:
            sems[s] = e(nc.semaphore(s))
        block = e(nc.Block())

        P = Plan()

        def bf(off, width, rows=128, row0=0):
            return AB[row0:row0 + rows, off:off + width]

        def ff(off, width, rows=128, row0=0):
            return AZ[row0:row0 + rows, off:off + width]

        def sdma(sem, dst, src, waits=()):
            return P.op("sync", lambda g, d=dst, s=src: g.dma_start(out=d, in_=s),
                        waits=waits, inc=sem, amt=16, dma=True)

        def gdma(sem, dst, src, waits=()):
            return P.op("gpsimd", lambda g, d=dst, s=src: g.dma_start(out=d, in_=s),
                        waits=waits, inc=sem, amt=16, dma=True)

        def cdma(sem, dst, src, waits=()):
            return P.op("scalar", lambda g, d=dst, s=src: g.dma_start(out=d, in_=s),
                        waits=waits, inc=sem, amt=16, dma=True)

        P.op("gpsimd", lambda g: g.iota(iotf[:, :], [[1, 128]], base=0,
                                        channel_multiplier=-1,
                                        allow_small_or_imprecise_dtypes=True), inc="gp")
        t_iota = ("gp", P.cnt["gp"])
        P.op("vector", lambda v: v.tensor_single_scalar(cmask[:, :], iotf[:, :], 0.0, ALU.is_ge),
             waits=[t_iota], inc="dve")
        P.op("vector", lambda v: v.tensor_single_scalar(ident[:, :], iotf[:, :], 0.0, ALU.is_equal),
             waits=[t_iota], inc="dve")
        t_masks = ("dve", P.cnt["dve"])
        P.op("vector", lambda v: v.memset(onesb[:, :], 1.0), inc="dve")
        for h in range(4):
            ap = AB[:, A_V[h]:A_V[h] + 4160].rearrange("p (k c) -> p k c", c=65)[:, :, 64:65]
            P.op("vector", lambda v, ap=ap: v.memset(ap, 1.0), inc="dve")
        t_setup_v = ("dve", P.cnt["dve"])

        def r_tp(t):
            return t.ap().rearrange("(t p) c -> p t c", p=128)

        def bf3(off, width, inner):
            return bf(off, width).rearrange("p (t c) -> p t c", c=inner)

        gdma("lg", bf3(A_QWB, 2048, 256), r_tp(qw))
        gdma("lg", bf3(A_KWB, 2048, 256), r_tp(kw))
        gdma("lg", bf3(A_VWB, 2048, 256), r_tp(vw))
        gdma("lg", bf3(A_OW, 8192, 1024), r_tp(ow))
        gdma("lg", ob_sb[:, :], ob.ap().rearrange("(a c) -> a c", a=1))
        gdma("lg", w2b_sb[:, :], w2b.ap().rearrange("(a c) -> a c", a=1))
        t_wsetup = ("lg", P.cnt["lg"])
        sdma("ls", ff(Z_QBS, 2), qbT[:, :])
        sdma("ls", ff(Z_KBC, 2), kbT[:, :])
        for p in range(2):
            sdma("ls", ff(Z_VBBC + p * 128, 128), bass.AP(vb, p * 128, [[0, 128], [1, 128]]))
        sdma("ls", ff(Z_W1BT, 32), w1bT[:, :])
        sdma("ls", rksb[:, :], rk[:, :])
        t_bsetup = ("ls", P.cnt["ls"])
        P.op("vector", lambda v: v.tensor_scalar_mul(ff(Z_QBS, 2), ff(Z_QBS, 2), 0.125),
             waits=[t_bsetup], inc="dve")
        t_qbs = ("dve", P.cnt["dve"])

        NTB = 16
        qk_rot = 0
        v_rot = 0
        t_qkevac = {}
        t_vevac = {}
        t_xtb = {}
        t_p1mm_lastx = {}
        t_qspill = {}
        t_qstg_free = {}
        qk_slots = (pL0.ap()[:, 0:512], pL0.ap()[:, 512:1024],
                    pL1.ap()[:, 0:512], pL1.ap()[:, 512:1024])
        for tb in range(NTB):
            par = tb % 2
            src = bass.AP(xT, tb * 512, [[S, 128], [128 * S, 8], [1, 512]])
            t_xtb[par] = gdma("lx%d" % par, bf(A_XTB[par], 4096), src,
                              waits=[t_p1mm_lastx.get(par)])
            last_mm = None
            for which in range(2):
                woff = (A_QWB, A_KWB)[which]
                bz = (Z_QBS, Z_KBC)[which]
                sc = (0.125, 1.0)[which]
                for p in range(2):
                    slot = qk_rot % 4
                    ps = qk_slots[slot]
                    for mt in range(8):
                        w = ([t_xtb[par], t_wsetup, t_qkevac.get(slot)]
                             if mt == 0 else [])
                        tok = P.op("tensor", lambda t, ps=ps,
                                   l=bf(woff + mt * 256 + p * 128, 128),
                                   r=bf(A_XTB[par] + mt * 512, 512),
                                   st=(mt == 0), sp=(mt == 7):
                                   t.matmul(ps, l, r, start=st, stop=sp),
                                   waits=w, inc="pe" if mt == 7 else None)
                    last_mm = tok
                    if which == 0:
                        stg = (tb * 2 + p) % 2
                        d = bf(A_QSTG + stg * 512, 512)
                        ev = P.op(
                            "scalar", lambda s, d=d, ps=ps, pp=p:
                            s.activation(d, ps, AF.Identity, bias=ff(Z_QBS + pp, 1),
                                         scale=0.125),
                            waits=[tok, t_qbs, t_qstg_free.get(stg)], inc="act")
                        t_qspill[(tb, p)] = cdma(
                            "sq", qdram.ap()[p * 128:(p + 1) * 128, tb * 512:(tb + 1) * 512],
                            d, waits=[ev])
                        t_qstg_free[stg] = t_qspill[(tb, p)]
                        t_qkevac[slot] = ev
                    else:
                        t_qkevac[slot] = P.op(
                            "scalar", lambda s, d=bf(A_KT[p] + tb * 512, 512), ps=ps, pp=p:
                            s.activation(d, ps, AF.Identity, bias=ff(Z_KBC + pp, 1),
                                         scale=1.0),
                            waits=[tok, t_qbs], inc="act")
                    qk_rot += 1
            for p in range(2):
                for tt in range(4):
                    ps = (c01, c23)[v_rot % 2]
                    for mt in range(8):
                        w = ([t_xtb[par], t_wsetup, t_vevac.get(v_rot % 2)]
                             if mt == 0 else [])
                        tok = P.op("tensor", lambda t, ps=ps,
                                   l=bf(A_XTB[par] + mt * 512 + tt * 128, 128),
                                   r=bf(A_VWB + mt * 256 + p * 128, 128),
                                   st=(mt == 0), sp=(mt == 7):
                                   t.matmul(ps[:, 0:128], l, r, start=st, stop=sp),
                                   waits=w, inc="pe" if mt == 7 else None)
                    last_mm = tok
                    ktg = tb * 4 + tt
                    ev = None
                    for hl in range(2):
                        ev = P.op(
                            "vector", lambda v, d=bf(A_V[2 * p + hl] + ktg * 65, 64),
                            ps=ps, hl=hl, p=p:
                            v.tensor_add(d, ps[:, hl * 64:hl * 64 + 64],
                                         ff(Z_VBBC + p * 128 + hl * 64, 64)),
                            waits=[tok, t_bsetup, t_setup_v], inc="dve")
                    t_vevac[v_rot % 2] = ev
                    v_rot += 1
            t_p1mm_lastx[par] = last_mm
        t_p1_act = ("act", P.cnt["act"])
        t_p1_dve = ("dve", P.cnt["dve"])

        steps = []
        block_last_step = {}
        for j in range(16):
            for pair in range(2):
                nkt = 4 * j + 4
                for kt in range(nkt):
                    steps.append((j, pair, kt, nkt))
                block_last_step[2 * j + pair] = len(steps) - 1
        NSTEP = len(steps)

        PP_IN = tuple(56576 + i * 1024 for i in range(4))
        PP_OUT = tuple(60672 + i * 1024 for i in range(4))
        PP_SC = tuple(AB.ap().bitcast(F32)[0:128, 32384 + i * 1024:33408 + i * 1024]
                      for i in range(4))
        t_pp_in = {}
        t_pp_sc = {}
        t_pp_stt = {}
        t_pp_st = {}

        def pp_loads(g):
            if g < 32:
                kt, qt = g // 4, g % 4
                src = bass.AP(w1q, kt * 128 * 4096 + qt * 1024, [[4096, 128], [1, 1024]])
                ssrc = bass.AP(w1s, kt * 4096 + qt * 1024, [[0, 128], [1, 1024]])
            else:
                ft = g - 32
                src = bass.AP(w2q, ft * 128 * DM, [[DM, 128], [1, DM]])
                ssrc = bass.AP(w2s, ft * DM, [[0, 128], [1, 1024]])
            t_pp_in[g] = gdma("pql", bf(PP_IN[g % 4], 1024), src,
                              waits=[t_pp_stt.get(g - 4)])
            t_pp_sc[g] = cdma("lc%d" % (g % 2), PP_SC[g % 4], ssrc,
                              waits=[t_pp_stt.get(g - 4)])

        def pp_work(g):
            t = P.op("vector", lambda v, g=g:
                     v.scalar_tensor_tensor(bf(PP_OUT[g % 4], 1024), bf(PP_IN[g % 4], 1024),
                                            -128.0, PP_SC[g % 4],
                                            ALU.add, ALU.mult),
                     waits=[t_pp_in[g], t_pp_sc[g], t_pp_st.get(g - 4)], inc="dve")
            t_pp_stt[g] = t
            dst = (w1d.ap()[0:128, g * 1024:(g + 1) * 1024] if g < 32 else
                   w2d.ap()[0:128, (g - 32) * 1024:(g - 31) * 1024])
            t_pp_st[g] = cdma("pst", dst, bf(PP_OUT[g % 4], 1024), waits=[t])

        t_qld = {}
        t_qk = {}
        t_exp = {}
        t_pv = {}
        t_mask = {}
        t_epi_cp = {}
        t_den = {}
        t_ag_slot = {}
        t_dstg_free = {}

        def emit_qload(b):
            j, pair = b // 2, b % 2
            w = [t_qspill[(j, pair)], t_qspill[(15, pair)]]
            if b >= 2:
                w.append(t_qk[block_last_step[b - 2]])
            t_qld[b] = sdma("lq", bf(A_QLD + (b % 2) * 512, 512),
                            qdram.ap()[pair * 128:(pair + 1) * 128,
                                       j * 512:(j + 1) * 512], waits=w)

        for b in range(2):
            emit_qload(b)

        def emit_qk(s):
            j, pair, kt, nkt = steps[s]
            b = 2 * j + pair
            ch = s % 3
            for hl in range(2):
                w = []
                if hl == 0:
                    w = [t_exp.get(s - 3), t_qld[b]]
                    if s < 3:
                        w.append(t_p1_act)
                tok = P.op("tensor", lambda t, ps=pL[ch], hl=hl,
                           l=bf(A_KT[pair] + kt * 128, 128, rows=64, row0=hl * 64),
                           r=bf(A_QLD + (b % 2) * 512, 512, rows=64, row0=hl * 64),
                           tp=(hl * 64, 0):
                           t.matmul(ps.ap()[:, hl * 512:hl * 512 + 512], l, r,
                                    start=True, stop=True, tile_position=tp),
                           waits=w, inc="pe" if hl == 1 else None)
            t_qk[s] = tok
            odd = (s % 2 == 1)
            tok = P.op(
                "scalar", lambda sc_, d=bf(A_SC[ch], 1024), ps=pL[ch]:
                sc_.activation(d, ps.ap()[:, :], AF.Exp),
                waits=[t_qk[s], t_pv.get(s - 3)],
                inc="act" if odd else None, amt=2)
            if odd:
                t_exp[s] = tok
                t_exp[s - 1] = tok

        def emit_pv(s):
            j, pair, kt, nkt = steps[s]
            b = 2 * j + pair
            ch = s % 3
            off = (kt - 4 * j) * 128
            offc = max(off, 0)
            if off >= 0 and s not in t_mask:
                tm = None
                for hl in range(2):
                    tm = P.op(
                        "vector", lambda v, d=bf(A_SC[ch] + hl * 512 + off, 128):
                        v.tensor_mul(d, d, cmask[:, :]),
                        waits=[t_exp[s], t_masks], inc="dve" if hl == 1 else None)
                t_mask[s] = tm
            gate = t_mask.get(s, t_exp[s])
            for hl in range(2):
                c = (c01, c23)[hl]
                w = [gate]
                if kt == 0:
                    if b == 0:
                        w.append(t_p1_dve)
                    else:
                        w += [t_epi_cp.get((b - 1, hl)), t_den.get((b - 1, hl))]
                tok = P.op("tensor", lambda t, c=c,
                           l=AB[:, A_V[2 * pair + hl] + kt * 65:A_V[2 * pair + hl] + kt * 65 + 65],
                           r=bf(A_SC[ch] + hl * 512 + offc, 512 - offc),
                           o0=offc, st=(kt == 0), sp=(kt == nkt - 1):
                           t.matmul(c.ap()[0:65, o0:512], l, r, start=st, stop=sp),
                           waits=w, inc="pe" if hl == 1 else None)
            t_pv[s] = tok
            if kt == nkt - 1:
                for hl in range(2):
                    c = (c01, c23)[hl]
                    sp_ = (2 * b + hl) % 4
                    t_cp = P.op(
                        "vector", lambda v, d=bf(A_STG[sp_], 512, rows=64), c=c:
                        v.tensor_copy(d, c.ap()[0:64, 0:512]),
                        waits=[t_pv[s], t_ag_slot.get(sp_)], inc="dve")
                    t_epi_cp[(b, hl)] = t_cp
                    row0 = (pair * 2 + hl) * 64
                    t_ag_slot[sp_] = gdma(
                        "sg", ag_in.ap()[row0:row0 + 64, j * 512:(j + 1) * 512],
                        bf(A_STG[sp_], 512, rows=64), waits=[t_cp])
                    t_cpd = P.op(
                        "vector", lambda v, d=ff(Z_DSTG[hl], 512, rows=1), c=c:
                        v.tensor_copy(d, c.ap()[64:65, 0:512]),
                        waits=[t_pv[s], t_dstg_free.get(hl)], inc="dve")
                    t_den[(b, hl)] = t_cpd
                    t_dstg_free[hl] = cdma(
                        "sd", ag_den_in.ap()[2 * pair + hl:2 * pair + hl + 1,
                                             j * 512:(j + 1) * 512],
                        ff(Z_DSTG[hl], 512, rows=1), waits=[t_cpd])

        pp_l = 0
        pp_w = 0
        for s in range(NSTEP + 2):
            if s < NSTEP:
                emit_qk(s)
                if s % 17 == 0 and pp_l < 64:
                    pp_loads(pp_l)
                    pp_l += 1
                if s % 17 == 8 and pp_w < 64 and pp_l >= pp_w + 4:
                    pp_work(pp_w)
                    pp_w += 1
                j, pair, kt, nkt = steps[s]
                b = 2 * j + pair
                if kt == nkt - 1 and (b + 2) < 32 and (b + 2) not in t_qld:
                    emit_qload(b + 2)
            if s >= 2:
                emit_pv(s - 2)
        while pp_w < 64:
            pp_work(pp_w)
            pp_w += 1
        t_p2_act = ("act", P.cnt["act"])
        t_p2_pe = ("pe", P.cnt["pe"])
        t_ag_w = [("sg", P.cnt["sg"]), ("sd", P.cnt["sd"])]

        t_cc2 = P.op("gpsimd", lambda g: g.collective_compute(
            "AllGather", ALU.bypass,
            replica_groups=[[0, 1, 2, 3], [4, 5, 6, 7]],
            ins=[ag_den_in.ap().opt()], outs=[ag_den_out.ap().opt()]),
            waits=t_ag_w, inc="cc", amt=1, dma=True)
        t_cc1 = P.op("gpsimd", lambda g: g.collective_compute(
            "AllGather", ALU.bypass,
            replica_groups=[[0, 1, 2, 3], [4, 5, 6, 7]],
            ins=[ag_in.ap().opt()], outs=[ag_out.ap().opt()]),
            waits=[], inc="cc", amt=1, dma=True)

        t_w1ld = {}
        t_w2ld = {}

        def w1_data(kt, waits=()):
            t_w1ld[kt] = sdma("lw1", bf(A_W1 + kt * 4096, 4096),
                              w1d.ap()[0:128, kt * 4096:(kt + 1) * 4096],
                              waits=list(waits) + [t_pp_st[63]])

        def w2_data(c, waits=()):
            t_w2ld[c] = sdma("lw2", bf(A_W2 + c * 4096, 4096),
                             w2d.ap()[0:128, c * 4096:(c + 1) * 4096],
                             waits=list(waits) + [t_pp_st[63]])

        for kt in range(3):
            w1_data(kt, waits=[t_p2_pe])
        for c in range(2, 8):
            w2_data(c)

        t_la = {}

        def _ld_chunk(k):
            def go(g):
                rv = nc.values_load(rksb[0:1, 0:1], engines=[mybir.EngineType.SP],
                                    min_val=0, max_val=S - TPC,
                                    skip_runtime_bounds_check=True)
                return g.dma_start(out=bf(A_NRM + k * 2048, 2048),
                                   in_=ag_out[128 * k:128 * (k + 1),
                                              bass.DynSlice(rv, TPC)])
            return go

        den_view = AB.ap().bitcast(F32)[0:16, A_DEN // 2:A_DEN // 2 + 2048]

        def _ld_den(g):
            rv = nc.values_load(rksb[0:1, 0:1], engines=[mybir.EngineType.SP],
                                min_val=0, max_val=S - TPC,
                                skip_runtime_bounds_check=True)
            return g.dma_start(out=den_view,
                               in_=ag_den_out[0:16, bass.DynSlice(rv, TPC)])

        t_laden = P.op("sync", _ld_den, waits=[("cc", 1)], inc="la", amt=16, dma=True)

        for k in range(8):
            t_la[k] = P.op("sync", _ld_chunk(k), waits=[("cc", 2)] if k == 0 else [],
                           inc="la", amt=16, dma=True)

        P.op("gpsimd", lambda g: g.iota(bf(A_SELSCR, 1024, rows=16), [[1, 16], [0, 64]],
                                        base=0, channel_multiplier=-1,
                                        allow_small_or_imprecise_dtypes=True), inc="gp")
        t_iosel = ("gp", P.cnt["gp"])
        t_sel = P.op("vector", lambda v: v.tensor_single_scalar(
            bf(A_SEL, 1024, rows=16), bf(A_SELSCR, 1024, rows=16), 0.0, ALU.is_equal),
            waits=[t_iosel], inc="dve")

        cdma("lgb", ff(Z_GAM, 1024), bass.AP(gam, 0, [[0, 128], [1, 1024]]))
        t_gab = cdma("lgb", ff(Z_BET, 1024), bass.AP(bet, 0, [[0, 128], [1, 1024]]))

        FT_ORDER = list(range(8, 32)) + list(range(8))

        def _recip(v):
            with nc.allow_low_precision(reason="bf16 divisor: becomes a bf16 "
                                        "matmul operand like every other matmul "
                                        "input; numerators are bf16 already"):
                return v.reciprocal(bf(A_RHI, 2048, rows=16), den_view)

        t_recip = P.op("vector", _recip, waits=[t_laden], inc="dve")
        t_div = {}
        t_nrm_mult = {}
        for k in range(8):
            for h in range(2):
                par = (k * 2 + h) % 2
                w = [t_recip, t_sel, t_la[k], t_nrm_mult.get(par)]
                if k == 0 and h == 0:
                    w.append(t_p2_act)
                for q in range(2):
                    tok = P.op(
                        "tensor", lambda t, ps=pL[par], k=k, h=h, q=q:
                        t.matmul(ps.ap()[:, q * 512:(q + 1) * 512],
                                 bf(A_SEL + k * 128, 128, rows=16),
                                 bf(A_RHI + h * 1024 + q * 512, 512, rows=16),
                                 start=True, stop=True),
                        waits=w if q == 0 else [], inc="pe" if q == 1 else None)
                t_div[(k, h)] = tok
                t_nrm_mult[par] = P.op(
                    "vector", lambda v, d=bf(A_NRM + k * 2048 + h * 1024, 1024),
                    ps=pL[par]:
                    v.tensor_mul(d, d, ps.ap()[:, :]),
                    waits=[t_div[(k, h)], t_la[k]], inc="dve")
        t_nrm_done = ("dve", P.cnt["dve"])
        for kt in range(3, 7):
            w1_data(kt, waits=[t_nrm_done])

        t_xr = {}
        t_o_mm = {}
        t_x1pre = {}
        t_x1sp_cnt = {0: 0, 1: 0}
        t_x1n_cnt = {0: 0, 1: 0}
        t_sx_tok = {}
        t_sn_tok = {}
        t_apply = {}
        t_oproj_done = None
        for tt in range(16):
            par = tt % 2
            ps = pL[par]
            t_xr[par] = gdma("lr%d" % par, ff(Z_XRES[par], 1024),
                             x_res.ap()[tt * 128:(tt + 1) * 128, :],
                             waits=[("sx%d" % par, t_x1sp_cnt[par] * 16)])
            for ht in range(8):
                for mc in range(2):
                    w = ([t_nrm_done, t_x1pre.get(par)]
                         if (ht == 0 and mc == 0) else [])
                    P.op("tensor", lambda t, ps=ps, l=bf(A_NRM + ht * 2048 + tt * 128, 128),
                         r=bf(A_OW + ht * 1024 + mc * 512, 512), mc=mc, st=(ht == 0):
                         t.matmul(ps.ap()[:, mc * 512:mc * 512 + 512], l, r,
                                  start=st, stop=False),
                         waits=w, inc=None)
            for mc in range(2):
                tok = P.op("tensor", lambda t, ps=ps, mc=mc:
                           t.matmul(ps.ap()[:, mc * 512:mc * 512 + 512], onesb[0:1, :],
                                    ob_sb[0:1, mc * 512:mc * 512 + 512],
                                    start=False, stop=True),
                           waits=[], inc="pe" if mc == 1 else None)
            t_o_mm[par] = tok
            if tt == 15:
                t_oproj_done = tok
            t_pre = P.op("vector", lambda v, ps=ps, par=par:
                         v.tensor_add(ff(Z_UPRE[par], 1024), ps.ap()[:, :],
                                      ff(Z_XRES[par], 1024)),
                         waits=[t_o_mm[par], t_xr[par], t_apply.get(par)]
                         + ([("dp%d" % par, P.cnt.get("dp%d" % par, 0))] if DEBUG else []),
                         inc="dve")
            t_x1pre[par] = t_pre
            if DEBUG:
                cdma("dp%d" % par, dbg_pre.ap()[tt * 128:(tt + 1) * 128, :],
                     ff(Z_UPRE[par], 1024), waits=[t_pre])
            P.op("vector", lambda v, par=par: v.bn_stats(ff(Z_BN6P[par], 6),
                                                         ff(Z_UPRE[par], 512)),
                 inc="dve")
            P.op("vector", lambda v, par=par:
                 v.bn_stats(ff(Z_BN6P[par] + 6, 6), ff(Z_UPRE[par] + 512, 512)),
                 inc="dve")
            P.op("vector", lambda v, par=par: v.bn_aggr(ff(Z_MVP[par], 2),
                                                        ff(Z_BN6P[par], 12)), inc="dve")
            P.op("vector", lambda v, par=par: v.tensor_scalar_add(
                ff(Z_SDP[par], 1), ff(Z_MVP[par] + 1, 1), EPS1), inc="dve")
            P.op("vector", lambda v, par=par: v.tensor_scalar(
                ff(Z_SDP[par] + 1, 1), ff(Z_MVP[par] + 1, 1),
                1.0 + EPS1, EPS1 * EPS1, ALU.mult, ALU.add), inc="dve")
            t_v = ("dve", P.cnt["dve"])
            t_sq = P.op("scalar", lambda s, par=par: s.activation(
                ff(Z_SDP[par], 2), ff(Z_SDP[par], 2), AF.Sqrt),
                waits=[t_v], inc="act")
            P.op("vector", lambda v, par=par: v.reciprocal(ff(Z_RSTP[par], 2),
                                                           ff(Z_SDP[par], 2)),
                 waits=[t_sq], inc="dve")
            t_nm = P.op("vector", lambda v, par=par: v.tensor_scalar(
                ff(Z_NMP[par], 2), ff(Z_RSTP[par], 2),
                ff(Z_MVP[par], 1), -1.0, ALU.mult, ALU.mult), inc="dve")
            t_x1 = P.op("scalar", lambda s, par=par: s.activation(
                ff(Z_XRES[par], 1024), ff(Z_UPRE[par], 1024), AF.Identity,
                bias=ff(Z_NMP[par], 1), scale=ff(Z_RSTP[par], 1)),
                waits=[t_nm], inc="act")
            t_sx_tok[tt] = cdma("sx%d" % par, x1_sp.ap()[tt * 128:(tt + 1) * 128, :],
                                ff(Z_XRES[par], 1024), waits=[t_x1])
            t_x1sp_cnt[par] += 1
            t_xn = P.op("scalar", lambda s, par=par: s.activation(
                bf(A_X1N[par], 1024), ff(Z_UPRE[par], 1024), AF.Identity,
                bias=ff(Z_NMP[par] + 1, 1), scale=ff(Z_RSTP[par] + 1, 1)),
                waits=[t_nm, ("sn%d" % par, t_x1n_cnt[par] * 16)], inc="act")
            t_apply[par] = t_xn
            P.op("vector", lambda v, par=par:
                 v.tensor_mul(bf(A_X1N[par], 1024), bf(A_X1N[par], 1024),
                              ff(Z_GAM, 1024)), waits=[t_xn, t_gab], inc="dve")
            t_n = P.op("vector", lambda v, par=par:
                       v.tensor_add(bf(A_X1N[par], 1024), bf(A_X1N[par], 1024),
                                    ff(Z_BET, 1024)), inc="dve")
            t_sn_tok[tt] = cdma("sn%d" % par, x1n_sp.ap()[tt * 128:(tt + 1) * 128, :],
                                bf(A_X1N[par], 1024), waits=[t_n])
            t_x1n_cnt[par] += 1
        t_sn_all = [("sn0", t_x1n_cnt[0] * 16), ("sn1", t_x1n_cnt[1] * 16)]
        t_sx_all = [("sx0", t_x1sp_cnt[0] * 16), ("sx1", t_x1sp_cnt[1] * 16)]

        w1_data(7, waits=[t_oproj_done])
        for c in range(2):
            w2_data(c, waits=[t_oproj_done])
        t_zfree = {0: t_x1pre[0], 1: t_x1pre[1]}

        ctr = pL2.ap().bitcast(BF16)
        t_x1nld = {}
        t_tr_cp = None
        t_fc1 = {}
        t_x1ld = {}
        t_outts = {}
        t_so_cnt = {0: 0, 1: 0}
        for blk in range(8):
            for tsl in range(2):
                t_x1nld[tsl] = sdma(
                    "ln%d" % tsl, bf(A_X1NR[tsl], 1024),
                    x1n_sp.ap()[blk * 256 + tsl * 128: blk * 256 + (tsl + 1) * 128, :],
                    waits=[t_sn_tok[blk * 2 + tsl], t_tr_cp])
            for tsl in range(2):
                t_tr = None
                for mt in range(8):
                    w = ([t_x1nld[tsl], t_masks, t_tr_cp] if mt == 0 else [])
                    t_tr = P.op("tensor", lambda t, tsl=tsl, mt=mt:
                                t.transpose(ctr[:, mt * 128:(mt + 1) * 128],
                                            bf(A_X1NR[tsl] + mt * 128, 128), ident[:, :]),
                                waits=w, inc="pe" if mt == 7 else None)
                dst = AB[:, A_NTB:A_NTB + 2048].rearrange(
                    "p (m c) -> p m c", m=8)[:, :, tsl * 128:(tsl + 1) * 128]
                srcp = ctr[:, 0:1024].rearrange("p (m c) -> p m c", m=8)
                t_tr_cp = P.op("vector", lambda v, d=dst, s=srcp:
                               v.tensor_copy(d, s), waits=[t_tr], inc="dve")
            for ft in range(32):
                cbank = (c01, c23)[ft % 2]
                for kt in range(8):
                    w = [t_w1ld[kt]] if blk == 0 else []
                    if kt == 0:
                        w += [t_tr_cp, t_fc1.get(ft % 2)]
                    tok = P.op("tensor", lambda t, ps=cbank,
                               l=bf(A_W1 + kt * 4096 + ft * 128, 128),
                               r=bf(A_NTB + kt * 256, 256), st=(kt == 0):
                               t.matmul(ps.ap()[:, 0:256], l, r, start=st, stop=(kt == 7)),
                               waits=w, inc="pe" if kt == 7 else None)
                t_fc1[ft % 2] = P.op(
                    "scalar", lambda s, ps=cbank, d=bf(A_YT + ft * 256, 256), ft=ft:
                    s.activation(d, ps.ap()[:, 0:256], AF.Silu, bias=ff(Z_W1BT + ft, 1)),
                    waits=[tok, t_bsetup], inc="act")
            t_y = ("act", P.cnt["act"])
            for tsub in range(2):
                g = blk * 2 + tsub
                par = g % 2
                ps = pL[par]
                for hc in range(2):
                    for i, ft in enumerate(FT_ORDER):
                        w = []
                        if blk == 0:
                            w.append(t_w2ld[ft // 4])
                        if hc == 0 and i == 0:
                            w += [t_y, t_zfree.get(par)]
                        P.op("tensor", lambda t, ps=ps,
                             l=bf(A_YT + ft * 256 + tsub * 128, 128),
                             r=bf(A_W2 + ft * 1024 + hc * 512, 512),
                             hc=hc, st=(i == 0):
                             t.matmul(ps.ap()[:, hc * 512:hc * 512 + 512], l, r,
                                      start=st, stop=False),
                             waits=w, inc=None)
                    tok = P.op("tensor", lambda t, ps=ps, hc=hc:
                               t.matmul(ps.ap()[:, hc * 512:hc * 512 + 512], onesb[0:1, :],
                                        w2b_sb[0:1, hc * 512:hc * 512 + 512],
                                        start=False, stop=True),
                               waits=[], inc="pe" if hc == 1 else None)
                t_zmm = tok
                t_x1ld[par] = gdma("lr%d" % par, ff(Z_XRES[par], 1024),
                                   x1_sp.ap()[g * 128:(g + 1) * 128, :],
                                   waits=[t_sx_tok[g], t_zfree.get(par),
                                          ("so%d" % par, t_so_cnt[par] * 16)])
                t_uu = P.op("vector", lambda v, ps=ps, par=par:
                            v.scalar_tensor_tensor(ff(Z_UPRE[par], 1024), ps.ap()[:, :], 0.25,
                                                   ff(Z_XRES[par], 1024), ALU.mult, ALU.add),
                            waits=[t_zmm, t_x1ld[par], t_outts.get(par)]
                            + ([t_apply.get(par)] if blk == 0 else []), inc="dve")
                t_zfree[par] = t_uu
                P.op("vector", lambda v, par=par:
                     v.bn_stats(ff(Z_BN6P[par], 6), ff(Z_UPRE[par], 512)), inc="dve")
                P.op("vector", lambda v, par=par:
                     v.bn_stats(ff(Z_BN6P[par] + 6, 6), ff(Z_UPRE[par] + 512, 512)), inc="dve")
                P.op("vector", lambda v, g=g:
                     v.bn_aggr(ff(Z_MV2A + 2 * g, 2), ff(Z_BN6P[g % 2], 12)), inc="dve")
                P.op("vector", lambda v, par=par, g=g: v.tensor_scalar_add(
                    ff(Z_SDP[par], 1), ff(Z_MV2A + 2 * g + 1, 1), EPS2), inc="dve")
                t_v = ("dve", P.cnt["dve"])
                t_sq = P.op("scalar", lambda s, par=par: s.activation(
                    ff(Z_SDP[par], 1), ff(Z_SDP[par], 1), AF.Sqrt),
                    waits=[t_v], inc="act")
                t_rs = P.op("vector", lambda v, par=par: v.reciprocal(
                    ff(Z_RSTP[par], 1), ff(Z_SDP[par], 1)), waits=[t_sq], inc="dve")
                t_outts[par] = P.op("vector", lambda v, par=par, g=g:
                                    v.tensor_scalar(ff(Z_XRES[par], 1024),
                                                    ff(Z_UPRE[par], 1024),
                                                    ff(Z_MV2A + 2 * g, 1),
                                                    ff(Z_RSTP[par], 1),
                                                    ALU.subtract, ALU.mult),
                                    waits=[t_rs], inc="dve")
                cdma("so%d" % par, out.ap()[g * 128:(g + 1) * 128, :],
                     ff(Z_XRES[par], 1024), waits=[t_outts[par]])
                t_so_cnt[par] += 1
        if DEBUG:
            dmap = {
                "d_agin": (ag_in, t_ag_w),
                "d_agden": (ag_den_in, t_ag_w),
                "d_agout": (ag_out, [("cc", 1)]),
                "d_agdout": (ag_den_out, [("cc", 2)]),
                "d_w1d": (w1d, [("pst", 64 * 16)]),
                "d_w2d": (w2d, [("pst", 64 * 16)]),
                "d_x1": (x1_sp, t_sx_all),
                "d_x1n": (x1n_sp, t_sn_all),
            }
            for nm, (src_, w) in dmap.items():
                dt_ = nc.dram_tensor(nm, list(src_.shape), src_.dtype, kind="ExternalOutput")
                P.op("gpsimd", lambda g, d=dt_, s=src_: g.dma_start(out=d.ap(), in_=s.ap()),
                     waits=w, inc="lg", amt=16, dma=True)
            P.op("gpsimd", lambda g: g.nop(), waits=[("lg", P.cnt["lg"])])

        P.op("sync", lambda s: s.nop(),
             waits=[("so0", t_so_cnt[0] * 16), ("so1", t_so_cnt[1] * 16)])

        @block.sync
        def _(engobj):
            P.emit("sync", engobj, sems)

        @block.gpsimd
        def _(engobj):
            P.emit("gpsimd", engobj, sems)

        @block.tensor
        def _(engobj):
            P.emit("tensor", engobj, sems)

        @block.vector
        def _(engobj):
            P.emit("vector", engobj, sems)

        @block.scalar
        def _(engobj):
            P.emit("scalar", engobj, sems)

    return nc


def kernel(**inputs):
    x = np.asarray(inputs["L0_attn_x"])
    Qw = np.asarray(inputs["L0_attn_Q_w"])
    Qb = np.asarray(inputs["L0_attn_Q_b"])
    Kw = np.asarray(inputs["L0_attn_K_w"])
    Kb = np.asarray(inputs["L0_attn_K_b"])
    Vw = np.asarray(inputs["L0_attn_V_w"])
    Vb = np.asarray(inputs["L0_attn_V_b"])
    Ow = np.asarray(inputs["L0_attn_O_w"])
    Ob = np.asarray(inputs["L0_attn_O_b"])
    gam = np.asarray(inputs["L0_ffn_ln_gamma"])
    bet = np.asarray(inputs["L0_ffn_ln_beta"])
    w1q = np.ascontiguousarray(inputs["L0_ffn_fc1_q"])
    w1s = np.asarray(inputs["L0_ffn_fc1_scales"])
    w1b = np.asarray(inputs["L0_ffn_fc1_bias"])
    w2q = np.ascontiguousarray(inputs["L0_ffn_fc2_q"])
    w2s = np.asarray(inputs["L0_ffn_fc2_scales"])
    w2b = np.asarray(inputs["L0_ffn_fc2_bias"])

    nc = build_nc()
    owf = np.ascontiguousarray(Ow.reshape(1024, 1024))
    w1bT = np.ascontiguousarray(w1b.reshape(32, 128).T)
    in_maps = []
    for c in range(NCORE):
        b, r = c // 4, c % 4
        hs = slice(4 * r, 4 * r + 4)
        in_maps.append({
            "xT": np.ascontiguousarray(x[b].T),
            "x_res": np.ascontiguousarray(x[b, r * TPC:(r + 1) * TPC]),
            "qw": np.ascontiguousarray(Qw[hs].transpose(1, 0, 2).reshape(DM, 256)),
            "kw": np.ascontiguousarray(Kw[hs].transpose(1, 0, 2).reshape(DM, 256)),
            "vw": np.ascontiguousarray(Vw[hs].transpose(1, 0, 2).reshape(DM, 256)),
            "qbT": np.ascontiguousarray(Qb[hs].reshape(2, 128).T),
            "kbT": np.ascontiguousarray(Kb[hs].reshape(2, 128).T),
            "vb": np.ascontiguousarray(Vb[hs].reshape(2, 128)),
            "ow": owf, "ob": Ob, "gam": gam, "bet": bet,
            "w1q": w1q, "w1s": w1s, "w1bT": w1bT,
            "w2q": w2q, "w2s": w2s, "w2b": w2b,
            "rk": (r * TPC + 128 * np.arange(16, dtype=np.int32)).reshape(1, 16),
        })
    res = run_bass_kernel_spmd(nc, in_maps, core_ids=list(range(NCORE)))
    globals()["_last_res"] = res
    outarr = np.empty((2, S, DM), dtype=np.float32)
    for c in range(NCORE):
        b, r = c // 4, c % 4
        outarr[b, r * TPC:(r + 1) * TPC] = res.results[c]["out"]
    return outarr


# version 18
# speedup vs baseline: 1.1276x; 1.0148x over previous
import numpy as np
import concourse.bass as bass
import concourse.mybir as mybir
from concourse.bass_utils import run_bass_kernel_spmd

F32 = mybir.dt.float32
BF16 = mybir.dt.bfloat16
I32 = mybir.dt.int32
AF = mybir.ActivationFunctionType
ALU = mybir.AluOpType

S = 8192
DM = 1024
NCORE = 8
TPC = 2048
EPS1 = 1e-5
EPS2 = EPS1 / 4.0
DEBUG = False

A_QSTG = 0
A_QLD = 0
A_KT = (1024, 9216)
A_V = tuple(17408 + h * 4160 for h in range(4))
A_XTB = (34048, 38144)
A_SC = (34048, 35072, 36096)
A_DEN = 34048
A_STG = (38144, 38656, 39168, 39680)
A_SELSCR = 38144
A_RHI = 39168
A_SEL = 41216
A_QWB, A_KWB, A_VWB = 42240, 44288, 46336
A_OW = 48384
A_NRM = 0
A_X1N = (16384, 17408)
A_W1 = 18432
A_W2 = 51200
A_YT = 0
A_NTB = 8192
A_X1NR = (10240, 11264)
ARENA_BF = 83968

Z_SCALE = (0, 1024)
Z_XRES = (2048, 3072)
Z_UPRE = (4096, 5120)
Z_GAM = 6144
Z_BET = 7168
Z_VBBC = 8192
Z_W1BT = 8448
Z_MV2A = 8480
Z_VEPS16 = 8512
Z_RSTD16 = 8528
Z_BN6P = (8544, 8556)
Z_MVP = (8568, 8570)
Z_SDP = (8572, 8574)
Z_RSTP = (8576, 8578)
Z_NMP = (8580, 8582)
Z_QBS = 8584
Z_KBC = 8586
Z_DSTG = (8592, 9104)
ARENA_F32 = 9616

SEM_NAMES = (
    "pe", "act", "dve", "gp", "cc",
    "ls",
    "lg",
    "lq",
    "sq",
    "lx0", "lx1",
    "lw1", "lw2",
    "pql", "pst",
    "lgb",
    "lc0", "lc1",
    "la",
    "lr0", "lr1",
    "lp0", "lp1",
    "ln0", "ln1",
    "sg",
    "sd",
    "sx0", "sx1",
    "sn0", "sn1",
    "su0", "su1",
    "so0", "so1",
    "dp0", "dp1",
)


class Plan:

    DRAIN_ENGINES = ("vector", "scalar", "gpsimd")

    def __init__(self):
        self.ops = []
        self.cnt = {}

    def op(self, eng, build, waits=(), inc=None, amt=1, dma=False):
        tok = None
        if inc is not None:
            self.cnt[inc] = self.cnt.get(inc, 0) + amt
            tok = (inc, self.cnt[inc])
        ws = tuple((s, v) for w in waits if w is not None for (s, v) in (w,))
        self.ops.append((eng, build, ws, inc, amt, dma))
        return tok

    def emit(self, section, engobj, sems):
        hw = {}
        for eng, build, waits, inc, amt, dma in self.ops:
            if eng != section:
                continue
            for s, v in waits:
                if v > hw.get(s, 0):
                    engobj.wait_ge(sems[s], v)
                    hw[s] = v
            ins = build(engobj)
            if inc is not None:
                if not dma and eng in self.DRAIN_ENGINES:
                    engobj.drain().then_inc(sems[inc], amt)
                else:
                    ins.then_inc(sems[inc], amt)


def build_nc():
    nc = bass.Bass("TRN2", target_bir_lowering=False)

    xT = nc.dram_tensor("xT", [DM, S], F32, kind="ExternalInput")
    x_res = nc.dram_tensor("x_res", [TPC, DM], F32, kind="ExternalInput")
    qw = nc.dram_tensor("qw", [DM, 256], F32, kind="ExternalInput")
    kw = nc.dram_tensor("kw", [DM, 256], F32, kind="ExternalInput")
    vw = nc.dram_tensor("vw", [DM, 256], F32, kind="ExternalInput")
    qbT = nc.dram_tensor("qbT", [128, 2], F32, kind="ExternalInput")
    kbT = nc.dram_tensor("kbT", [128, 2], F32, kind="ExternalInput")
    vb = nc.dram_tensor("vb", [2, 128], F32, kind="ExternalInput")
    ow = nc.dram_tensor("ow", [DM, DM], F32, kind="ExternalInput")
    ob = nc.dram_tensor("ob", [DM], F32, kind="ExternalInput")
    gam = nc.dram_tensor("gam", [DM], F32, kind="ExternalInput")
    bet = nc.dram_tensor("bet", [DM], F32, kind="ExternalInput")
    w1q = nc.dram_tensor("w1q", [DM, 4096], I32, kind="ExternalInput")
    w1s = nc.dram_tensor("w1s", [8, 4096], F32, kind="ExternalInput")
    w1bT = nc.dram_tensor("w1bT", [128, 32], F32, kind="ExternalInput")
    w2q = nc.dram_tensor("w2q", [4096, DM], I32, kind="ExternalInput")
    w2s = nc.dram_tensor("w2s", [32, DM], F32, kind="ExternalInput")
    w2b = nc.dram_tensor("w2b", [DM], F32, kind="ExternalInput")
    rk = nc.dram_tensor("rk", [1, 16], I32, kind="ExternalInput")
    out = nc.dram_tensor("out", [TPC, DM], F32, kind="ExternalOutput")

    qdram = nc.dram_tensor("qdram", [256, S], BF16, kind="Internal")
    w1d = nc.dram_tensor("w1d", [128, 32768], BF16, kind="Internal")
    w2d = nc.dram_tensor("w2d", [128, 32768], BF16, kind="Internal")
    ag_in = nc.dram_tensor("ag_in", [256, S], BF16, kind="Internal")
    ag_den_in = nc.dram_tensor("ag_den_in", [4, S], F32, kind="Internal")
    ag_out = nc.dram_tensor("ag_out", [DM, S], BF16, kind="Internal", addr_space="Local")
    ag_den_out = nc.dram_tensor("ag_den_out", [16, S], F32, kind="Internal",
                                addr_space="Local")
    x1_sp = nc.dram_tensor("x1_sp", [TPC, DM], F32, kind="Internal")
    x1n_sp = nc.dram_tensor("x1n_sp", [TPC, DM], BF16, kind="Internal")
    dbg_pre = (nc.dram_tensor("d_pre", [TPC, DM], F32, kind="ExternalOutput")
               if DEBUG else None)

    from contextlib import ExitStack
    with ExitStack() as ctx:
        e = ctx.enter_context
        AB = e(nc.sbuf_tensor("AB", [128, ARENA_BF], BF16))
        AZ = e(nc.sbuf_tensor("AZ", [128, ARENA_F32], F32))
        cmask = e(nc.sbuf_tensor("cmask", [128, 128], BF16))
        ident = e(nc.sbuf_tensor("ident", [128, 128], BF16))
        onesb = e(nc.sbuf_tensor("onesb", [1, 128], BF16))
        ob_sb = e(nc.sbuf_tensor("ob_sb", [1, 1024], BF16))
        w2b_sb = e(nc.sbuf_tensor("w2b_sb", [1, 1024], BF16))
        iotf = e(nc.sbuf_tensor("iotf", [128, 128], F32))
        rksb = e(nc.sbuf_tensor("rksb", [1, 16], I32))
        pL0 = e(nc.psum_tensor("pL0", [128, 1024], F32))
        pL1 = e(nc.psum_tensor("pL1", [128, 1024], F32))
        pL2 = e(nc.psum_tensor("pL2", [128, 1024], F32))
        c01 = e(nc.psum_tensor("c01", [128, 512], F32))
        c23 = e(nc.psum_tensor("c23", [128, 512], F32))
        pL = (pL0, pL1, pL2)
        sems = {}
        for s in SEM_NAMES:
            sems[s] = e(nc.semaphore(s))
        block = e(nc.Block())

        P = Plan()

        def bf(off, width, rows=128, row0=0):
            return AB[row0:row0 + rows, off:off + width]

        def ff(off, width, rows=128, row0=0):
            return AZ[row0:row0 + rows, off:off + width]

        def sdma(sem, dst, src, waits=()):
            return P.op("sync", lambda g, d=dst, s=src: g.dma_start(out=d, in_=s),
                        waits=waits, inc=sem, amt=16, dma=True)

        def gdma(sem, dst, src, waits=()):
            return P.op("gpsimd", lambda g, d=dst, s=src: g.dma_start(out=d, in_=s),
                        waits=waits, inc=sem, amt=16, dma=True)

        def cdma(sem, dst, src, waits=()):
            return P.op("scalar", lambda g, d=dst, s=src: g.dma_start(out=d, in_=s),
                        waits=waits, inc=sem, amt=16, dma=True)

        P.op("gpsimd", lambda g: g.iota(iotf[:, :], [[1, 128]], base=0,
                                        channel_multiplier=-1,
                                        allow_small_or_imprecise_dtypes=True), inc="gp")
        t_iota = ("gp", P.cnt["gp"])
        P.op("vector", lambda v: v.tensor_single_scalar(cmask[:, :], iotf[:, :], 0.0, ALU.is_ge),
             waits=[t_iota], inc="dve")
        P.op("vector", lambda v: v.tensor_single_scalar(ident[:, :], iotf[:, :], 0.0, ALU.is_equal),
             waits=[t_iota], inc="dve")
        t_masks = ("dve", P.cnt["dve"])
        P.op("vector", lambda v: v.memset(onesb[:, :], 1.0), inc="dve")
        for h in range(4):
            ap = AB[:, A_V[h]:A_V[h] + 4160].rearrange("p (k c) -> p k c", c=65)[:, :, 64:65]
            P.op("vector", lambda v, ap=ap: v.memset(ap, 1.0), inc="dve")
        t_setup_v = ("dve", P.cnt["dve"])

        def r_tp(t):
            return t.ap().rearrange("(t p) c -> p t c", p=128)

        def bf3(off, width, inner):
            return bf(off, width).rearrange("p (t c) -> p t c", c=inner)

        gdma("lg", bf3(A_QWB, 2048, 256), r_tp(qw))
        gdma("lg", bf3(A_KWB, 2048, 256), r_tp(kw))
        gdma("lg", bf3(A_VWB, 2048, 256), r_tp(vw))
        gdma("lg", bf3(A_OW, 8192, 1024), r_tp(ow))
        gdma("lg", ob_sb[:, :], ob.ap().rearrange("(a c) -> a c", a=1))
        gdma("lg", w2b_sb[:, :], w2b.ap().rearrange("(a c) -> a c", a=1))
        t_wsetup = ("lg", P.cnt["lg"])
        sdma("ls", ff(Z_QBS, 2), qbT[:, :])
        sdma("ls", ff(Z_KBC, 2), kbT[:, :])
        for p in range(2):
            sdma("ls", ff(Z_VBBC + p * 128, 128), bass.AP(vb, p * 128, [[0, 128], [1, 128]]))
        sdma("ls", ff(Z_W1BT, 32), w1bT[:, :])
        sdma("ls", rksb[:, :], rk[:, :])
        t_bsetup = ("ls", P.cnt["ls"])
        P.op("vector", lambda v: v.tensor_scalar_mul(ff(Z_QBS, 2), ff(Z_QBS, 2), 0.125),
             waits=[t_bsetup], inc="dve")
        t_qbs = ("dve", P.cnt["dve"])

        NTB = 16
        qk_rot = 0
        v_rot = 0
        t_qkevac = {}
        t_vevac = {}
        t_xtb = {}
        t_p1mm_lastx = {}
        t_qspill = {}
        t_qstg_free = {}
        qk_slots = (pL0.ap()[:, 0:512], pL0.ap()[:, 512:1024],
                    pL1.ap()[:, 0:512], pL1.ap()[:, 512:1024])
        for tb in range(NTB):
            par = tb % 2
            src = bass.AP(xT, tb * 512, [[S, 128], [128 * S, 8], [1, 512]])
            t_xtb[par] = gdma("lx%d" % par, bf(A_XTB[par], 4096), src,
                              waits=[t_p1mm_lastx.get(par)])
            last_mm = None
            for which in range(2):
                woff = (A_QWB, A_KWB)[which]
                bz = (Z_QBS, Z_KBC)[which]
                sc = (0.125, 1.0)[which]
                for p in range(2):
                    slot = qk_rot % 4
                    ps = qk_slots[slot]
                    for mt in range(8):
                        w = ([t_xtb[par], t_wsetup, t_qkevac.get(slot)]
                             if mt == 0 else [])
                        tok = P.op("tensor", lambda t, ps=ps,
                                   l=bf(woff + mt * 256 + p * 128, 128),
                                   r=bf(A_XTB[par] + mt * 512, 512),
                                   st=(mt == 0), sp=(mt == 7):
                                   t.matmul(ps, l, r, start=st, stop=sp),
                                   waits=w, inc="pe" if mt == 7 else None)
                    last_mm = tok
                    if which == 0:
                        stg = (tb * 2 + p) % 2
                        d = bf(A_QSTG + stg * 512, 512)
                        ev = P.op(
                            "scalar", lambda s, d=d, ps=ps, pp=p:
                            s.activation(d, ps, AF.Identity, bias=ff(Z_QBS + pp, 1),
                                         scale=0.125),
                            waits=[tok, t_qbs, t_qstg_free.get(stg)], inc="act")
                        t_qspill[(tb, p)] = cdma(
                            "sq", qdram.ap()[p * 128:(p + 1) * 128, tb * 512:(tb + 1) * 512],
                            d, waits=[ev])
                        t_qstg_free[stg] = t_qspill[(tb, p)]
                        t_qkevac[slot] = ev
                    else:
                        t_qkevac[slot] = P.op(
                            "scalar", lambda s, d=bf(A_KT[p] + tb * 512, 512), ps=ps, pp=p:
                            s.activation(d, ps, AF.Identity, bias=ff(Z_KBC + pp, 1),
                                         scale=1.0),
                            waits=[tok, t_qbs], inc="act")
                    qk_rot += 1
            for p in range(2):
                for tt in range(4):
                    ps = (c01, c23)[v_rot % 2]
                    for mt in range(8):
                        w = ([t_xtb[par], t_wsetup, t_vevac.get(v_rot % 2)]
                             if mt == 0 else [])
                        tok = P.op("tensor", lambda t, ps=ps,
                                   l=bf(A_XTB[par] + mt * 512 + tt * 128, 128),
                                   r=bf(A_VWB + mt * 256 + p * 128, 128),
                                   st=(mt == 0), sp=(mt == 7):
                                   t.matmul(ps[:, 0:128], l, r, start=st, stop=sp),
                                   waits=w, inc="pe" if mt == 7 else None)
                    last_mm = tok
                    ktg = tb * 4 + tt
                    ev = None
                    for hl in range(2):
                        ev = P.op(
                            "vector", lambda v, d=bf(A_V[2 * p + hl] + ktg * 65, 64),
                            ps=ps, hl=hl, p=p:
                            v.tensor_add(d, ps[:, hl * 64:hl * 64 + 64],
                                         ff(Z_VBBC + p * 128 + hl * 64, 64)),
                            waits=[tok, t_bsetup, t_setup_v], inc="dve")
                    t_vevac[v_rot % 2] = ev
                    v_rot += 1
            t_p1mm_lastx[par] = last_mm
        t_p1_act = ("act", P.cnt["act"])
        t_p1_dve = ("dve", P.cnt["dve"])

        steps = []
        block_last_step = {}
        for j in range(16):
            for pair in range(2):
                nkt = 4 * j + 4
                for kt in range(nkt):
                    steps.append((j, pair, kt, nkt))
                block_last_step[2 * j + pair] = len(steps) - 1
        NSTEP = len(steps)

        PP_IN = tuple(56576 + i * 1024 for i in range(4))
        PP_OUT = tuple(60672 + i * 1024 for i in range(4))
        PP_SC = tuple(AB.ap().bitcast(F32)[0:128, 32384 + i * 1024:33408 + i * 1024]
                      for i in range(4))
        t_pp_in = {}
        t_pp_sc = {}
        t_pp_stt = {}
        t_pp_st = {}

        def pp_loads(g):
            if g < 32:
                kt, qt = g // 4, g % 4
                src = bass.AP(w1q, kt * 128 * 4096 + qt * 1024, [[4096, 128], [1, 1024]])
                ssrc = bass.AP(w1s, kt * 4096 + qt * 1024, [[0, 128], [1, 1024]])
            else:
                ft = g - 32
                src = bass.AP(w2q, ft * 128 * DM, [[DM, 128], [1, DM]])
                ssrc = bass.AP(w2s, ft * DM, [[0, 128], [1, 1024]])
            t_pp_in[g] = gdma("pql", bf(PP_IN[g % 4], 1024), src,
                              waits=[t_pp_stt.get(g - 4)])
            t_pp_sc[g] = cdma("lc%d" % (g % 2), PP_SC[g % 4], ssrc,
                              waits=[t_pp_stt.get(g - 4)])

        def pp_work(g):
            t = P.op("vector", lambda v, g=g:
                     v.scalar_tensor_tensor(bf(PP_OUT[g % 4], 1024), bf(PP_IN[g % 4], 1024),
                                            -128.0, PP_SC[g % 4],
                                            ALU.add, ALU.mult),
                     waits=[t_pp_in[g], t_pp_sc[g], t_pp_st.get(g - 4)], inc="dve")
            t_pp_stt[g] = t
            dst = (w1d.ap()[0:128, g * 1024:(g + 1) * 1024] if g < 32 else
                   w2d.ap()[0:128, (g - 32) * 1024:(g - 31) * 1024])
            t_pp_st[g] = cdma("pst", dst, bf(PP_OUT[g % 4], 1024), waits=[t])

        t_qld = {}
        t_qk = {}
        t_exp = {}
        t_pv = {}
        t_mask = {}
        t_epi_cp = {}
        t_den = {}
        t_ag_slot = {}
        t_dstg_free = {}

        def emit_qload(b):
            j, pair = b // 2, b % 2
            w = [t_qspill[(j, pair)], t_qspill[(15, pair)]]
            if b >= 2:
                w.append(t_qk[block_last_step[b - 2]])
            t_qld[b] = sdma("lq", bf(A_QLD + (b % 2) * 512, 512),
                            qdram.ap()[pair * 128:(pair + 1) * 128,
                                       j * 512:(j + 1) * 512], waits=w)

        for b in range(2):
            emit_qload(b)

        def emit_qk(s):
            j, pair, kt, nkt = steps[s]
            b = 2 * j + pair
            ch = s % 3
            for hl in range(2):
                w = []
                if hl == 0:
                    w = [t_exp.get(s - 3), t_qld[b]]
                    if s < 3:
                        w.append(t_p1_act)
                tok = P.op("tensor", lambda t, ps=pL[ch], hl=hl,
                           l=bf(A_KT[pair] + kt * 128, 128, rows=64, row0=hl * 64),
                           r=bf(A_QLD + (b % 2) * 512, 512, rows=64, row0=hl * 64),
                           tp=(hl * 64, 0):
                           t.matmul(ps.ap()[:, hl * 512:hl * 512 + 512], l, r,
                                    start=True, stop=True, tile_position=tp),
                           waits=w, inc="pe" if hl == 1 else None)
            t_qk[s] = tok
            odd = (s % 2 == 1)
            tok = P.op(
                "scalar", lambda sc_, d=bf(A_SC[ch], 1024), ps=pL[ch]:
                sc_.activation(d, ps.ap()[:, :], AF.Exp),
                waits=[t_qk[s], t_pv.get(s - 3)],
                inc="act" if odd else None, amt=2)
            if odd:
                t_exp[s] = tok
                t_exp[s - 1] = tok

        def emit_pv(s):
            j, pair, kt, nkt = steps[s]
            b = 2 * j + pair
            ch = s % 3
            off = (kt - 4 * j) * 128
            offc = max(off, 0)
            if off >= 0 and s not in t_mask:
                tm = None
                for hl in range(2):
                    tm = P.op(
                        "vector", lambda v, d=bf(A_SC[ch] + hl * 512 + off, 128):
                        v.tensor_mul(d, d, cmask[:, :]),
                        waits=[t_exp[s], t_masks], inc="dve" if hl == 1 else None)
                t_mask[s] = tm
            gate = t_mask.get(s, t_exp[s])
            for hl in range(2):
                c = (c01, c23)[hl]
                w = [gate]
                if kt == 0:
                    if b == 0:
                        w.append(t_p1_dve)
                    else:
                        w += [t_epi_cp.get((b - 1, hl)), t_den.get((b - 1, hl))]
                tok = P.op("tensor", lambda t, c=c,
                           l=AB[:, A_V[2 * pair + hl] + kt * 65:A_V[2 * pair + hl] + kt * 65 + 65],
                           r=bf(A_SC[ch] + hl * 512 + offc, 512 - offc),
                           o0=offc, st=(kt == 0), sp=(kt == nkt - 1):
                           t.matmul(c.ap()[0:65, o0:512], l, r, start=st, stop=sp),
                           waits=w, inc="pe" if hl == 1 else None)
            t_pv[s] = tok
            if kt == nkt - 1:
                for hl in range(2):
                    c = (c01, c23)[hl]
                    sp_ = (2 * b + hl) % 4
                    t_cp = P.op(
                        "vector", lambda v, d=bf(A_STG[sp_], 512, rows=64), c=c:
                        v.tensor_copy(d, c.ap()[0:64, 0:512]),
                        waits=[t_pv[s], t_ag_slot.get(sp_)], inc="dve")
                    t_epi_cp[(b, hl)] = t_cp
                    row0 = (pair * 2 + hl) * 64
                    t_ag_slot[sp_] = gdma(
                        "sg", ag_in.ap()[row0:row0 + 64, j * 512:(j + 1) * 512],
                        bf(A_STG[sp_], 512, rows=64), waits=[t_cp])
                    t_cpd = P.op(
                        "vector", lambda v, d=ff(Z_DSTG[hl], 512, rows=1), c=c:
                        v.tensor_copy(d, c.ap()[64:65, 0:512]),
                        waits=[t_pv[s], t_dstg_free.get(hl)], inc="dve")
                    t_den[(b, hl)] = t_cpd
                    t_dstg_free[hl] = cdma(
                        "sd", ag_den_in.ap()[2 * pair + hl:2 * pair + hl + 1,
                                             j * 512:(j + 1) * 512],
                        ff(Z_DSTG[hl], 512, rows=1), waits=[t_cpd])

        pp_l = 0
        pp_w = 0
        for s in range(NSTEP + 2):
            if s < NSTEP:
                emit_qk(s)
                if s % 17 == 0 and pp_l < 64:
                    pp_loads(pp_l)
                    pp_l += 1
                if s % 17 == 8 and pp_w < 64 and pp_l >= pp_w + 4:
                    pp_work(pp_w)
                    pp_w += 1
                j, pair, kt, nkt = steps[s]
                b = 2 * j + pair
                if kt == nkt - 1 and (b + 2) < 32 and (b + 2) not in t_qld:
                    emit_qload(b + 2)
            if s >= 2:
                emit_pv(s - 2)
        while pp_w < 64:
            pp_work(pp_w)
            pp_w += 1
        t_p2_act = ("act", P.cnt["act"])
        t_p2_pe = ("pe", P.cnt["pe"])
        t_ag_w = [("sg", P.cnt["sg"]), ("sd", P.cnt["sd"])]

        t_cc2 = P.op("gpsimd", lambda g: g.collective_compute(
            "AllGather", ALU.bypass,
            replica_groups=[[0, 1, 2, 3], [4, 5, 6, 7]],
            ins=[ag_den_in.ap().opt()], outs=[ag_den_out.ap().opt()]),
            waits=t_ag_w, inc="cc", amt=1, dma=True)
        t_cc1 = P.op("gpsimd", lambda g: g.collective_compute(
            "AllGather", ALU.bypass,
            replica_groups=[[0, 1, 2, 3], [4, 5, 6, 7]],
            ins=[ag_in.ap().opt()], outs=[ag_out.ap().opt()]),
            waits=[], inc="cc", amt=1, dma=True)

        t_w1ld = {}
        t_w2ld = {}

        def w1_data(kt, waits=()):
            t_w1ld[kt] = sdma("lw1", bf(A_W1 + kt * 4096, 4096),
                              w1d.ap()[0:128, kt * 4096:(kt + 1) * 4096],
                              waits=list(waits) + [t_pp_st[63]])

        def w2_data(c, waits=()):
            t_w2ld[c] = sdma("lw2", bf(A_W2 + c * 4096, 4096),
                             w2d.ap()[0:128, c * 4096:(c + 1) * 4096],
                             waits=list(waits) + [t_pp_st[63]])

        for kt in range(3):
            w1_data(kt, waits=[t_p2_pe])
        for c in range(2, 8):
            w2_data(c)

        t_la = {}

        def _ld_chunk(k):
            def go(g):
                rv = nc.values_load(rksb[0:1, 0:1], engines=[mybir.EngineType.SP],
                                    min_val=0, max_val=S - TPC,
                                    skip_runtime_bounds_check=True)
                return g.dma_start(out=bf(A_NRM + k * 2048, 2048),
                                   in_=ag_out[128 * k:128 * (k + 1),
                                              bass.DynSlice(rv, TPC)])
            return go

        den_view = AB.ap().bitcast(F32)[0:16, A_DEN // 2:A_DEN // 2 + 2048]

        def _ld_den(g):
            rv = nc.values_load(rksb[0:1, 0:1], engines=[mybir.EngineType.SP],
                                min_val=0, max_val=S - TPC,
                                skip_runtime_bounds_check=True)
            return g.dma_start(out=den_view,
                               in_=ag_den_out[0:16, bass.DynSlice(rv, TPC)])

        t_laden = P.op("sync", _ld_den, waits=[("cc", 1)], inc="la", amt=16, dma=True)

        for k in range(8):
            t_la[k] = P.op("sync", _ld_chunk(k), waits=[("cc", 2)] if k == 0 else [],
                           inc="la", amt=16, dma=True)

        P.op("gpsimd", lambda g: g.iota(bf(A_SELSCR, 1024, rows=16), [[1, 16], [0, 64]],
                                        base=0, channel_multiplier=-1,
                                        allow_small_or_imprecise_dtypes=True), inc="gp")
        t_iosel = ("gp", P.cnt["gp"])
        t_sel = P.op("vector", lambda v: v.tensor_single_scalar(
            bf(A_SEL, 1024, rows=16), bf(A_SELSCR, 1024, rows=16), 0.0, ALU.is_equal),
            waits=[t_iosel], inc="dve")

        cdma("lgb", ff(Z_GAM, 1024), bass.AP(gam, 0, [[0, 128], [1, 1024]]))
        t_gab = cdma("lgb", ff(Z_BET, 1024), bass.AP(bet, 0, [[0, 128], [1, 1024]]))

        FT_ORDER = list(range(8, 32)) + list(range(8))

        def _recip(v):
            with nc.allow_low_precision(reason="bf16 divisor: becomes a bf16 "
                                        "matmul operand like every other matmul "
                                        "input; numerators are bf16 already"):
                return v.reciprocal(bf(A_RHI, 2048, rows=16), den_view)

        t_recip = P.op("vector", _recip, waits=[t_laden], inc="dve")
        t_div = {}
        t_nrm_mult = {}
        for k in range(8):
            for h in range(2):
                par = (k * 2 + h) % 2
                w = [t_recip, t_sel, t_la[k], t_nrm_mult.get(par)]
                if k == 0 and h == 0:
                    w.append(t_p2_act)
                for q in range(2):
                    tok = P.op(
                        "tensor", lambda t, ps=pL[par], k=k, h=h, q=q:
                        t.matmul(ps.ap()[:, q * 512:(q + 1) * 512],
                                 bf(A_SEL + k * 128, 128, rows=16),
                                 bf(A_RHI + h * 1024 + q * 512, 512, rows=16),
                                 start=True, stop=True),
                        waits=w if q == 0 else [], inc="pe" if q == 1 else None)
                t_div[(k, h)] = tok
                t_nrm_mult[par] = P.op(
                    "vector", lambda v, d=bf(A_NRM + k * 2048 + h * 1024, 1024),
                    ps=pL[par]:
                    v.tensor_mul(d, d, ps.ap()[:, :]),
                    waits=[t_div[(k, h)], t_la[k]], inc="dve")
        t_nrm_done = ("dve", P.cnt["dve"])
        for kt in range(3, 7):
            w1_data(kt, waits=[t_nrm_done])

        t_xr = {}
        t_o_mm = {}
        t_x1pre = {}
        t_x1sp_cnt = {0: 0, 1: 0}
        t_x1n_cnt = {0: 0, 1: 0}
        t_sx_tok = {}
        t_sn_tok = {}
        t_apply = {}
        t_oproj_done = None
        for tt in range(16):
            par = tt % 2
            ps = pL[par]
            t_xr[par] = gdma("lr%d" % par, ff(Z_XRES[par], 1024),
                             x_res.ap()[tt * 128:(tt + 1) * 128, :],
                             waits=[("sx%d" % par, t_x1sp_cnt[par] * 16)])
            for ht in range(8):
                for mc in range(2):
                    w = ([t_nrm_done, t_x1pre.get(par)]
                         if (ht == 0 and mc == 0) else [])
                    P.op("tensor", lambda t, ps=ps, l=bf(A_NRM + ht * 2048 + tt * 128, 128),
                         r=bf(A_OW + ht * 1024 + mc * 512, 512), mc=mc, st=(ht == 0):
                         t.matmul(ps.ap()[:, mc * 512:mc * 512 + 512], l, r,
                                  start=st, stop=False),
                         waits=w, inc=None)
            for mc in range(2):
                tok = P.op("tensor", lambda t, ps=ps, mc=mc:
                           t.matmul(ps.ap()[:, mc * 512:mc * 512 + 512], onesb[0:1, :],
                                    ob_sb[0:1, mc * 512:mc * 512 + 512],
                                    start=False, stop=True),
                           waits=[], inc="pe" if mc == 1 else None)
            t_o_mm[par] = tok
            if tt == 15:
                t_oproj_done = tok
            t_pre = P.op("vector", lambda v, ps=ps, par=par:
                         v.tensor_add(ff(Z_UPRE[par], 1024), ps.ap()[:, :],
                                      ff(Z_XRES[par], 1024)),
                         waits=[t_o_mm[par], t_xr[par], t_apply.get(par)]
                         + ([("dp%d" % par, P.cnt.get("dp%d" % par, 0))] if DEBUG else []),
                         inc="dve")
            t_x1pre[par] = t_pre
            if DEBUG:
                cdma("dp%d" % par, dbg_pre.ap()[tt * 128:(tt + 1) * 128, :],
                     ff(Z_UPRE[par], 1024), waits=[t_pre])
            P.op("vector", lambda v, par=par: v.bn_stats(ff(Z_BN6P[par], 6),
                                                         ff(Z_UPRE[par], 512)),
                 inc="dve")
            P.op("vector", lambda v, par=par:
                 v.bn_stats(ff(Z_BN6P[par] + 6, 6), ff(Z_UPRE[par] + 512, 512)),
                 inc="dve")
            P.op("vector", lambda v, par=par: v.bn_aggr(ff(Z_MVP[par], 2),
                                                        ff(Z_BN6P[par], 12)), inc="dve")
            P.op("vector", lambda v, par=par: v.tensor_scalar_add(
                ff(Z_SDP[par], 1), ff(Z_MVP[par] + 1, 1), EPS1), inc="dve")
            P.op("vector", lambda v, par=par: v.tensor_scalar(
                ff(Z_SDP[par] + 1, 1), ff(Z_MVP[par] + 1, 1),
                1.0 + EPS1, EPS1 * EPS1, ALU.mult, ALU.add), inc="dve")
            t_v = ("dve", P.cnt["dve"])
            t_sq = P.op("scalar", lambda s, par=par: s.activation(
                ff(Z_SDP[par], 2), ff(Z_SDP[par], 2), AF.Sqrt),
                waits=[t_v], inc="act")
            P.op("vector", lambda v, par=par: v.reciprocal(ff(Z_RSTP[par], 2),
                                                           ff(Z_SDP[par], 2)),
                 waits=[t_sq], inc="dve")
            t_nm = P.op("vector", lambda v, par=par: v.tensor_scalar(
                ff(Z_NMP[par], 2), ff(Z_RSTP[par], 2),
                ff(Z_MVP[par], 1), -1.0, ALU.mult, ALU.mult), inc="dve")
            t_x1 = P.op("scalar", lambda s, par=par: s.activation(
                ff(Z_XRES[par], 1024), ff(Z_UPRE[par], 1024), AF.Identity,
                bias=ff(Z_NMP[par], 1), scale=ff(Z_RSTP[par], 1)),
                waits=[t_nm], inc="act")
            t_sx_tok[tt] = cdma("sx%d" % par, x1_sp.ap()[tt * 128:(tt + 1) * 128, :],
                                ff(Z_XRES[par], 1024), waits=[t_x1])
            t_x1sp_cnt[par] += 1
            t_xn = P.op("scalar", lambda s, par=par: s.activation(
                bf(A_X1N[par], 1024), ff(Z_UPRE[par], 1024), AF.Identity,
                bias=ff(Z_NMP[par] + 1, 1), scale=ff(Z_RSTP[par] + 1, 1)),
                waits=[t_nm, ("sn%d" % par, t_x1n_cnt[par] * 16)], inc="act")
            t_apply[par] = t_xn
            P.op("vector", lambda v, par=par:
                 v.tensor_mul(bf(A_X1N[par], 1024), bf(A_X1N[par], 1024),
                              ff(Z_GAM, 1024)), waits=[t_xn, t_gab], inc="dve")
            t_n = P.op("vector", lambda v, par=par:
                       v.tensor_add(bf(A_X1N[par], 1024), bf(A_X1N[par], 1024),
                                    ff(Z_BET, 1024)), inc="dve")
            t_sn_tok[tt] = cdma("sn%d" % par, x1n_sp.ap()[tt * 128:(tt + 1) * 128, :],
                                bf(A_X1N[par], 1024), waits=[t_n])
            t_x1n_cnt[par] += 1
        t_sn_all = [("sn0", t_x1n_cnt[0] * 16), ("sn1", t_x1n_cnt[1] * 16)]
        t_sx_all = [("sx0", t_x1sp_cnt[0] * 16), ("sx1", t_x1sp_cnt[1] * 16)]

        w1_data(7, waits=[t_oproj_done])
        for c in range(2):
            w2_data(c, waits=[t_oproj_done])
        t_zfree = {0: t_x1pre[0], 1: t_x1pre[1]}

        ctr = pL2.ap().bitcast(BF16)
        t_x1nld = {}
        t_tr_cp = None
        t_tr_cp_h = {}
        t_fc1 = {}
        t_x1ld = {}
        t_outts = {}
        t_so_cnt = {0: 0, 1: 0}
        for blk in range(8):
            for tsl in range(2):
                t_x1nld[tsl] = sdma(
                    "ln%d" % tsl, bf(A_X1NR[tsl], 1024),
                    x1n_sp.ap()[blk * 256 + tsl * 128: blk * 256 + (tsl + 1) * 128, :],
                    waits=[t_sn_tok[blk * 2 + tsl], t_tr_cp_h.get(tsl)])
            for tsl in range(2):
                for mt in range(8):
                    w = ([t_x1nld[tsl], t_masks, t_tr_cp_h.get(tsl)]
                         if mt == 0 else [])
                    t_tr = P.op("tensor", lambda t, tsl=tsl, mt=mt:
                                t.transpose(ctr[:, tsl * 1024 + mt * 128:
                                                tsl * 1024 + (mt + 1) * 128],
                                            bf(A_X1NR[tsl] + mt * 128, 128), ident[:, :]),
                                waits=w, inc="pe" if mt == 7 else None)
                dst = AB[:, A_NTB:A_NTB + 2048].rearrange(
                    "p (m c) -> p m c", m=8)[:, :, tsl * 128:(tsl + 1) * 128]
                srcp = ctr[:, tsl * 1024:tsl * 1024 + 1024].rearrange(
                    "p (m c) -> p m c", m=8)
                t_tr_cp_h[tsl] = P.op("vector", lambda v, d=dst, s=srcp:
                                      v.tensor_copy(d, s), waits=[t_tr], inc="dve")
            t_tr_cp = t_tr_cp_h[1]
            for ft in range(32):
                cbank = (c01, c23)[ft % 2]
                for kt in range(8):
                    w = [t_w1ld[kt]] if blk == 0 else []
                    if kt == 0:
                        w += [t_tr_cp_h.get(0), t_tr_cp_h.get(1), t_fc1.get(ft % 2)]
                    tok = P.op("tensor", lambda t, ps=cbank,
                               l=bf(A_W1 + kt * 4096 + ft * 128, 128),
                               r=bf(A_NTB + kt * 256, 256), st=(kt == 0):
                               t.matmul(ps.ap()[:, 0:256], l, r, start=st, stop=(kt == 7)),
                               waits=w, inc="pe" if kt == 7 else None)
                t_fc1[ft % 2] = P.op(
                    "scalar", lambda s, ps=cbank, d=bf(A_YT + ft * 256, 256), ft=ft:
                    s.activation(d, ps.ap()[:, 0:256], AF.Silu, bias=ff(Z_W1BT + ft, 1)),
                    waits=[tok, t_bsetup], inc="act")
            t_y = ("act", P.cnt["act"])
            for tsub in range(2):
                g = blk * 2 + tsub
                par = g % 2
                ps = pL[par]
                for hc in range(2):
                    for i, ft in enumerate(FT_ORDER):
                        w = []
                        if blk == 0:
                            w.append(t_w2ld[ft // 4])
                        if hc == 0 and i == 0:
                            w += [t_y, t_zfree.get(par)]
                        P.op("tensor", lambda t, ps=ps,
                             l=bf(A_YT + ft * 256 + tsub * 128, 128),
                             r=bf(A_W2 + ft * 1024 + hc * 512, 512),
                             hc=hc, st=(i == 0):
                             t.matmul(ps.ap()[:, hc * 512:hc * 512 + 512], l, r,
                                      start=st, stop=False),
                             waits=w, inc=None)
                    tok = P.op("tensor", lambda t, ps=ps, hc=hc:
                               t.matmul(ps.ap()[:, hc * 512:hc * 512 + 512], onesb[0:1, :],
                                        w2b_sb[0:1, hc * 512:hc * 512 + 512],
                                        start=False, stop=True),
                               waits=[], inc="pe" if hc == 1 else None)
                t_zmm = tok
                t_x1ld[par] = gdma("lr%d" % par, ff(Z_XRES[par], 1024),
                                   x1_sp.ap()[g * 128:(g + 1) * 128, :],
                                   waits=[t_sx_tok[g], t_zfree.get(par),
                                          ("so%d" % par, t_so_cnt[par] * 16)])
                t_uu = P.op("vector", lambda v, ps=ps, par=par:
                            v.scalar_tensor_tensor(ff(Z_UPRE[par], 1024), ps.ap()[:, :], 0.25,
                                                   ff(Z_XRES[par], 1024), ALU.mult, ALU.add),
                            waits=[t_zmm, t_x1ld[par], t_outts.get(par)]
                            + ([t_apply.get(par)] if blk == 0 else []), inc="dve")
                t_zfree[par] = t_uu
                P.op("vector", lambda v, par=par:
                     v.bn_stats(ff(Z_BN6P[par], 6), ff(Z_UPRE[par], 512)), inc="dve")
                P.op("vector", lambda v, par=par:
                     v.bn_stats(ff(Z_BN6P[par] + 6, 6), ff(Z_UPRE[par] + 512, 512)), inc="dve")
                P.op("vector", lambda v, g=g:
                     v.bn_aggr(ff(Z_MV2A + 2 * g, 2), ff(Z_BN6P[g % 2], 12)), inc="dve")
                P.op("vector", lambda v, par=par, g=g: v.tensor_scalar_add(
                    ff(Z_SDP[par], 1), ff(Z_MV2A + 2 * g + 1, 1), EPS2), inc="dve")
                t_v = ("dve", P.cnt["dve"])
                t_sq = P.op("scalar", lambda s, par=par: s.activation(
                    ff(Z_SDP[par], 1), ff(Z_SDP[par], 1), AF.Sqrt),
                    waits=[t_v], inc="act")
                t_rs = P.op("vector", lambda v, par=par: v.reciprocal(
                    ff(Z_RSTP[par], 1), ff(Z_SDP[par], 1)), waits=[t_sq], inc="dve")
                t_outts[par] = P.op("vector", lambda v, par=par, g=g:
                                    v.tensor_scalar(ff(Z_XRES[par], 1024),
                                                    ff(Z_UPRE[par], 1024),
                                                    ff(Z_MV2A + 2 * g, 1),
                                                    ff(Z_RSTP[par], 1),
                                                    ALU.subtract, ALU.mult),
                                    waits=[t_rs], inc="dve")
                cdma("so%d" % par, out.ap()[g * 128:(g + 1) * 128, :],
                     ff(Z_XRES[par], 1024), waits=[t_outts[par]])
                t_so_cnt[par] += 1
        if DEBUG:
            dmap = {
                "d_agin": (ag_in, t_ag_w),
                "d_agden": (ag_den_in, t_ag_w),
                "d_agout": (ag_out, [("cc", 1)]),
                "d_agdout": (ag_den_out, [("cc", 2)]),
                "d_w1d": (w1d, [("pst", 64 * 16)]),
                "d_w2d": (w2d, [("pst", 64 * 16)]),
                "d_x1": (x1_sp, t_sx_all),
                "d_x1n": (x1n_sp, t_sn_all),
            }
            for nm, (src_, w) in dmap.items():
                dt_ = nc.dram_tensor(nm, list(src_.shape), src_.dtype, kind="ExternalOutput")
                P.op("gpsimd", lambda g, d=dt_, s=src_: g.dma_start(out=d.ap(), in_=s.ap()),
                     waits=w, inc="lg", amt=16, dma=True)
            P.op("gpsimd", lambda g: g.nop(), waits=[("lg", P.cnt["lg"])])

        P.op("sync", lambda s: s.nop(),
             waits=[("so0", t_so_cnt[0] * 16), ("so1", t_so_cnt[1] * 16)])

        @block.sync
        def _(engobj):
            P.emit("sync", engobj, sems)

        @block.gpsimd
        def _(engobj):
            P.emit("gpsimd", engobj, sems)

        @block.tensor
        def _(engobj):
            P.emit("tensor", engobj, sems)

        @block.vector
        def _(engobj):
            P.emit("vector", engobj, sems)

        @block.scalar
        def _(engobj):
            P.emit("scalar", engobj, sems)

    return nc


def kernel(**inputs):
    x = np.asarray(inputs["L0_attn_x"])
    Qw = np.asarray(inputs["L0_attn_Q_w"])
    Qb = np.asarray(inputs["L0_attn_Q_b"])
    Kw = np.asarray(inputs["L0_attn_K_w"])
    Kb = np.asarray(inputs["L0_attn_K_b"])
    Vw = np.asarray(inputs["L0_attn_V_w"])
    Vb = np.asarray(inputs["L0_attn_V_b"])
    Ow = np.asarray(inputs["L0_attn_O_w"])
    Ob = np.asarray(inputs["L0_attn_O_b"])
    gam = np.asarray(inputs["L0_ffn_ln_gamma"])
    bet = np.asarray(inputs["L0_ffn_ln_beta"])
    w1q = np.ascontiguousarray(inputs["L0_ffn_fc1_q"])
    w1s = np.asarray(inputs["L0_ffn_fc1_scales"])
    w1b = np.asarray(inputs["L0_ffn_fc1_bias"])
    w2q = np.ascontiguousarray(inputs["L0_ffn_fc2_q"])
    w2s = np.asarray(inputs["L0_ffn_fc2_scales"])
    w2b = np.asarray(inputs["L0_ffn_fc2_bias"])

    nc = build_nc()
    owf = np.ascontiguousarray(Ow.reshape(1024, 1024))
    w1bT = np.ascontiguousarray(w1b.reshape(32, 128).T)
    in_maps = []
    for c in range(NCORE):
        b, r = c // 4, c % 4
        hs = slice(4 * r, 4 * r + 4)
        in_maps.append({
            "xT": np.ascontiguousarray(x[b].T),
            "x_res": np.ascontiguousarray(x[b, r * TPC:(r + 1) * TPC]),
            "qw": np.ascontiguousarray(Qw[hs].transpose(1, 0, 2).reshape(DM, 256)),
            "kw": np.ascontiguousarray(Kw[hs].transpose(1, 0, 2).reshape(DM, 256)),
            "vw": np.ascontiguousarray(Vw[hs].transpose(1, 0, 2).reshape(DM, 256)),
            "qbT": np.ascontiguousarray(Qb[hs].reshape(2, 128).T),
            "kbT": np.ascontiguousarray(Kb[hs].reshape(2, 128).T),
            "vb": np.ascontiguousarray(Vb[hs].reshape(2, 128)),
            "ow": owf, "ob": Ob, "gam": gam, "bet": bet,
            "w1q": w1q, "w1s": w1s, "w1bT": w1bT,
            "w2q": w2q, "w2s": w2s, "w2b": w2b,
            "rk": (r * TPC + 128 * np.arange(16, dtype=np.int32)).reshape(1, 16),
        })
    res = run_bass_kernel_spmd(nc, in_maps, core_ids=list(range(NCORE)))
    globals()["_last_res"] = res
    outarr = np.empty((2, S, DM), dtype=np.float32)
    for c in range(NCORE):
        b, r = c // 4, c % 4
        outarr[b, r * TPC:(r + 1) * TPC] = res.results[c]["out"]
    return outarr


# version 19
# speedup vs baseline: 1.1329x; 1.0047x over previous
import numpy as np
import concourse.bass as bass
import concourse.mybir as mybir
from concourse.bass_utils import run_bass_kernel_spmd

F32 = mybir.dt.float32
BF16 = mybir.dt.bfloat16
I32 = mybir.dt.int32
AF = mybir.ActivationFunctionType
ALU = mybir.AluOpType

S = 8192
DM = 1024
NCORE = 8
TPC = 2048
EPS1 = 1e-5
EPS2 = EPS1 / 4.0
DEBUG = False

A_QSTG = 0
A_QLD = 0
A_KT = (1024, 9216)
A_V = tuple(17408 + h * 4160 for h in range(4))
A_XTB = (34048, 38144)
A_SC = (34048, 35072, 36096)
A_DEN = 34048
A_STG = (38144, 38656, 39168, 39680)
A_SELSCR = 38144
A_RHI = 39168
A_SEL = 41216
A_QWB, A_KWB, A_VWB = 42240, 44288, 46336
A_OW = 48384
A_NRM = 0
A_X1N = (16384, 17408)
A_W1 = 18432
A_W2 = 51200
A_YT = 0
A_NTB = 8192
A_X1NR = (10240, 11264)
ARENA_BF = 83968

Z_SCALE = (0, 1024)
Z_XRES = (2048, 3072)
Z_UPRE = (4096, 5120)
Z_GAM = 6144
Z_BET = 7168
Z_VBBC = 8192
Z_W1BT = 8448
Z_MV2A = 8480
Z_VEPS16 = 8512
Z_RSTD16 = 8528
Z_BN6P = (8544, 8556)
Z_MVP = (8568, 8570)
Z_SDP = (8572, 8574)
Z_RSTP = (8576, 8578)
Z_NMP = (8580, 8582)
Z_QBS = 8584
Z_KBC = 8586
Z_DSTG = (8592, 9104)
ARENA_F32 = 9616

SEM_NAMES = (
    "pe", "act", "dve", "gp", "cc",
    "ls",
    "lg",
    "lq",
    "sq",
    "lx0", "lx1",
    "lw1", "lw2",
    "pql", "pst",
    "lgb",
    "lc0", "lc1",
    "la",
    "lr0", "lr1",
    "lp0", "lp1",
    "ln0", "ln1",
    "sg",
    "sd",
    "sx0", "sx1",
    "sn0", "sn1",
    "su0", "su1",
    "so0", "so1",
    "dp0", "dp1",
)


class Plan:

    DRAIN_ENGINES = ("vector", "scalar", "gpsimd")

    def __init__(self):
        self.ops = []
        self.cnt = {}

    def op(self, eng, build, waits=(), inc=None, amt=1, dma=False):
        tok = None
        if inc is not None:
            self.cnt[inc] = self.cnt.get(inc, 0) + amt
            tok = (inc, self.cnt[inc])
        ws = tuple((s, v) for w in waits if w is not None for (s, v) in (w,))
        self.ops.append((eng, build, ws, inc, amt, dma))
        return tok

    def emit(self, section, engobj, sems):
        hw = {}
        for eng, build, waits, inc, amt, dma in self.ops:
            if eng != section:
                continue
            for s, v in waits:
                if v > hw.get(s, 0):
                    engobj.wait_ge(sems[s], v)
                    hw[s] = v
            ins = build(engobj)
            if inc is not None:
                if not dma and eng in self.DRAIN_ENGINES:
                    engobj.drain().then_inc(sems[inc], amt)
                else:
                    ins.then_inc(sems[inc], amt)


def build_nc():
    nc = bass.Bass("TRN2", target_bir_lowering=False)

    xT = nc.dram_tensor("xT", [DM, S], F32, kind="ExternalInput")
    x_res = nc.dram_tensor("x_res", [TPC, DM], F32, kind="ExternalInput")
    qw = nc.dram_tensor("qw", [DM, 256], F32, kind="ExternalInput")
    kw = nc.dram_tensor("kw", [DM, 256], F32, kind="ExternalInput")
    vw = nc.dram_tensor("vw", [DM, 256], F32, kind="ExternalInput")
    qbT = nc.dram_tensor("qbT", [128, 2], F32, kind="ExternalInput")
    kbT = nc.dram_tensor("kbT", [128, 2], F32, kind="ExternalInput")
    vb = nc.dram_tensor("vb", [2, 128], F32, kind="ExternalInput")
    ow = nc.dram_tensor("ow", [DM, DM], F32, kind="ExternalInput")
    ob = nc.dram_tensor("ob", [DM], F32, kind="ExternalInput")
    gam = nc.dram_tensor("gam", [DM], F32, kind="ExternalInput")
    bet = nc.dram_tensor("bet", [DM], F32, kind="ExternalInput")
    w1q = nc.dram_tensor("w1q", [DM, 4096], I32, kind="ExternalInput")
    w1s = nc.dram_tensor("w1s", [8, 4096], F32, kind="ExternalInput")
    w1bT = nc.dram_tensor("w1bT", [128, 32], F32, kind="ExternalInput")
    w2q = nc.dram_tensor("w2q", [4096, DM], I32, kind="ExternalInput")
    w2s = nc.dram_tensor("w2s", [32, DM], F32, kind="ExternalInput")
    w2b = nc.dram_tensor("w2b", [DM], F32, kind="ExternalInput")
    rk = nc.dram_tensor("rk", [1, 16], I32, kind="ExternalInput")
    out = nc.dram_tensor("out", [TPC, DM], F32, kind="ExternalOutput")

    qdram = nc.dram_tensor("qdram", [256, S], BF16, kind="Internal")
    w1d = nc.dram_tensor("w1d", [128, 32768], BF16, kind="Internal")
    w2d = nc.dram_tensor("w2d", [128, 32768], BF16, kind="Internal")
    ag_in = nc.dram_tensor("ag_in", [256, S], BF16, kind="Internal")
    ag_den_in = nc.dram_tensor("ag_den_in", [4, S], F32, kind="Internal")
    ag_out = nc.dram_tensor("ag_out", [DM, S], BF16, kind="Internal", addr_space="Local")
    ag_den_out = nc.dram_tensor("ag_den_out", [16, S], F32, kind="Internal",
                                addr_space="Local")
    x1_sp = nc.dram_tensor("x1_sp", [TPC, DM], F32, kind="Internal")
    x1n_sp = nc.dram_tensor("x1n_sp", [TPC, DM], BF16, kind="Internal")
    dbg_pre = (nc.dram_tensor("d_pre", [TPC, DM], F32, kind="ExternalOutput")
               if DEBUG else None)

    from contextlib import ExitStack
    with ExitStack() as ctx:
        e = ctx.enter_context
        AB = e(nc.sbuf_tensor("AB", [128, ARENA_BF], BF16))
        AZ = e(nc.sbuf_tensor("AZ", [128, ARENA_F32], F32))
        cmask = e(nc.sbuf_tensor("cmask", [128, 128], BF16))
        ident = e(nc.sbuf_tensor("ident", [128, 128], BF16))
        onesb = e(nc.sbuf_tensor("onesb", [1, 128], BF16))
        ob_sb = e(nc.sbuf_tensor("ob_sb", [1, 1024], BF16))
        w2b_sb = e(nc.sbuf_tensor("w2b_sb", [1, 1024], BF16))
        iotf = e(nc.sbuf_tensor("iotf", [128, 128], F32))
        rksb = e(nc.sbuf_tensor("rksb", [1, 16], I32))
        pL0 = e(nc.psum_tensor("pL0", [128, 1024], F32))
        pL1 = e(nc.psum_tensor("pL1", [128, 1024], F32))
        pL2 = e(nc.psum_tensor("pL2", [128, 1024], F32))
        c01 = e(nc.psum_tensor("c01", [128, 512], F32))
        c23 = e(nc.psum_tensor("c23", [128, 512], F32))
        pL = (pL0, pL1, pL2)
        sems = {}
        for s in SEM_NAMES:
            sems[s] = e(nc.semaphore(s))
        block = e(nc.Block())

        P = Plan()

        def bf(off, width, rows=128, row0=0):
            return AB[row0:row0 + rows, off:off + width]

        def ff(off, width, rows=128, row0=0):
            return AZ[row0:row0 + rows, off:off + width]

        def sdma(sem, dst, src, waits=()):
            return P.op("sync", lambda g, d=dst, s=src: g.dma_start(out=d, in_=s),
                        waits=waits, inc=sem, amt=16, dma=True)

        def gdma(sem, dst, src, waits=()):
            return P.op("gpsimd", lambda g, d=dst, s=src: g.dma_start(out=d, in_=s),
                        waits=waits, inc=sem, amt=16, dma=True)

        def cdma(sem, dst, src, waits=()):
            return P.op("scalar", lambda g, d=dst, s=src: g.dma_start(out=d, in_=s),
                        waits=waits, inc=sem, amt=16, dma=True)

        P.op("gpsimd", lambda g: g.iota(iotf[:, :], [[1, 128]], base=0,
                                        channel_multiplier=-1,
                                        allow_small_or_imprecise_dtypes=True), inc="gp")
        t_iota = ("gp", P.cnt["gp"])
        P.op("vector", lambda v: v.tensor_single_scalar(cmask[:, :], iotf[:, :], 0.0, ALU.is_ge),
             waits=[t_iota], inc="dve")
        P.op("vector", lambda v: v.tensor_single_scalar(ident[:, :], iotf[:, :], 0.0, ALU.is_equal),
             waits=[t_iota], inc="dve")
        t_masks = ("dve", P.cnt["dve"])
        P.op("vector", lambda v: v.memset(onesb[:, :], 1.0), inc="dve")
        for h in range(4):
            ap = AB[:, A_V[h]:A_V[h] + 4160].rearrange("p (k c) -> p k c", c=65)[:, :, 64:65]
            P.op("vector", lambda v, ap=ap: v.memset(ap, 1.0), inc="dve")
        t_setup_v = ("dve", P.cnt["dve"])

        def r_tp(t):
            return t.ap().rearrange("(t p) c -> p t c", p=128)

        def bf3(off, width, inner):
            return bf(off, width).rearrange("p (t c) -> p t c", c=inner)

        gdma("lg", bf3(A_QWB, 2048, 256), r_tp(qw))
        gdma("lg", bf3(A_KWB, 2048, 256), r_tp(kw))
        gdma("lg", bf3(A_VWB, 2048, 256), r_tp(vw))
        gdma("lg", bf3(A_OW, 8192, 1024), r_tp(ow))
        gdma("lg", ob_sb[:, :], ob.ap().rearrange("(a c) -> a c", a=1))
        gdma("lg", w2b_sb[:, :], w2b.ap().rearrange("(a c) -> a c", a=1))
        t_wsetup = ("lg", P.cnt["lg"])
        sdma("ls", ff(Z_QBS, 2), qbT[:, :])
        sdma("ls", ff(Z_KBC, 2), kbT[:, :])
        for p in range(2):
            sdma("ls", ff(Z_VBBC + p * 128, 128), bass.AP(vb, p * 128, [[0, 128], [1, 128]]))
        sdma("ls", ff(Z_W1BT, 32), w1bT[:, :])
        sdma("ls", rksb[:, :], rk[:, :])
        t_bsetup = ("ls", P.cnt["ls"])
        P.op("vector", lambda v: v.tensor_scalar_mul(ff(Z_QBS, 2), ff(Z_QBS, 2), 0.125),
             waits=[t_bsetup], inc="dve")
        t_qbs = ("dve", P.cnt["dve"])

        NTB = 16
        qk_rot = 0
        v_rot = 0
        t_qkevac = {}
        t_vevac = {}
        t_xtb = {}
        t_p1mm_lastx = {}
        t_qspill = {}
        t_qstg_free = {}
        qk_slots = (pL0.ap()[:, 0:512], pL0.ap()[:, 512:1024],
                    pL1.ap()[:, 0:512], pL1.ap()[:, 512:1024])
        for tb in range(NTB):
            par = tb % 2
            src = bass.AP(xT, tb * 512, [[S, 128], [128 * S, 8], [1, 512]])
            t_xtb[par] = gdma("lx%d" % par, bf(A_XTB[par], 4096), src,
                              waits=[t_p1mm_lastx.get(par)])
            last_mm = None
            for which in range(2):
                woff = (A_QWB, A_KWB)[which]
                bz = (Z_QBS, Z_KBC)[which]
                sc = (0.125, 1.0)[which]
                for p in range(2):
                    slot = qk_rot % 4
                    ps = qk_slots[slot]
                    for mt in range(8):
                        w = ([t_xtb[par], t_wsetup, t_qkevac.get(slot)]
                             if mt == 0 else [])
                        tok = P.op("tensor", lambda t, ps=ps,
                                   l=bf(woff + mt * 256 + p * 128, 128),
                                   r=bf(A_XTB[par] + mt * 512, 512),
                                   st=(mt == 0), sp=(mt == 7):
                                   t.matmul(ps, l, r, start=st, stop=sp),
                                   waits=w, inc="pe" if mt == 7 else None)
                    last_mm = tok
                    if which == 0:
                        stg = (tb * 2 + p) % 2
                        d = bf(A_QSTG + stg * 512, 512)
                        ev = P.op(
                            "scalar", lambda s, d=d, ps=ps, pp=p:
                            s.activation(d, ps, AF.Identity, bias=ff(Z_QBS + pp, 1),
                                         scale=0.125),
                            waits=[tok, t_qbs, t_qstg_free.get(stg)], inc="act")
                        t_qspill[(tb, p)] = cdma(
                            "sq", qdram.ap()[p * 128:(p + 1) * 128, tb * 512:(tb + 1) * 512],
                            d, waits=[ev])
                        t_qstg_free[stg] = t_qspill[(tb, p)]
                        t_qkevac[slot] = ev
                    else:
                        t_qkevac[slot] = P.op(
                            "scalar", lambda s, d=bf(A_KT[p] + tb * 512, 512), ps=ps, pp=p:
                            s.activation(d, ps, AF.Identity, bias=ff(Z_KBC + pp, 1),
                                         scale=1.0),
                            waits=[tok, t_qbs], inc="act")
                    qk_rot += 1
            for p in range(2):
                for tt in range(4):
                    ps = (c01, c23)[v_rot % 2]
                    for mt in range(8):
                        w = ([t_xtb[par], t_wsetup, t_vevac.get(v_rot % 2)]
                             if mt == 0 else [])
                        tok = P.op("tensor", lambda t, ps=ps,
                                   l=bf(A_XTB[par] + mt * 512 + tt * 128, 128),
                                   r=bf(A_VWB + mt * 256 + p * 128, 128),
                                   st=(mt == 0), sp=(mt == 7):
                                   t.matmul(ps[:, 0:128], l, r, start=st, stop=sp),
                                   waits=w, inc="pe" if mt == 7 else None)
                    last_mm = tok
                    ktg = tb * 4 + tt
                    ev = None
                    for hl in range(2):
                        ev = P.op(
                            "vector", lambda v, d=bf(A_V[2 * p + hl] + ktg * 65, 64),
                            ps=ps, hl=hl, p=p:
                            v.tensor_add(d, ps[:, hl * 64:hl * 64 + 64],
                                         ff(Z_VBBC + p * 128 + hl * 64, 64)),
                            waits=[tok, t_bsetup, t_setup_v], inc="dve")
                    t_vevac[v_rot % 2] = ev
                    v_rot += 1
            t_p1mm_lastx[par] = last_mm
        t_p1_act = ("act", P.cnt["act"])
        t_p1_dve = ("dve", P.cnt["dve"])

        steps = []
        block_last_step = {}
        for j in range(16):
            for pair in range(2):
                nkt = 4 * j + 4
                for kt in range(nkt):
                    steps.append((j, pair, kt, nkt))
                block_last_step[2 * j + pair] = len(steps) - 1
        NSTEP = len(steps)

        PP_IN = tuple(56576 + i * 1024 for i in range(4))
        PP_OUT = tuple(60672 + i * 1024 for i in range(4))
        PP_SC = tuple(AB.ap().bitcast(F32)[0:128, 32384 + i * 1024:33408 + i * 1024]
                      for i in range(4))
        t_pp_in = {}
        t_pp_sc = {}
        t_pp_stt = {}
        t_pp_st = {}

        def pp_loads(g):
            if g < 32:
                kt, qt = g // 4, g % 4
                src = bass.AP(w1q, kt * 128 * 4096 + qt * 1024, [[4096, 128], [1, 1024]])
                ssrc = bass.AP(w1s, kt * 4096 + qt * 1024, [[0, 128], [1, 1024]])
            else:
                ft = g - 32
                src = bass.AP(w2q, ft * 128 * DM, [[DM, 128], [1, DM]])
                ssrc = bass.AP(w2s, ft * DM, [[0, 128], [1, 1024]])
            t_pp_in[g] = gdma("pql", bf(PP_IN[g % 4], 1024), src,
                              waits=[t_pp_stt.get(g - 4)])
            t_pp_sc[g] = cdma("lc%d" % (g % 2), PP_SC[g % 4], ssrc,
                              waits=[t_pp_stt.get(g - 4)])

        def pp_work(g):
            t = P.op("vector", lambda v, g=g:
                     v.scalar_tensor_tensor(bf(PP_OUT[g % 4], 1024), bf(PP_IN[g % 4], 1024),
                                            -128.0, PP_SC[g % 4],
                                            ALU.add, ALU.mult),
                     waits=[t_pp_in[g], t_pp_sc[g], t_pp_st.get(g - 4)], inc="dve")
            t_pp_stt[g] = t
            dst = (w1d.ap()[0:128, g * 1024:(g + 1) * 1024] if g < 32 else
                   w2d.ap()[0:128, (g - 32) * 1024:(g - 31) * 1024])
            t_pp_st[g] = cdma("pst", dst, bf(PP_OUT[g % 4], 1024), waits=[t])

        t_qld = {}
        t_qk = {}
        t_exp = {}
        t_pv = {}
        t_mask = {}
        t_epi_cp = {}
        t_den = {}
        t_ag_slot = {}
        t_dstg_free = {}

        def emit_qload(b):
            j, pair = b // 2, b % 2
            w = [t_qspill[(j, pair)], t_qspill[(15, pair)]]
            if b >= 2:
                w.append(t_qk[block_last_step[b - 2]])
            t_qld[b] = sdma("lq", bf(A_QLD + (b % 2) * 512, 512),
                            qdram.ap()[pair * 128:(pair + 1) * 128,
                                       j * 512:(j + 1) * 512], waits=w)

        for b in range(2):
            emit_qload(b)

        def emit_qk(s):
            j, pair, kt, nkt = steps[s]
            b = 2 * j + pair
            ch = s % 3
            for hl in range(2):
                w = []
                if hl == 0:
                    w = [t_exp.get(s - 3), t_qld[b]]
                    if s < 3:
                        w.append(t_p1_act)
                tok = P.op("tensor", lambda t, ps=pL[ch], hl=hl,
                           l=bf(A_KT[pair] + kt * 128, 128, rows=64, row0=hl * 64),
                           r=bf(A_QLD + (b % 2) * 512, 512, rows=64, row0=hl * 64),
                           tp=(hl * 64, 0):
                           t.matmul(ps.ap()[:, hl * 512:hl * 512 + 512], l, r,
                                    start=True, stop=True, tile_position=tp),
                           waits=w, inc="pe" if hl == 1 else None)
            t_qk[s] = tok
            odd = (s % 2 == 1)
            tok = P.op(
                "scalar", lambda sc_, d=bf(A_SC[ch], 1024), ps=pL[ch]:
                sc_.activation(d, ps.ap()[:, :], AF.Exp),
                waits=[t_qk[s], t_pv.get(s - 3)],
                inc="act" if odd else None, amt=2)
            if odd:
                t_exp[s] = tok
                t_exp[s - 1] = tok

        def emit_pv(s):
            j, pair, kt, nkt = steps[s]
            b = 2 * j + pair
            ch = s % 3
            off = (kt - 4 * j) * 128
            offc = max(off, 0)
            if off >= 0 and s not in t_mask:
                tm = None
                for hl in range(2):
                    tm = P.op(
                        "vector", lambda v, d=bf(A_SC[ch] + hl * 512 + off, 128):
                        v.tensor_mul(d, d, cmask[:, :]),
                        waits=[t_exp[s], t_masks], inc="dve" if hl == 1 else None)
                t_mask[s] = tm
            gate = t_mask.get(s, t_exp[s])
            for hl in range(2):
                c = (c01, c23)[hl]
                w = [gate]
                if kt == 0:
                    if b == 0:
                        w.append(t_p1_dve)
                    else:
                        w += [t_epi_cp.get((b - 1, hl)), t_den.get((b - 1, hl))]
                tok = P.op("tensor", lambda t, c=c,
                           l=AB[:, A_V[2 * pair + hl] + kt * 65:A_V[2 * pair + hl] + kt * 65 + 65],
                           r=bf(A_SC[ch] + hl * 512 + offc, 512 - offc),
                           o0=offc, st=(kt == 0), sp=(kt == nkt - 1):
                           t.matmul(c.ap()[0:65, o0:512], l, r, start=st, stop=sp),
                           waits=w, inc="pe" if hl == 1 else None)
            t_pv[s] = tok
            if kt == nkt - 1:
                for hl in range(2):
                    c = (c01, c23)[hl]
                    sp_ = (2 * b + hl) % 4
                    t_cp = P.op(
                        "vector", lambda v, d=bf(A_STG[sp_], 512, rows=64), c=c:
                        v.tensor_copy(d, c.ap()[0:64, 0:512]),
                        waits=[t_pv[s], t_ag_slot.get(sp_)], inc="dve")
                    t_epi_cp[(b, hl)] = t_cp
                    row0 = (pair * 2 + hl) * 64
                    t_ag_slot[sp_] = gdma(
                        "sg", ag_in.ap()[row0:row0 + 64, j * 512:(j + 1) * 512],
                        bf(A_STG[sp_], 512, rows=64), waits=[t_cp])
                    t_cpd = P.op(
                        "vector", lambda v, d=ff(Z_DSTG[hl], 512, rows=1), c=c:
                        v.tensor_copy(d, c.ap()[64:65, 0:512]),
                        waits=[t_pv[s], t_dstg_free.get(hl)], inc="dve")
                    t_den[(b, hl)] = t_cpd
                    t_dstg_free[hl] = cdma(
                        "sd", ag_den_in.ap()[2 * pair + hl:2 * pair + hl + 1,
                                             j * 512:(j + 1) * 512],
                        ff(Z_DSTG[hl], 512, rows=1), waits=[t_cpd])

        pp_l = 0
        pp_w = 0
        for s in range(NSTEP + 2):
            if s < NSTEP:
                emit_qk(s)
                if s % 17 == 0 and pp_l < 64:
                    pp_loads(pp_l)
                    pp_l += 1
                if s % 17 == 8 and pp_w < 64 and pp_l >= pp_w + 4:
                    pp_work(pp_w)
                    pp_w += 1
                j, pair, kt, nkt = steps[s]
                b = 2 * j + pair
                if kt == nkt - 1 and (b + 2) < 32 and (b + 2) not in t_qld:
                    emit_qload(b + 2)
            if s >= 2:
                emit_pv(s - 2)
        while pp_w < 64:
            pp_work(pp_w)
            pp_w += 1
        t_p2_act = ("act", P.cnt["act"])
        t_p2_pe = ("pe", P.cnt["pe"])
        t_ag_w = [("sg", P.cnt["sg"]), ("sd", P.cnt["sd"]), ("pst", P.cnt["pst"])]

        t_cc2 = P.op("gpsimd", lambda g: g.collective_compute(
            "AllGather", ALU.bypass,
            replica_groups=[[0, 1, 2, 3], [4, 5, 6, 7]],
            ins=[ag_den_in.ap().opt()], outs=[ag_den_out.ap().opt()]),
            waits=t_ag_w, inc="cc", amt=1, dma=True)
        t_cc1 = P.op("gpsimd", lambda g: g.collective_compute(
            "AllGather", ALU.bypass,
            replica_groups=[[0, 1, 2, 3], [4, 5, 6, 7]],
            ins=[ag_in.ap().opt()], outs=[ag_out.ap().opt()]),
            waits=[], inc="cc", amt=1, dma=True)

        t_w1ld = {}
        t_w2ld = {}

        def w1_data(kt, waits=()):
            t_w1ld[kt] = sdma("lw1", bf(A_W1 + kt * 4096, 4096),
                              w1d.ap()[0:128, kt * 4096:(kt + 1) * 4096],
                              waits=list(waits) + [t_pp_st[63]])

        def w2_data(c, waits=()):
            t_w2ld[c] = sdma("lw2", bf(A_W2 + c * 4096, 4096),
                             w2d.ap()[0:128, c * 4096:(c + 1) * 4096],
                             waits=list(waits) + [t_pp_st[63]])

        for kt in range(3):
            w1_data(kt, waits=[t_p2_pe])
        for c in range(2, 8):
            w2_data(c)

        t_la = {}

        def _ld_chunk(k):
            def go(g):
                rv = nc.values_load(rksb[0:1, 0:1], engines=[mybir.EngineType.SP],
                                    min_val=0, max_val=S - TPC,
                                    skip_runtime_bounds_check=True)
                return g.dma_start(out=bf(A_NRM + k * 2048, 2048),
                                   in_=ag_out[128 * k:128 * (k + 1),
                                              bass.DynSlice(rv, TPC)])
            return go

        den_view = AB.ap().bitcast(F32)[0:16, A_DEN // 2:A_DEN // 2 + 2048]

        def _ld_den(g):
            rv = nc.values_load(rksb[0:1, 0:1], engines=[mybir.EngineType.SP],
                                min_val=0, max_val=S - TPC,
                                skip_runtime_bounds_check=True)
            return g.dma_start(out=den_view,
                               in_=ag_den_out[0:16, bass.DynSlice(rv, TPC)])

        t_laden = P.op("sync", _ld_den, waits=[("cc", 1)], inc="la", amt=16, dma=True)

        for k in range(8):
            t_la[k] = P.op("sync", _ld_chunk(k), waits=[("cc", 2)] if k == 0 else [],
                           inc="la", amt=16, dma=True)

        P.op("gpsimd", lambda g: g.iota(bf(A_SELSCR, 1024, rows=16), [[1, 16], [0, 64]],
                                        base=0, channel_multiplier=-1,
                                        allow_small_or_imprecise_dtypes=True), inc="gp")
        t_iosel = ("gp", P.cnt["gp"])
        t_sel = P.op("vector", lambda v: v.tensor_single_scalar(
            bf(A_SEL, 1024, rows=16), bf(A_SELSCR, 1024, rows=16), 0.0, ALU.is_equal),
            waits=[t_iosel], inc="dve")

        cdma("lgb", ff(Z_GAM, 1024), bass.AP(gam, 0, [[0, 128], [1, 1024]]))
        t_gab = cdma("lgb", ff(Z_BET, 1024), bass.AP(bet, 0, [[0, 128], [1, 1024]]))

        FT_ORDER = list(range(8, 32)) + list(range(8))

        def _recip(v):
            with nc.allow_low_precision(reason="bf16 divisor: becomes a bf16 "
                                        "matmul operand like every other matmul "
                                        "input; numerators are bf16 already"):
                return v.reciprocal(bf(A_RHI, 2048, rows=16), den_view)

        t_recip = P.op("vector", _recip, waits=[t_laden], inc="dve")
        t_div = {}
        t_nrm_mult = {}
        for k in range(8):
            for h in range(2):
                par = (k * 2 + h) % 2
                w = [t_recip, t_sel, t_la[k], t_nrm_mult.get(par)]
                if k == 0 and h == 0:
                    w.append(t_p2_act)
                for q in range(2):
                    tok = P.op(
                        "tensor", lambda t, ps=pL[par], k=k, h=h, q=q:
                        t.matmul(ps.ap()[:, q * 512:(q + 1) * 512],
                                 bf(A_SEL + k * 128, 128, rows=16),
                                 bf(A_RHI + h * 1024 + q * 512, 512, rows=16),
                                 start=True, stop=True),
                        waits=w if q == 0 else [], inc="pe" if q == 1 else None)
                t_div[(k, h)] = tok
                t_nrm_mult[par] = P.op(
                    "vector", lambda v, d=bf(A_NRM + k * 2048 + h * 1024, 1024),
                    ps=pL[par]:
                    v.tensor_mul(d, d, ps.ap()[:, :]),
                    waits=[t_div[(k, h)], t_la[k]], inc="dve")
        t_nrm_done = ("dve", P.cnt["dve"])
        for kt in range(3, 7):
            w1_data(kt, waits=[t_nrm_done])

        t_xr = {}
        t_o_mm = {}
        t_x1pre = {}
        t_x1sp_cnt = {0: 0, 1: 0}
        t_x1n_cnt = {0: 0, 1: 0}
        t_sx_tok = {}
        t_sn_tok = {}
        t_apply = {}
        t_oproj_done = None
        for tt in range(16):
            par = tt % 2
            ps = pL[par]
            t_xr[par] = gdma("lr%d" % par, ff(Z_XRES[par], 1024),
                             x_res.ap()[tt * 128:(tt + 1) * 128, :],
                             waits=[("sx%d" % par, t_x1sp_cnt[par] * 16)])
            for ht in range(8):
                for mc in range(2):
                    w = ([t_nrm_done, t_x1pre.get(par)]
                         if (ht == 0 and mc == 0) else [])
                    P.op("tensor", lambda t, ps=ps, l=bf(A_NRM + ht * 2048 + tt * 128, 128),
                         r=bf(A_OW + ht * 1024 + mc * 512, 512), mc=mc, st=(ht == 0):
                         t.matmul(ps.ap()[:, mc * 512:mc * 512 + 512], l, r,
                                  start=st, stop=False),
                         waits=w, inc=None)
            for mc in range(2):
                tok = P.op("tensor", lambda t, ps=ps, mc=mc:
                           t.matmul(ps.ap()[:, mc * 512:mc * 512 + 512], onesb[0:1, :],
                                    ob_sb[0:1, mc * 512:mc * 512 + 512],
                                    start=False, stop=True),
                           waits=[], inc="pe" if mc == 1 else None)
            t_o_mm[par] = tok
            if tt == 15:
                t_oproj_done = tok
            t_pre = P.op("vector", lambda v, ps=ps, par=par:
                         v.tensor_add(ff(Z_UPRE[par], 1024), ps.ap()[:, :],
                                      ff(Z_XRES[par], 1024)),
                         waits=[t_o_mm[par], t_xr[par], t_apply.get(par)]
                         + ([("dp%d" % par, P.cnt.get("dp%d" % par, 0))] if DEBUG else []),
                         inc="dve")
            t_x1pre[par] = t_pre
            if DEBUG:
                cdma("dp%d" % par, dbg_pre.ap()[tt * 128:(tt + 1) * 128, :],
                     ff(Z_UPRE[par], 1024), waits=[t_pre])
            P.op("vector", lambda v, par=par: v.bn_stats(ff(Z_BN6P[par], 6),
                                                         ff(Z_UPRE[par], 512)),
                 inc="dve")
            P.op("vector", lambda v, par=par:
                 v.bn_stats(ff(Z_BN6P[par] + 6, 6), ff(Z_UPRE[par] + 512, 512)),
                 inc="dve")
            P.op("vector", lambda v, par=par: v.bn_aggr(ff(Z_MVP[par], 2),
                                                        ff(Z_BN6P[par], 12)), inc="dve")
            P.op("vector", lambda v, par=par: v.tensor_scalar_add(
                ff(Z_SDP[par], 1), ff(Z_MVP[par] + 1, 1), EPS1), inc="dve")
            P.op("vector", lambda v, par=par: v.tensor_scalar(
                ff(Z_SDP[par] + 1, 1), ff(Z_MVP[par] + 1, 1),
                1.0 + EPS1, EPS1 * EPS1, ALU.mult, ALU.add), inc="dve")
            t_v = ("dve", P.cnt["dve"])
            t_sq = P.op("scalar", lambda s, par=par: s.activation(
                ff(Z_SDP[par], 2), ff(Z_SDP[par], 2), AF.Sqrt),
                waits=[t_v], inc="act")
            P.op("vector", lambda v, par=par: v.reciprocal(ff(Z_RSTP[par], 2),
                                                           ff(Z_SDP[par], 2)),
                 waits=[t_sq], inc="dve")
            t_nm = P.op("vector", lambda v, par=par: v.tensor_scalar(
                ff(Z_NMP[par], 2), ff(Z_RSTP[par], 2),
                ff(Z_MVP[par], 1), -1.0, ALU.mult, ALU.mult), inc="dve")
            t_x1 = P.op("scalar", lambda s, par=par: s.activation(
                ff(Z_XRES[par], 1024), ff(Z_UPRE[par], 1024), AF.Identity,
                bias=ff(Z_NMP[par], 1), scale=ff(Z_RSTP[par], 1)),
                waits=[t_nm], inc="act")
            t_sx_tok[tt] = cdma("sx%d" % par, x1_sp.ap()[tt * 128:(tt + 1) * 128, :],
                                ff(Z_XRES[par], 1024), waits=[t_x1])
            t_x1sp_cnt[par] += 1
            t_xn = P.op("scalar", lambda s, par=par: s.activation(
                bf(A_X1N[par], 1024), ff(Z_UPRE[par], 1024), AF.Identity,
                bias=ff(Z_NMP[par] + 1, 1), scale=ff(Z_RSTP[par] + 1, 1)),
                waits=[t_nm, ("sn%d" % par, t_x1n_cnt[par] * 16)], inc="act")
            t_apply[par] = t_xn
            P.op("vector", lambda v, par=par:
                 v.tensor_mul(bf(A_X1N[par], 1024), bf(A_X1N[par], 1024),
                              ff(Z_GAM, 1024)), waits=[t_xn, t_gab], inc="dve")
            t_n = P.op("vector", lambda v, par=par:
                       v.tensor_add(bf(A_X1N[par], 1024), bf(A_X1N[par], 1024),
                                    ff(Z_BET, 1024)), inc="dve")
            t_sn_tok[tt] = cdma("sn%d" % par, x1n_sp.ap()[tt * 128:(tt + 1) * 128, :],
                                bf(A_X1N[par], 1024), waits=[t_n])
            t_x1n_cnt[par] += 1
        t_sn_all = [("sn0", t_x1n_cnt[0] * 16), ("sn1", t_x1n_cnt[1] * 16)]
        t_sx_all = [("sx0", t_x1sp_cnt[0] * 16), ("sx1", t_x1sp_cnt[1] * 16)]

        w1_data(7, waits=[t_oproj_done])
        for c in range(2):
            w2_data(c, waits=[t_oproj_done])
        t_zfree = {0: t_x1pre[0], 1: t_x1pre[1]}

        ctr = pL2.ap().bitcast(BF16)
        t_x1nld = {}
        t_tr_cp = None
        t_tr_cp_h = {}
        t_fc1 = {}
        t_x1ld = {}
        t_outts = {}
        t_so_cnt = {0: 0, 1: 0}
        for blk in range(8):
            for tsl in range(2):
                t_x1nld[tsl] = sdma(
                    "ln%d" % tsl, bf(A_X1NR[tsl], 1024),
                    x1n_sp.ap()[blk * 256 + tsl * 128: blk * 256 + (tsl + 1) * 128, :],
                    waits=[t_sn_tok[blk * 2 + tsl], t_tr_cp_h.get(tsl)])
            for tsl in range(2):
                for mt in range(8):
                    w = ([t_x1nld[tsl], t_masks, t_tr_cp_h.get(tsl)]
                         if mt == 0 else [])
                    t_tr = P.op("tensor", lambda t, tsl=tsl, mt=mt:
                                t.transpose(ctr[:, tsl * 1024 + mt * 128:
                                                tsl * 1024 + (mt + 1) * 128],
                                            bf(A_X1NR[tsl] + mt * 128, 128), ident[:, :]),
                                waits=w, inc="pe" if mt == 7 else None)
                dst = AB[:, A_NTB:A_NTB + 2048].rearrange(
                    "p (m c) -> p m c", m=8)[:, :, tsl * 128:(tsl + 1) * 128]
                srcp = ctr[:, tsl * 1024:tsl * 1024 + 1024].rearrange(
                    "p (m c) -> p m c", m=8)
                t_tr_cp_h[tsl] = P.op("vector", lambda v, d=dst, s=srcp:
                                      v.tensor_copy(d, s), waits=[t_tr], inc="dve")
            t_tr_cp = t_tr_cp_h[1]
            for ft in range(32):
                cbank = (c01, c23)[ft % 2]
                for kt in range(8):
                    w = [t_w1ld[kt]] if blk == 0 else []
                    if kt == 0:
                        w += [t_tr_cp_h.get(0), t_tr_cp_h.get(1), t_fc1.get(ft % 2)]
                    tok = P.op("tensor", lambda t, ps=cbank,
                               l=bf(A_W1 + kt * 4096 + ft * 128, 128),
                               r=bf(A_NTB + kt * 256, 256), st=(kt == 0):
                               t.matmul(ps.ap()[:, 0:256], l, r, start=st, stop=(kt == 7)),
                               waits=w, inc="pe" if kt == 7 else None)
                t_fc1[ft % 2] = P.op(
                    "scalar", lambda s, ps=cbank, d=bf(A_YT + ft * 256, 256), ft=ft:
                    s.activation(d, ps.ap()[:, 0:256], AF.Silu, bias=ff(Z_W1BT + ft, 1)),
                    waits=[tok, t_bsetup], inc="act")
            t_y = ("act", P.cnt["act"])
            for tsub in range(2):
                g = blk * 2 + tsub
                par = g % 2
                ps = pL[par]
                for hc in range(2):
                    for i, ft in enumerate(FT_ORDER):
                        w = []
                        if blk == 0:
                            w.append(t_w2ld[ft // 4])
                        if hc == 0 and i == 0:
                            w += [t_y, t_zfree.get(par)]
                        P.op("tensor", lambda t, ps=ps,
                             l=bf(A_YT + ft * 256 + tsub * 128, 128),
                             r=bf(A_W2 + ft * 1024 + hc * 512, 512),
                             hc=hc, st=(i == 0):
                             t.matmul(ps.ap()[:, hc * 512:hc * 512 + 512], l, r,
                                      start=st, stop=False),
                             waits=w, inc=None)
                    tok = P.op("tensor", lambda t, ps=ps, hc=hc:
                               t.matmul(ps.ap()[:, hc * 512:hc * 512 + 512], onesb[0:1, :],
                                        w2b_sb[0:1, hc * 512:hc * 512 + 512],
                                        start=False, stop=True),
                               waits=[], inc="pe" if hc == 1 else None)
                t_zmm = tok
                t_x1ld[par] = gdma("lr%d" % par, ff(Z_XRES[par], 1024),
                                   x1_sp.ap()[g * 128:(g + 1) * 128, :],
                                   waits=[t_sx_tok[g], t_zfree.get(par),
                                          ("so%d" % par, t_so_cnt[par] * 16)])
                t_uu = P.op("vector", lambda v, ps=ps, par=par:
                            v.scalar_tensor_tensor(ff(Z_UPRE[par], 1024), ps.ap()[:, :], 0.25,
                                                   ff(Z_XRES[par], 1024), ALU.mult, ALU.add),
                            waits=[t_zmm, t_x1ld[par], t_outts.get(par)]
                            + ([t_apply.get(par)] if blk == 0 else []), inc="dve")
                t_zfree[par] = t_uu
                P.op("vector", lambda v, par=par:
                     v.bn_stats(ff(Z_BN6P[par], 6), ff(Z_UPRE[par], 512)), inc="dve")
                P.op("vector", lambda v, par=par:
                     v.bn_stats(ff(Z_BN6P[par] + 6, 6), ff(Z_UPRE[par] + 512, 512)), inc="dve")
                P.op("vector", lambda v, g=g:
                     v.bn_aggr(ff(Z_MV2A + 2 * g, 2), ff(Z_BN6P[g % 2], 12)), inc="dve")
                P.op("vector", lambda v, par=par, g=g: v.tensor_scalar_add(
                    ff(Z_SDP[par], 1), ff(Z_MV2A + 2 * g + 1, 1), EPS2), inc="dve")
                t_v = ("dve", P.cnt["dve"])
                t_sq = P.op("scalar", lambda s, par=par: s.activation(
                    ff(Z_SDP[par], 1), ff(Z_SDP[par], 1), AF.Sqrt),
                    waits=[t_v], inc="act")
                t_rs = P.op("vector", lambda v, par=par: v.reciprocal(
                    ff(Z_RSTP[par], 1), ff(Z_SDP[par], 1)), waits=[t_sq], inc="dve")
                t_outts[par] = P.op("vector", lambda v, par=par, g=g:
                                    v.tensor_scalar(ff(Z_XRES[par], 1024),
                                                    ff(Z_UPRE[par], 1024),
                                                    ff(Z_MV2A + 2 * g, 1),
                                                    ff(Z_RSTP[par], 1),
                                                    ALU.subtract, ALU.mult),
                                    waits=[t_rs], inc="dve")
                cdma("so%d" % par, out.ap()[g * 128:(g + 1) * 128, :],
                     ff(Z_XRES[par], 1024), waits=[t_outts[par]])
                t_so_cnt[par] += 1
        if DEBUG:
            dmap = {
                "d_agin": (ag_in, t_ag_w),
                "d_agden": (ag_den_in, t_ag_w),
                "d_agout": (ag_out, [("cc", 1)]),
                "d_agdout": (ag_den_out, [("cc", 2)]),
                "d_w1d": (w1d, [("pst", 64 * 16)]),
                "d_w2d": (w2d, [("pst", 64 * 16)]),
                "d_x1": (x1_sp, t_sx_all),
                "d_x1n": (x1n_sp, t_sn_all),
            }
            for nm, (src_, w) in dmap.items():
                dt_ = nc.dram_tensor(nm, list(src_.shape), src_.dtype, kind="ExternalOutput")
                P.op("gpsimd", lambda g, d=dt_, s=src_: g.dma_start(out=d.ap(), in_=s.ap()),
                     waits=w, inc="lg", amt=16, dma=True)
            P.op("gpsimd", lambda g: g.nop(), waits=[("lg", P.cnt["lg"])])

        P.op("sync", lambda s: s.nop(),
             waits=[("so0", t_so_cnt[0] * 16), ("so1", t_so_cnt[1] * 16)])

        @block.sync
        def _(engobj):
            P.emit("sync", engobj, sems)

        @block.gpsimd
        def _(engobj):
            P.emit("gpsimd", engobj, sems)

        @block.tensor
        def _(engobj):
            P.emit("tensor", engobj, sems)

        @block.vector
        def _(engobj):
            P.emit("vector", engobj, sems)

        @block.scalar
        def _(engobj):
            P.emit("scalar", engobj, sems)

    return nc


def kernel(**inputs):
    x = np.asarray(inputs["L0_attn_x"])
    Qw = np.asarray(inputs["L0_attn_Q_w"])
    Qb = np.asarray(inputs["L0_attn_Q_b"])
    Kw = np.asarray(inputs["L0_attn_K_w"])
    Kb = np.asarray(inputs["L0_attn_K_b"])
    Vw = np.asarray(inputs["L0_attn_V_w"])
    Vb = np.asarray(inputs["L0_attn_V_b"])
    Ow = np.asarray(inputs["L0_attn_O_w"])
    Ob = np.asarray(inputs["L0_attn_O_b"])
    gam = np.asarray(inputs["L0_ffn_ln_gamma"])
    bet = np.asarray(inputs["L0_ffn_ln_beta"])
    w1q = np.ascontiguousarray(inputs["L0_ffn_fc1_q"])
    w1s = np.asarray(inputs["L0_ffn_fc1_scales"])
    w1b = np.asarray(inputs["L0_ffn_fc1_bias"])
    w2q = np.ascontiguousarray(inputs["L0_ffn_fc2_q"])
    w2s = np.asarray(inputs["L0_ffn_fc2_scales"])
    w2b = np.asarray(inputs["L0_ffn_fc2_bias"])

    nc = build_nc()
    owf = np.ascontiguousarray(Ow.reshape(1024, 1024))
    w1bT = np.ascontiguousarray(w1b.reshape(32, 128).T)
    in_maps = []
    for c in range(NCORE):
        b, r = c // 4, c % 4
        hs = slice(4 * r, 4 * r + 4)
        in_maps.append({
            "xT": np.ascontiguousarray(x[b].T),
            "x_res": np.ascontiguousarray(x[b, r * TPC:(r + 1) * TPC]),
            "qw": np.ascontiguousarray(Qw[hs].transpose(1, 0, 2).reshape(DM, 256)),
            "kw": np.ascontiguousarray(Kw[hs].transpose(1, 0, 2).reshape(DM, 256)),
            "vw": np.ascontiguousarray(Vw[hs].transpose(1, 0, 2).reshape(DM, 256)),
            "qbT": np.ascontiguousarray(Qb[hs].reshape(2, 128).T),
            "kbT": np.ascontiguousarray(Kb[hs].reshape(2, 128).T),
            "vb": np.ascontiguousarray(Vb[hs].reshape(2, 128)),
            "ow": owf, "ob": Ob, "gam": gam, "bet": bet,
            "w1q": w1q, "w1s": w1s, "w1bT": w1bT,
            "w2q": w2q, "w2s": w2s, "w2b": w2b,
            "rk": (r * TPC + 128 * np.arange(16, dtype=np.int32)).reshape(1, 16),
        })
    res = run_bass_kernel_spmd(nc, in_maps, core_ids=list(range(NCORE)))
    globals()["_last_res"] = res
    outarr = np.empty((2, S, DM), dtype=np.float32)
    for c in range(NCORE):
        b, r = c // 4, c % 4
        outarr[b, r * TPC:(r + 1) * TPC] = res.results[c]["out"]
    return outarr


# version 21
# speedup vs baseline: 1.1514x; 1.0164x over previous
import numpy as np
import concourse.bass as bass
import concourse.mybir as mybir
from concourse.bass_utils import run_bass_kernel_spmd

F32 = mybir.dt.float32
BF16 = mybir.dt.bfloat16
I32 = mybir.dt.int32
AF = mybir.ActivationFunctionType
ALU = mybir.AluOpType

S = 8192
DM = 1024
NCORE = 8
TPC = 2048
EPS1 = 1e-5
EPS2 = EPS1 / 4.0
DEBUG = False

A_QSTG = 0
A_QLD = 0
A_KT = (1024, 9216)
A_V = tuple(17408 + h * 4160 for h in range(4))
A_XTB = (34048, 38144)
A_SC = (34048, 35072, 36096)
A_DEN = 34048
A_STG = (38144, 38656, 39168, 39680)
A_SELSCR = 38144
A_RHI = 39168
A_SEL = 41216
A_QWB, A_KWB, A_VWB = 42240, 44288, 46336
A_OW = 48384
A_NRM = 0
A_X1N = (16384, 17408)
A_W1 = 18432
A_W2 = 51200
A_YT = 0
A_NTB = 8192
A_X1NR = (10240, 11264)
ARENA_BF = 83968

Z_SCALE = (0, 1024)
Z_XRES = (2048, 3072)
Z_UPRE = (4096, 5120)
Z_GAM = 6144
Z_BET = 7168
Z_VBBC = 8192
Z_W1BT = 8448
Z_MV2A = 8480
Z_VEPS16 = 8512
Z_RSTD16 = 8528
Z_BN6P = (8544, 8556)
Z_MVP = (8568, 8570)
Z_SDP = (8572, 8574)
Z_RSTP = (8576, 8578)
Z_NMP = (8580, 8582)
Z_QBS = 8584
Z_KBC = 8586
Z_DSTG = (8592, 9104)
ARENA_F32 = 9616

SEM_NAMES = (
    "pe", "act", "dve", "gp", "cc",
    "ls",
    "lg",
    "lq",
    "sq",
    "lx0", "lx1",
    "lw1", "lw2",
    "pql", "pst",
    "lgb",
    "lc0", "lc1",
    "la",
    "lr0", "lr1",
    "lp0", "lp1",
    "ln0", "ln1",
    "sg",
    "sd",
    "sx0", "sx1",
    "sn0", "sn1",
    "su0", "su1",
    "so0", "so1",
    "dp0", "dp1",
)


class Plan:

    DRAIN_ENGINES = ("vector", "scalar", "gpsimd")

    def __init__(self):
        self.ops = []
        self.cnt = {}

    def op(self, eng, build, waits=(), inc=None, amt=1, dma=False):
        tok = None
        if inc is not None:
            self.cnt[inc] = self.cnt.get(inc, 0) + amt
            tok = (inc, self.cnt[inc])
        ws = tuple((s, v) for w in waits if w is not None for (s, v) in (w,))
        self.ops.append((eng, build, ws, inc, amt, dma))
        return tok

    def emit(self, section, engobj, sems):
        hw = {}
        for eng, build, waits, inc, amt, dma in self.ops:
            if eng != section:
                continue
            for s, v in waits:
                if v > hw.get(s, 0):
                    engobj.wait_ge(sems[s], v)
                    hw[s] = v
            ins = build(engobj)
            if inc is not None:
                if not dma and eng in self.DRAIN_ENGINES:
                    engobj.drain().then_inc(sems[inc], amt)
                else:
                    ins.then_inc(sems[inc], amt)


def build_nc():
    nc = bass.Bass("TRN2", target_bir_lowering=False)

    xT = nc.dram_tensor("xT", [DM, S], F32, kind="ExternalInput")
    x_res = nc.dram_tensor("x_res", [TPC, DM], F32, kind="ExternalInput")
    qw = nc.dram_tensor("qw", [DM, 256], F32, kind="ExternalInput")
    kw = nc.dram_tensor("kw", [DM, 256], F32, kind="ExternalInput")
    vw = nc.dram_tensor("vw", [DM, 256], F32, kind="ExternalInput")
    qbT = nc.dram_tensor("qbT", [128, 2], F32, kind="ExternalInput")
    kbT = nc.dram_tensor("kbT", [128, 2], F32, kind="ExternalInput")
    vb = nc.dram_tensor("vb", [2, 128], F32, kind="ExternalInput")
    ow = nc.dram_tensor("ow", [DM, DM], F32, kind="ExternalInput")
    ob = nc.dram_tensor("ob", [DM], F32, kind="ExternalInput")
    gam = nc.dram_tensor("gam", [DM], F32, kind="ExternalInput")
    bet = nc.dram_tensor("bet", [DM], F32, kind="ExternalInput")
    w1q = nc.dram_tensor("w1q", [DM, 4096], I32, kind="ExternalInput")
    w1s = nc.dram_tensor("w1s", [8, 4096], F32, kind="ExternalInput")
    w1bT = nc.dram_tensor("w1bT", [128, 32], F32, kind="ExternalInput")
    w2q = nc.dram_tensor("w2q", [4096, DM], I32, kind="ExternalInput")
    w2s = nc.dram_tensor("w2s", [32, DM], F32, kind="ExternalInput")
    w2b = nc.dram_tensor("w2b", [DM], F32, kind="ExternalInput")
    rk = nc.dram_tensor("rk", [1, 16], I32, kind="ExternalInput")
    out = nc.dram_tensor("out", [TPC, DM], F32, kind="ExternalOutput")

    qdram = nc.dram_tensor("qdram", [256, S], BF16, kind="Internal")
    w1d = nc.dram_tensor("w1d", [128, 32768], BF16, kind="Internal")
    w2d = nc.dram_tensor("w2d", [128, 32768], BF16, kind="Internal")
    ag_in = nc.dram_tensor("ag_in", [256, S], BF16, kind="Internal")
    ag_den_in = nc.dram_tensor("ag_den_in", [4, S], F32, kind="Internal")
    ag_out = nc.dram_tensor("ag_out", [DM, S], BF16, kind="Internal", addr_space="Local")
    ag_den_out = nc.dram_tensor("ag_den_out", [16, S], F32, kind="Internal",
                                addr_space="Local")
    x1_sp = nc.dram_tensor("x1_sp", [TPC, DM], F32, kind="Internal")
    x1n_sp = nc.dram_tensor("x1n_sp", [TPC, DM], BF16, kind="Internal")
    dbg_pre = (nc.dram_tensor("d_pre", [TPC, DM], F32, kind="ExternalOutput")
               if DEBUG else None)

    from contextlib import ExitStack
    with ExitStack() as ctx:
        e = ctx.enter_context
        AB = e(nc.sbuf_tensor("AB", [128, ARENA_BF], BF16))
        AZ = e(nc.sbuf_tensor("AZ", [128, ARENA_F32], F32))
        cmask = e(nc.sbuf_tensor("cmask", [128, 128], BF16))
        ident = e(nc.sbuf_tensor("ident", [128, 128], BF16))
        onesb = e(nc.sbuf_tensor("onesb", [1, 128], BF16))
        ob_sb = e(nc.sbuf_tensor("ob_sb", [1, 1024], BF16))
        w2b_sb = e(nc.sbuf_tensor("w2b_sb", [1, 1024], BF16))
        iotf = e(nc.sbuf_tensor("iotf", [128, 128], F32))
        rksb = e(nc.sbuf_tensor("rksb", [1, 16], I32))
        pL0 = e(nc.psum_tensor("pL0", [128, 1024], F32))
        pL1 = e(nc.psum_tensor("pL1", [128, 1024], F32))
        pL2 = e(nc.psum_tensor("pL2", [128, 1024], F32))
        c01 = e(nc.psum_tensor("c01", [128, 512], F32))
        c23 = e(nc.psum_tensor("c23", [128, 512], F32))
        pL = (pL0, pL1, pL2)
        sems = {}
        for s in SEM_NAMES:
            sems[s] = e(nc.semaphore(s))
        block = e(nc.Block())

        P = Plan()

        def bf(off, width, rows=128, row0=0):
            return AB[row0:row0 + rows, off:off + width]

        def ff(off, width, rows=128, row0=0):
            return AZ[row0:row0 + rows, off:off + width]

        def sdma(sem, dst, src, waits=()):
            return P.op("sync", lambda g, d=dst, s=src: g.dma_start(out=d, in_=s),
                        waits=waits, inc=sem, amt=16, dma=True)

        def gdma(sem, dst, src, waits=()):
            return P.op("gpsimd", lambda g, d=dst, s=src: g.dma_start(out=d, in_=s),
                        waits=waits, inc=sem, amt=16, dma=True)

        def cdma(sem, dst, src, waits=()):
            return P.op("scalar", lambda g, d=dst, s=src: g.dma_start(out=d, in_=s),
                        waits=waits, inc=sem, amt=16, dma=True)

        P.op("gpsimd", lambda g: g.iota(iotf[:, :], [[1, 128]], base=0,
                                        channel_multiplier=-1,
                                        allow_small_or_imprecise_dtypes=True), inc="gp")
        t_iota = ("gp", P.cnt["gp"])
        P.op("vector", lambda v: v.tensor_single_scalar(cmask[:, :], iotf[:, :], 0.0, ALU.is_ge),
             waits=[t_iota], inc="dve")
        P.op("vector", lambda v: v.tensor_single_scalar(ident[:, :], iotf[:, :], 0.0, ALU.is_equal),
             waits=[t_iota], inc="dve")
        t_masks = ("dve", P.cnt["dve"])
        P.op("vector", lambda v: v.memset(onesb[:, :], 1.0), inc="dve")
        for h in range(4):
            ap = AB[:, A_V[h]:A_V[h] + 4160].rearrange("p (k c) -> p k c", c=65)[:, :, 64:65]
            P.op("vector", lambda v, ap=ap: v.memset(ap, 1.0), inc="dve")
        t_setup_v = ("dve", P.cnt["dve"])

        def r_tp(t):
            return t.ap().rearrange("(t p) c -> p t c", p=128)

        def bf3(off, width, inner):
            return bf(off, width).rearrange("p (t c) -> p t c", c=inner)

        gdma("lg", bf3(A_QWB, 2048, 256), r_tp(qw))
        gdma("lg", bf3(A_KWB, 2048, 256), r_tp(kw))
        gdma("lg", bf3(A_VWB, 2048, 256), r_tp(vw))
        gdma("lg", bf3(A_OW, 8192, 1024), r_tp(ow))
        gdma("lg", ob_sb[:, :], ob.ap().rearrange("(a c) -> a c", a=1))
        gdma("lg", w2b_sb[:, :], w2b.ap().rearrange("(a c) -> a c", a=1))
        t_wsetup = ("lg", P.cnt["lg"])
        sdma("ls", ff(Z_QBS, 2), qbT[:, :])
        sdma("ls", ff(Z_KBC, 2), kbT[:, :])
        for p in range(2):
            sdma("ls", ff(Z_VBBC + p * 128, 128), bass.AP(vb, p * 128, [[0, 128], [1, 128]]))
        sdma("ls", ff(Z_W1BT, 32), w1bT[:, :])
        sdma("ls", rksb[:, :], rk[:, :])
        t_bsetup = ("ls", P.cnt["ls"])
        P.op("vector", lambda v: v.tensor_scalar_mul(ff(Z_QBS, 2), ff(Z_QBS, 2), 0.125),
             waits=[t_bsetup], inc="dve")
        t_qbs = ("dve", P.cnt["dve"])

        NTB = 16
        qk_rot = 0
        v_rot = 0
        t_qkevac = {}
        t_vevac = {}
        t_xtb = {}
        t_p1mm_lastx = {}
        t_qspill = {}
        t_qstg_free = {}
        qk_slots = (pL0.ap()[:, 0:512], pL0.ap()[:, 512:1024],
                    pL1.ap()[:, 0:512], pL1.ap()[:, 512:1024])
        for tb in range(NTB):
            par = tb % 2
            src = bass.AP(xT, tb * 512, [[S, 128], [128 * S, 8], [1, 512]])
            t_xtb[par] = gdma("lx%d" % par, bf(A_XTB[par], 4096), src,
                              waits=[t_p1mm_lastx.get(par)])
            last_mm = None
            for which in range(2):
                woff = (A_QWB, A_KWB)[which]
                bz = (Z_QBS, Z_KBC)[which]
                sc = (0.125, 1.0)[which]
                for p in range(2):
                    slot = qk_rot % 4
                    ps = qk_slots[slot]
                    for mt in range(8):
                        w = ([t_xtb[par], t_wsetup, t_qkevac.get(slot)]
                             if mt == 0 else [])
                        tok = P.op("tensor", lambda t, ps=ps,
                                   l=bf(woff + mt * 256 + p * 128, 128),
                                   r=bf(A_XTB[par] + mt * 512, 512),
                                   st=(mt == 0), sp=(mt == 7):
                                   t.matmul(ps, l, r, start=st, stop=sp),
                                   waits=w, inc="pe" if mt == 7 else None)
                    last_mm = tok
                    if which == 0:
                        stg = (tb * 2 + p) % 2
                        d = bf(A_QSTG + stg * 512, 512)
                        ev = P.op(
                            "scalar", lambda s, d=d, ps=ps, pp=p:
                            s.activation(d, ps, AF.Identity, bias=ff(Z_QBS + pp, 1),
                                         scale=0.125),
                            waits=[tok, t_qbs, t_qstg_free.get(stg)], inc="act")
                        t_qspill[(tb, p)] = cdma(
                            "sq", qdram.ap()[p * 128:(p + 1) * 128, tb * 512:(tb + 1) * 512],
                            d, waits=[ev])
                        t_qstg_free[stg] = t_qspill[(tb, p)]
                        t_qkevac[slot] = ev
                    else:
                        t_qkevac[slot] = P.op(
                            "scalar", lambda s, d=bf(A_KT[p] + tb * 512, 512), ps=ps, pp=p:
                            s.activation(d, ps, AF.Identity, bias=ff(Z_KBC + pp, 1),
                                         scale=1.0),
                            waits=[tok, t_qbs], inc="act")
                    qk_rot += 1
            for p in range(2):
                for tt in range(4):
                    ps = (c01, c23)[v_rot % 2]
                    for mt in range(8):
                        w = ([t_xtb[par], t_wsetup, t_vevac.get(v_rot % 2)]
                             if mt == 0 else [])
                        tok = P.op("tensor", lambda t, ps=ps,
                                   l=bf(A_XTB[par] + mt * 512 + tt * 128, 128),
                                   r=bf(A_VWB + mt * 256 + p * 128, 128),
                                   st=(mt == 0), sp=(mt == 7):
                                   t.matmul(ps[:, 0:128], l, r, start=st, stop=sp),
                                   waits=w, inc="pe" if mt == 7 else None)
                    last_mm = tok
                    ktg = tb * 4 + tt
                    ev = None
                    for hl in range(2):
                        ev = P.op(
                            "vector", lambda v, d=bf(A_V[2 * p + hl] + ktg * 65, 64),
                            ps=ps, hl=hl, p=p:
                            v.tensor_add(d, ps[:, hl * 64:hl * 64 + 64],
                                         ff(Z_VBBC + p * 128 + hl * 64, 64)),
                            waits=[tok, t_bsetup, t_setup_v], inc="dve")
                    t_vevac[v_rot % 2] = ev
                    v_rot += 1
            t_p1mm_lastx[par] = last_mm
        t_p1_act = ("act", P.cnt["act"])
        t_p1_dve = ("dve", P.cnt["dve"])

        steps = []
        block_last_step = {}
        for j in range(16):
            for pair in range(2):
                nkt = 4 * j + 4
                for kt in range(nkt):
                    steps.append((j, pair, kt, nkt))
                block_last_step[2 * j + pair] = len(steps) - 1
        NSTEP = len(steps)

        PP_IN = tuple(56576 + i * 1024 for i in range(4))
        PP_OUT = tuple(60672 + i * 1024 for i in range(4))
        PP_SC = tuple(AB.ap().bitcast(F32)[0:128, 32384 + i * 1024:33408 + i * 1024]
                      for i in range(4))
        t_pp_in = {}
        t_pp_sc = {}
        t_pp_stt = {}
        t_pp_st = {}

        def pp_loads(g):
            if g < 32:
                kt, qt = g // 4, g % 4
                src = bass.AP(w1q, kt * 128 * 4096 + qt * 1024, [[4096, 128], [1, 1024]])
                ssrc = bass.AP(w1s, kt * 4096 + qt * 1024, [[0, 128], [1, 1024]])
            else:
                ft = g - 32
                src = bass.AP(w2q, ft * 128 * DM, [[DM, 128], [1, DM]])
                ssrc = bass.AP(w2s, ft * DM, [[0, 128], [1, 1024]])
            t_pp_in[g] = gdma("pql", bf(PP_IN[g % 4], 1024), src,
                              waits=[t_pp_stt.get(g - 4)])
            t_pp_sc[g] = cdma("lc%d" % (g % 2), PP_SC[g % 4], ssrc,
                              waits=[t_pp_stt.get(g - 4)])

        def pp_work(g):
            t = P.op("vector", lambda v, g=g:
                     v.scalar_tensor_tensor(bf(PP_OUT[g % 4], 1024), bf(PP_IN[g % 4], 1024),
                                            -128.0, PP_SC[g % 4],
                                            ALU.add, ALU.mult),
                     waits=[t_pp_in[g], t_pp_sc[g], t_pp_st.get(g - 4)], inc="dve")
            t_pp_stt[g] = t
            dst = (w1d.ap()[0:128, g * 1024:(g + 1) * 1024] if g < 32 else
                   w2d.ap()[0:128, (g - 32) * 1024:(g - 31) * 1024])
            t_pp_st[g] = cdma("pst", dst, bf(PP_OUT[g % 4], 1024), waits=[t])

        t_qld = {}
        t_qk = {}
        t_exp = {}
        t_pv = {}
        t_mask = {}
        t_epi_cp = {}
        t_den = {}
        t_ag_slot = {}
        t_dstg_free = {}

        def emit_qload(b):
            j, pair = b // 2, b % 2
            w = [t_qspill[(j, pair)], t_qspill[(15, pair)]]
            if b >= 2:
                w.append(t_qk[block_last_step[b - 2]])
            t_qld[b] = sdma("lq", bf(A_QLD + (b % 2) * 512, 512),
                            qdram.ap()[pair * 128:(pair + 1) * 128,
                                       j * 512:(j + 1) * 512], waits=w)

        for b in range(2):
            emit_qload(b)

        def emit_qk(s):
            j, pair, kt, nkt = steps[s]
            b = 2 * j + pair
            ch = s % 3
            for hl in range(2):
                w = []
                if hl == 0:
                    w = [t_exp.get(s - 3), t_qld[b]]
                    if s < 3:
                        w.append(t_p1_act)
                tok = P.op("tensor", lambda t, ps=pL[ch], hl=hl,
                           l=bf(A_KT[pair] + kt * 128, 128, rows=64, row0=hl * 64),
                           r=bf(A_QLD + (b % 2) * 512, 512, rows=64, row0=hl * 64),
                           tp=(hl * 64, 0):
                           t.matmul(ps.ap()[:, hl * 512:hl * 512 + 512], l, r,
                                    start=True, stop=True, tile_position=tp),
                           waits=w, inc="pe" if hl == 1 else None)
            t_qk[s] = tok
            odd = (s % 2 == 1)
            tok = P.op(
                "scalar", lambda sc_, d=bf(A_SC[ch], 1024), ps=pL[ch]:
                sc_.activation(d, ps.ap()[:, :], AF.Exp),
                waits=[t_qk[s], t_pv.get(s - 3)],
                inc="act" if odd else None, amt=2)
            if odd:
                t_exp[s] = tok
                t_exp[s - 1] = tok

        def emit_pv(s):
            j, pair, kt, nkt = steps[s]
            b = 2 * j + pair
            ch = s % 3
            off = (kt - 4 * j) * 128
            offc = max(off, 0)
            if off >= 0 and s not in t_mask:
                tm = None
                for hl in range(2):
                    tm = P.op(
                        "vector", lambda v, d=bf(A_SC[ch] + hl * 512 + off, 128):
                        v.tensor_mul(d, d, cmask[:, :]),
                        waits=[t_exp[s], t_masks], inc="dve" if hl == 1 else None)
                t_mask[s] = tm
            gate = t_mask.get(s, t_exp[s])
            for hl in range(2):
                c = (c01, c23)[hl]
                w = [gate]
                if kt == 0:
                    if b == 0:
                        w.append(t_p1_dve)
                    else:
                        w += [t_epi_cp.get((b - 1, hl)), t_den.get((b - 1, hl))]
                tok = P.op("tensor", lambda t, c=c,
                           l=AB[:, A_V[2 * pair + hl] + kt * 65:A_V[2 * pair + hl] + kt * 65 + 65],
                           r=bf(A_SC[ch] + hl * 512 + offc, 512 - offc),
                           o0=offc, st=(kt == 0), sp=(kt == nkt - 1):
                           t.matmul(c.ap()[0:65, o0:512], l, r, start=st, stop=sp),
                           waits=w, inc="pe" if hl == 1 else None)
            t_pv[s] = tok
            if kt == nkt - 1:
                for hl in range(2):
                    c = (c01, c23)[hl]
                    sp_ = (2 * b + hl) % 4
                    t_cp = P.op(
                        "vector", lambda v, d=bf(A_STG[sp_], 512, rows=64), c=c:
                        v.tensor_copy(d, c.ap()[0:64, 0:512]),
                        waits=[t_pv[s], t_ag_slot.get(sp_)], inc="dve")
                    t_epi_cp[(b, hl)] = t_cp
                    row0 = (pair * 2 + hl) * 64
                    t_ag_slot[sp_] = gdma(
                        "sg", ag_in.ap()[row0:row0 + 64, j * 512:(j + 1) * 512],
                        bf(A_STG[sp_], 512, rows=64), waits=[t_cp])
                    t_cpd = P.op(
                        "vector", lambda v, d=ff(Z_DSTG[hl], 512, rows=1), c=c:
                        v.tensor_copy(d, c.ap()[64:65, 0:512]),
                        waits=[t_pv[s], t_dstg_free.get(hl)], inc="dve")
                    t_den[(b, hl)] = t_cpd
                    t_dstg_free[hl] = cdma(
                        "sd", ag_den_in.ap()[2 * pair + hl:2 * pair + hl + 1,
                                             j * 512:(j + 1) * 512],
                        ff(Z_DSTG[hl], 512, rows=1), waits=[t_cpd])

        pp_l = 0
        pp_w = 0
        for s in range(NSTEP + 2):
            if s < NSTEP:
                emit_qk(s)
                if s % 16 == 0 and pp_l < 64:
                    pp_loads(pp_l)
                    pp_l += 1
                if s % 16 == 8 and pp_w < 64 and pp_l >= pp_w + 4:
                    pp_work(pp_w)
                    pp_w += 1
                j, pair, kt, nkt = steps[s]
                b = 2 * j + pair
                if kt == nkt - 1 and (b + 2) < 32 and (b + 2) not in t_qld:
                    emit_qload(b + 2)
            if s >= 2:
                emit_pv(s - 2)
        while pp_w < 64:
            pp_work(pp_w)
            pp_w += 1
        t_p2_act = ("act", P.cnt["act"])
        t_p2_pe = ("pe", P.cnt["pe"])
        t_ag_w = [("sg", P.cnt["sg"]), ("sd", P.cnt["sd"]), ("pst", P.cnt["pst"])]

        t_cc2 = P.op("gpsimd", lambda g: g.collective_compute(
            "AllGather", ALU.bypass,
            replica_groups=[[0, 1, 2, 3], [4, 5, 6, 7]],
            ins=[ag_den_in.ap().opt()], outs=[ag_den_out.ap().opt()]),
            waits=t_ag_w, inc="cc", amt=1, dma=True)
        t_cc1 = P.op("gpsimd", lambda g: g.collective_compute(
            "AllGather", ALU.bypass,
            replica_groups=[[0, 1, 2, 3], [4, 5, 6, 7]],
            ins=[ag_in.ap().opt()], outs=[ag_out.ap().opt()]),
            waits=[], inc="cc", amt=1, dma=True)

        t_w1ld = {}
        t_w2ld = {}

        def w1_data(kt, waits=()):
            t_w1ld[kt] = sdma("lw1", bf(A_W1 + kt * 4096, 4096),
                              w1d.ap()[0:128, kt * 4096:(kt + 1) * 4096],
                              waits=list(waits) + [t_pp_st[63]])

        def w2_data(c, waits=()):
            t_w2ld[c] = sdma("lw2", bf(A_W2 + c * 4096, 4096),
                             w2d.ap()[0:128, c * 4096:(c + 1) * 4096],
                             waits=list(waits) + [t_pp_st[63]])

        for kt in range(3):
            w1_data(kt, waits=[t_p2_pe])
        for c in range(2, 8):
            w2_data(c)

        t_la = {}

        def _ld_chunk(k):
            def go(g):
                rv = nc.values_load(rksb[0:1, 0:1], engines=[mybir.EngineType.SP],
                                    min_val=0, max_val=S - TPC,
                                    skip_runtime_bounds_check=True)
                return g.dma_start(out=bf(A_NRM + k * 2048, 2048),
                                   in_=ag_out[128 * k:128 * (k + 1),
                                              bass.DynSlice(rv, TPC)])
            return go

        den_view = AB.ap().bitcast(F32)[0:16, A_DEN // 2:A_DEN // 2 + 2048]

        def _ld_den(g):
            rv = nc.values_load(rksb[0:1, 0:1], engines=[mybir.EngineType.SP],
                                min_val=0, max_val=S - TPC,
                                skip_runtime_bounds_check=True)
            return g.dma_start(out=den_view,
                               in_=ag_den_out[0:16, bass.DynSlice(rv, TPC)])

        t_laden = P.op("sync", _ld_den, waits=[("cc", 1)], inc="la", amt=16, dma=True)

        for k in range(8):
            t_la[k] = P.op("sync", _ld_chunk(k), waits=[("cc", 2)] if k == 0 else [],
                           inc="la", amt=16, dma=True)

        P.op("gpsimd", lambda g: g.iota(bf(A_SELSCR, 1024, rows=16), [[1, 16], [0, 64]],
                                        base=0, channel_multiplier=-1,
                                        allow_small_or_imprecise_dtypes=True), inc="gp")
        t_iosel = ("gp", P.cnt["gp"])
        t_sel = P.op("vector", lambda v: v.tensor_single_scalar(
            bf(A_SEL, 1024, rows=16), bf(A_SELSCR, 1024, rows=16), 0.0, ALU.is_equal),
            waits=[t_iosel], inc="dve")

        cdma("lgb", ff(Z_GAM, 1024), bass.AP(gam, 0, [[0, 128], [1, 1024]]))
        t_gab = cdma("lgb", ff(Z_BET, 1024), bass.AP(bet, 0, [[0, 128], [1, 1024]]))

        FT_ORDER = list(range(8, 32)) + list(range(8))

        def _recip(v):
            with nc.allow_low_precision(reason="bf16 divisor: becomes a bf16 "
                                        "matmul operand like every other matmul "
                                        "input; numerators are bf16 already"):
                return v.reciprocal(bf(A_RHI, 2048, rows=16), den_view)

        t_recip = P.op("vector", _recip, waits=[t_laden], inc="dve")
        t_div = {}
        t_nrm_mult = {}
        for k in range(8):
            for h in range(2):
                par = (k * 2 + h) % 2
                w = [t_recip, t_sel, t_la[k], t_nrm_mult.get(par)]
                if k == 0 and h == 0:
                    w.append(t_p2_act)
                for q in range(2):
                    tok = P.op(
                        "tensor", lambda t, ps=pL[par], k=k, h=h, q=q:
                        t.matmul(ps.ap()[:, q * 512:(q + 1) * 512],
                                 bf(A_SEL + k * 128, 128, rows=16),
                                 bf(A_RHI + h * 1024 + q * 512, 512, rows=16),
                                 start=True, stop=True),
                        waits=w if q == 0 else [], inc="pe" if q == 1 else None)
                t_div[(k, h)] = tok
                t_nrm_mult[par] = P.op(
                    "vector", lambda v, d=bf(A_NRM + k * 2048 + h * 1024, 1024),
                    ps=pL[par]:
                    v.tensor_mul(d, d, ps.ap()[:, :]),
                    waits=[t_div[(k, h)], t_la[k]], inc="dve")
        t_nrm_done = ("dve", P.cnt["dve"])
        for kt in range(3, 7):
            w1_data(kt, waits=[t_nrm_done])

        t_xr = {}
        t_o_mm = {}
        t_x1pre = {}
        t_x1sp_cnt = {0: 0, 1: 0}
        t_x1n_cnt = {0: 0, 1: 0}
        t_sx_tok = {}
        t_sn_tok = {}
        t_apply = {}
        t_oproj_done = None
        for tt in range(16):
            par = tt % 2
            ps = pL[par]
            t_xr[par] = gdma("lr%d" % par, ff(Z_XRES[par], 1024),
                             x_res.ap()[tt * 128:(tt + 1) * 128, :],
                             waits=[("sx%d" % par, t_x1sp_cnt[par] * 16)])
            for ht in range(8):
                for mc in range(2):
                    w = ([t_nrm_done, t_x1pre.get(par)]
                         if (ht == 0 and mc == 0) else [])
                    P.op("tensor", lambda t, ps=ps, l=bf(A_NRM + ht * 2048 + tt * 128, 128),
                         r=bf(A_OW + ht * 1024 + mc * 512, 512), mc=mc, st=(ht == 0):
                         t.matmul(ps.ap()[:, mc * 512:mc * 512 + 512], l, r,
                                  start=st, stop=False),
                         waits=w, inc=None)
            for mc in range(2):
                tok = P.op("tensor", lambda t, ps=ps, mc=mc:
                           t.matmul(ps.ap()[:, mc * 512:mc * 512 + 512], onesb[0:1, :],
                                    ob_sb[0:1, mc * 512:mc * 512 + 512],
                                    start=False, stop=True),
                           waits=[], inc="pe" if mc == 1 else None)
            t_o_mm[par] = tok
            if tt == 15:
                t_oproj_done = tok
            t_pre = P.op("vector", lambda v, ps=ps, par=par:
                         v.tensor_add(ff(Z_UPRE[par], 1024), ps.ap()[:, :],
                                      ff(Z_XRES[par], 1024)),
                         waits=[t_o_mm[par], t_xr[par], t_apply.get(par)]
                         + ([("dp%d" % par, P.cnt.get("dp%d" % par, 0))] if DEBUG else []),
                         inc="dve")
            t_x1pre[par] = t_pre
            if DEBUG:
                cdma("dp%d" % par, dbg_pre.ap()[tt * 128:(tt + 1) * 128, :],
                     ff(Z_UPRE[par], 1024), waits=[t_pre])
            P.op("vector", lambda v, par=par: v.bn_stats(ff(Z_BN6P[par], 6),
                                                         ff(Z_UPRE[par], 512)),
                 inc="dve")
            P.op("vector", lambda v, par=par:
                 v.bn_stats(ff(Z_BN6P[par] + 6, 6), ff(Z_UPRE[par] + 512, 512)),
                 inc="dve")
            P.op("vector", lambda v, par=par: v.bn_aggr(ff(Z_MVP[par], 2),
                                                        ff(Z_BN6P[par], 12)), inc="dve")
            P.op("vector", lambda v, par=par: v.tensor_scalar_add(
                ff(Z_SDP[par], 1), ff(Z_MVP[par] + 1, 1), EPS1), inc="dve")
            P.op("vector", lambda v, par=par: v.tensor_scalar(
                ff(Z_SDP[par] + 1, 1), ff(Z_MVP[par] + 1, 1),
                1.0 + EPS1, EPS1 * EPS1, ALU.mult, ALU.add), inc="dve")
            t_v = ("dve", P.cnt["dve"])
            t_sq = P.op("scalar", lambda s, par=par: s.activation(
                ff(Z_SDP[par], 2), ff(Z_SDP[par], 2), AF.Sqrt),
                waits=[t_v], inc="act")
            P.op("vector", lambda v, par=par: v.reciprocal(ff(Z_RSTP[par], 2),
                                                           ff(Z_SDP[par], 2)),
                 waits=[t_sq], inc="dve")
            t_nm = P.op("vector", lambda v, par=par: v.tensor_scalar(
                ff(Z_NMP[par], 2), ff(Z_RSTP[par], 2),
                ff(Z_MVP[par], 1), -1.0, ALU.mult, ALU.mult), inc="dve")
            t_x1 = P.op("scalar", lambda s, par=par: s.activation(
                ff(Z_XRES[par], 1024), ff(Z_UPRE[par], 1024), AF.Identity,
                bias=ff(Z_NMP[par], 1), scale=ff(Z_RSTP[par], 1)),
                waits=[t_nm], inc="act")
            t_sx_tok[tt] = cdma("sx%d" % par, x1_sp.ap()[tt * 128:(tt + 1) * 128, :],
                                ff(Z_XRES[par], 1024), waits=[t_x1])
            t_x1sp_cnt[par] += 1
            t_xn = P.op("scalar", lambda s, par=par: s.activation(
                bf(A_X1N[par], 1024), ff(Z_UPRE[par], 1024), AF.Identity,
                bias=ff(Z_NMP[par] + 1, 1), scale=ff(Z_RSTP[par] + 1, 1)),
                waits=[t_nm, ("sn%d" % par, t_x1n_cnt[par] * 16)], inc="act")
            t_apply[par] = t_xn
            P.op("vector", lambda v, par=par:
                 v.tensor_mul(bf(A_X1N[par], 1024), bf(A_X1N[par], 1024),
                              ff(Z_GAM, 1024)), waits=[t_xn, t_gab], inc="dve")
            t_n = P.op("vector", lambda v, par=par:
                       v.tensor_add(bf(A_X1N[par], 1024), bf(A_X1N[par], 1024),
                                    ff(Z_BET, 1024)), inc="dve")
            t_sn_tok[tt] = cdma("sn%d" % par, x1n_sp.ap()[tt * 128:(tt + 1) * 128, :],
                                bf(A_X1N[par], 1024), waits=[t_n])
            t_x1n_cnt[par] += 1
        t_sn_all = [("sn0", t_x1n_cnt[0] * 16), ("sn1", t_x1n_cnt[1] * 16)]
        t_sx_all = [("sx0", t_x1sp_cnt[0] * 16), ("sx1", t_x1sp_cnt[1] * 16)]

        w1_data(7, waits=[t_oproj_done])
        for c in range(2):
            w2_data(c, waits=[t_oproj_done])
        t_zfree = {0: t_x1pre[0], 1: t_x1pre[1]}

        ctr = pL2.ap().bitcast(BF16)
        t_x1nld = {}
        t_tr_cp = None
        t_tr_cp_h = {}
        t_fc1 = {}
        t_x1ld = {}
        t_outts = {}
        t_so_cnt = {0: 0, 1: 0}
        for blk in range(8):
            for tsl in range(2):
                t_x1nld[tsl] = sdma(
                    "ln%d" % tsl, bf(A_X1NR[tsl], 1024),
                    x1n_sp.ap()[blk * 256 + tsl * 128: blk * 256 + (tsl + 1) * 128, :],
                    waits=[t_sn_tok[blk * 2 + tsl], t_tr_cp_h.get(tsl)])
            for tsl in range(2):
                for mt in range(8):
                    w = ([t_x1nld[tsl], t_masks, t_tr_cp_h.get(tsl)]
                         if mt == 0 else [])
                    t_tr = P.op("tensor", lambda t, tsl=tsl, mt=mt:
                                t.transpose(ctr[:, tsl * 1024 + mt * 128:
                                                tsl * 1024 + (mt + 1) * 128],
                                            bf(A_X1NR[tsl] + mt * 128, 128), ident[:, :]),
                                waits=w, inc="pe" if mt == 7 else None)
                dst = AB[:, A_NTB:A_NTB + 2048].rearrange(
                    "p (m c) -> p m c", m=8)[:, :, tsl * 128:(tsl + 1) * 128]
                srcp = ctr[:, tsl * 1024:tsl * 1024 + 1024].rearrange(
                    "p (m c) -> p m c", m=8)
                t_tr_cp_h[tsl] = P.op("vector", lambda v, d=dst, s=srcp:
                                      v.tensor_copy(d, s), waits=[t_tr], inc="dve")
            t_tr_cp = t_tr_cp_h[1]
            for ft in range(32):
                cbank = (c01, c23)[ft % 2]
                for kt in range(8):
                    w = [t_w1ld[kt]] if blk == 0 else []
                    if kt == 0:
                        w += [t_tr_cp_h.get(0), t_tr_cp_h.get(1), t_fc1.get(ft % 2)]
                    tok = P.op("tensor", lambda t, ps=cbank,
                               l=bf(A_W1 + kt * 4096 + ft * 128, 128),
                               r=bf(A_NTB + kt * 256, 256), st=(kt == 0):
                               t.matmul(ps.ap()[:, 0:256], l, r, start=st, stop=(kt == 7)),
                               waits=w, inc="pe" if kt == 7 else None)
                t_fc1[ft % 2] = P.op(
                    "scalar", lambda s, ps=cbank, d=bf(A_YT + ft * 256, 256), ft=ft:
                    s.activation(d, ps.ap()[:, 0:256], AF.Silu, bias=ff(Z_W1BT + ft, 1)),
                    waits=[tok, t_bsetup], inc="act")
            t_y = ("act", P.cnt["act"])
            for tsub in range(2):
                g = blk * 2 + tsub
                par = g % 2
                ps = pL[par]
                for hc in range(2):
                    for i, ft in enumerate(FT_ORDER):
                        w = []
                        if blk == 0:
                            w.append(t_w2ld[ft // 4])
                        if hc == 0 and i == 0:
                            w += [t_y, t_zfree.get(par)]
                        P.op("tensor", lambda t, ps=ps,
                             l=bf(A_YT + ft * 256 + tsub * 128, 128),
                             r=bf(A_W2 + ft * 1024 + hc * 512, 512),
                             hc=hc, st=(i == 0):
                             t.matmul(ps.ap()[:, hc * 512:hc * 512 + 512], l, r,
                                      start=st, stop=False),
                             waits=w, inc=None)
                    tok = P.op("tensor", lambda t, ps=ps, hc=hc:
                               t.matmul(ps.ap()[:, hc * 512:hc * 512 + 512], onesb[0:1, :],
                                        w2b_sb[0:1, hc * 512:hc * 512 + 512],
                                        start=False, stop=True),
                               waits=[], inc="pe" if hc == 1 else None)
                t_zmm = tok
                t_x1ld[par] = gdma("lr%d" % par, ff(Z_XRES[par], 1024),
                                   x1_sp.ap()[g * 128:(g + 1) * 128, :],
                                   waits=[t_sx_tok[g], t_zfree.get(par),
                                          ("so%d" % par, t_so_cnt[par] * 16)])
                t_uu = P.op("vector", lambda v, ps=ps, par=par:
                            v.scalar_tensor_tensor(ff(Z_UPRE[par], 1024), ps.ap()[:, :], 0.25,
                                                   ff(Z_XRES[par], 1024), ALU.mult, ALU.add),
                            waits=[t_zmm, t_x1ld[par], t_outts.get(par)]
                            + ([t_apply.get(par)] if blk == 0 else []), inc="dve")
                t_zfree[par] = t_uu
                P.op("vector", lambda v, par=par:
                     v.bn_stats(ff(Z_BN6P[par], 6), ff(Z_UPRE[par], 512)), inc="dve")
                P.op("vector", lambda v, par=par:
                     v.bn_stats(ff(Z_BN6P[par] + 6, 6), ff(Z_UPRE[par] + 512, 512)), inc="dve")
                P.op("vector", lambda v, g=g:
                     v.bn_aggr(ff(Z_MV2A + 2 * g, 2), ff(Z_BN6P[g % 2], 12)), inc="dve")
                P.op("vector", lambda v, par=par, g=g: v.tensor_scalar_add(
                    ff(Z_SDP[par], 1), ff(Z_MV2A + 2 * g + 1, 1), EPS2), inc="dve")
                t_v = ("dve", P.cnt["dve"])
                t_sq = P.op("scalar", lambda s, par=par: s.activation(
                    ff(Z_SDP[par], 1), ff(Z_SDP[par], 1), AF.Sqrt),
                    waits=[t_v], inc="act")
                t_rs = P.op("vector", lambda v, par=par: v.reciprocal(
                    ff(Z_RSTP[par], 1), ff(Z_SDP[par], 1)), waits=[t_sq], inc="dve")
                t_outts[par] = P.op("vector", lambda v, par=par, g=g:
                                    v.tensor_scalar(ff(Z_XRES[par], 1024),
                                                    ff(Z_UPRE[par], 1024),
                                                    ff(Z_MV2A + 2 * g, 1),
                                                    ff(Z_RSTP[par], 1),
                                                    ALU.subtract, ALU.mult),
                                    waits=[t_rs], inc="dve")
                cdma("so%d" % par, out.ap()[g * 128:(g + 1) * 128, :],
                     ff(Z_XRES[par], 1024), waits=[t_outts[par]])
                t_so_cnt[par] += 1
        if DEBUG:
            dmap = {
                "d_agin": (ag_in, t_ag_w),
                "d_agden": (ag_den_in, t_ag_w),
                "d_agout": (ag_out, [("cc", 1)]),
                "d_agdout": (ag_den_out, [("cc", 2)]),
                "d_w1d": (w1d, [("pst", 64 * 16)]),
                "d_w2d": (w2d, [("pst", 64 * 16)]),
                "d_x1": (x1_sp, t_sx_all),
                "d_x1n": (x1n_sp, t_sn_all),
            }
            for nm, (src_, w) in dmap.items():
                dt_ = nc.dram_tensor(nm, list(src_.shape), src_.dtype, kind="ExternalOutput")
                P.op("gpsimd", lambda g, d=dt_, s=src_: g.dma_start(out=d.ap(), in_=s.ap()),
                     waits=w, inc="lg", amt=16, dma=True)
            P.op("gpsimd", lambda g: g.nop(), waits=[("lg", P.cnt["lg"])])

        P.op("sync", lambda s: s.nop(),
             waits=[("so0", t_so_cnt[0] * 16), ("so1", t_so_cnt[1] * 16)])

        @block.sync
        def _(engobj):
            P.emit("sync", engobj, sems)

        @block.gpsimd
        def _(engobj):
            P.emit("gpsimd", engobj, sems)

        @block.tensor
        def _(engobj):
            P.emit("tensor", engobj, sems)

        @block.vector
        def _(engobj):
            P.emit("vector", engobj, sems)

        @block.scalar
        def _(engobj):
            P.emit("scalar", engobj, sems)

    return nc


def kernel(**inputs):
    x = np.asarray(inputs["L0_attn_x"])
    Qw = np.asarray(inputs["L0_attn_Q_w"])
    Qb = np.asarray(inputs["L0_attn_Q_b"])
    Kw = np.asarray(inputs["L0_attn_K_w"])
    Kb = np.asarray(inputs["L0_attn_K_b"])
    Vw = np.asarray(inputs["L0_attn_V_w"])
    Vb = np.asarray(inputs["L0_attn_V_b"])
    Ow = np.asarray(inputs["L0_attn_O_w"])
    Ob = np.asarray(inputs["L0_attn_O_b"])
    gam = np.asarray(inputs["L0_ffn_ln_gamma"])
    bet = np.asarray(inputs["L0_ffn_ln_beta"])
    w1q = np.ascontiguousarray(inputs["L0_ffn_fc1_q"])
    w1s = np.asarray(inputs["L0_ffn_fc1_scales"])
    w1b = np.asarray(inputs["L0_ffn_fc1_bias"])
    w2q = np.ascontiguousarray(inputs["L0_ffn_fc2_q"])
    w2s = np.asarray(inputs["L0_ffn_fc2_scales"])
    w2b = np.asarray(inputs["L0_ffn_fc2_bias"])

    nc = build_nc()
    owf = np.ascontiguousarray(Ow.reshape(1024, 1024))
    w1bT = np.ascontiguousarray(w1b.reshape(32, 128).T)
    in_maps = []
    for c in range(NCORE):
        b, r = c // 4, c % 4
        hs = slice(4 * r, 4 * r + 4)
        in_maps.append({
            "xT": np.ascontiguousarray(x[b].T),
            "x_res": np.ascontiguousarray(x[b, r * TPC:(r + 1) * TPC]),
            "qw": np.ascontiguousarray(Qw[hs].transpose(1, 0, 2).reshape(DM, 256)),
            "kw": np.ascontiguousarray(Kw[hs].transpose(1, 0, 2).reshape(DM, 256)),
            "vw": np.ascontiguousarray(Vw[hs].transpose(1, 0, 2).reshape(DM, 256)),
            "qbT": np.ascontiguousarray(Qb[hs].reshape(2, 128).T),
            "kbT": np.ascontiguousarray(Kb[hs].reshape(2, 128).T),
            "vb": np.ascontiguousarray(Vb[hs].reshape(2, 128)),
            "ow": owf, "ob": Ob, "gam": gam, "bet": bet,
            "w1q": w1q, "w1s": w1s, "w1bT": w1bT,
            "w2q": w2q, "w2s": w2s, "w2b": w2b,
            "rk": (r * TPC + 128 * np.arange(16, dtype=np.int32)).reshape(1, 16),
        })
    res = run_bass_kernel_spmd(nc, in_maps, core_ids=list(range(NCORE)))
    globals()["_last_res"] = res
    outarr = np.empty((2, S, DM), dtype=np.float32)
    for c in range(NCORE):
        b, r = c // 4, c % 4
        outarr[b, r * TPC:(r + 1) * TPC] = res.results[c]["out"]
    return outarr
